# version 02
# baseline (speedup 1.0000x reference)
import numpy as np
import ml_dtypes
from contextlib import ExitStack

import concourse.bass as bass
import concourse.mybir as mybir
from concourse.bass_utils import run_bass_kernel_spmd

F32 = mybir.dt.float32
BF16 = mybir.dt.bfloat16
I32 = mybir.dt.int32
AF = mybir.ActivationFunctionType
ALU = mybir.AluOpType
AX = mybir.AxisListType

NCORE = 8
D = 1024
T = 4096
H = 4096
DC = D // 128
HC = H // 128
TT = 512
NT = T // TT
EPS = 1e-5
QP = 127.0
MAGIC = 12582912.0
MINR = 1e-6
INV_D = 1.0 / D



def recip_newton(vec, out, in_, scr):
    vec.reciprocal(out=out, in_=in_)
    vec.tensor_tensor(out=scr, in0=in_, in1=out, op=ALU.mult)
    vec.tensor_scalar(out=scr, in0=scr, scalar1=-1.0, scalar2=2.0,
                      op0=ALU.mult, op1=ALU.add)
    vec.tensor_tensor(out=out, in0=out, in1=scr, op=ALU.mult)


def quant_tiles(vec, out_bf, in_f32, r_ap, scr_i32):
    vec.tensor_scalar(out=scr_i32, in0=in_f32, scalar1=r_ap, scalar2=None,
                      op0=ALU.mult)
    vec.tensor_copy(out=out_bf, in_=scr_i32)




class FE:

    def __init__(self, eng, sem):
        self.eng, self.sem, self.n = eng, sem, 0

    def __call__(self, ins):
        self.n += 1
        ins.then_inc(self.sem, 1)
        self.eng.wait_ge(self.sem, self.n)


def recip_newton_f(vec, fe, out, in_, scr):
    fe(vec.reciprocal(out=out, in_=in_))
    fe(vec.tensor_tensor(out=scr, in0=in_, in1=out, op=ALU.mult))
    fe(vec.tensor_scalar(out=scr, in0=scr, scalar1=-1.0, scalar2=2.0,
                         op0=ALU.mult, op1=ALU.add))
    fe(vec.tensor_tensor(out=out, in0=out, in1=scr, op=ALU.mult))



def build_L1():
    nc = bass.Bass("TRN2", target_bir_lowering=False)
    u_t = nc.dram_tensor("u", [D, T], F32, kind="ExternalInput").ap()
    eye_t = nc.dram_tensor("eye", [128, 128], F32, kind="ExternalInput").ap()
    stats_t = nc.dram_tensor("stats", [2, T], F32, kind="ExternalOutput").ap()
    mloc_t = nc.dram_tensor("mloc", [1, 1], F32, kind="ExternalOutput").ap()

    with ExitStack() as st:
        S = st.enter_context
        us = [S(nc.sbuf_tensor(f"u{d}", [128, T], F32)) for d in range(DC)]
        eye = S(nc.sbuf_tensor("eye_sb", [128, 128], F32))
        onec = S(nc.sbuf_tensor("onec", [128, 1], F32))
        oner = S(nc.sbuf_tensor("oner", [1, 128], F32))
        sq = [S(nc.sbuf_tensor(f"sq{i}", [128, TT], F32)) for i in range(2)]
        mrow = S(nc.sbuf_tensor("mrow", [1, T], F32))
        rrow = S(nc.sbuf_tensor("rrow", [1, T], F32))
        t0 = S(nc.sbuf_tensor("t0", [1, TT], F32))
        t1 = S(nc.sbuf_tensor("t1", [1, TT], F32))
        vrow = S(nc.sbuf_tensor("vrow", [1, T], F32))
        lnt = S(nc.sbuf_tensor("lnt", [1, TT], F32))
        tmp8 = S(nc.sbuf_tensor("tmp8", [128, T], F32))
        red8 = S(nc.sbuf_tensor("red8", [128, DC], F32))
        rt = S(nc.sbuf_tensor("rt", [128, 1], F32))
        mx = S(nc.sbuf_tensor("mx", [128, 1], F32))
        sc = S(nc.sbuf_tensor("sc", [1, 1], F32))
        psA = S(nc.psum_tensor("psA", [128, TT], F32))
        psB = [S(nc.psum_tensor(f"psB{i}", [128, TT], F32)) for i in range(2)]
        psM = [S(nc.psum_tensor(f"psM{i}", [128, TT], F32)) for i in range(2)]
        psR = [S(nc.psum_tensor(f"psR{i}", [128, TT], F32)) for i in range(2)]
        ld = S(nc.semaphore("ld"))
        sa = S(nc.semaphore("sa"))
        sv = S(nc.semaphore("sv"))
        sp = S(nc.semaphore("sp"))
        so = S(nc.semaphore("so"))
        vf_s = S(nc.semaphore("vf"))
        blk = S(nc.Block())

        @blk.sync
        def _(sync):
            for d in range(DC):
                sync.dma_start(out=us[d][:], in_=u_t[d * 128:(d + 1) * 128, :]
                               ).then_inc(ld, 16)
            sync.dma_start(out=eye[:], in_=eye_t).then_inc(ld, 16)
            sync.wait_ge(sv, 2 + NT + 3 * NT + NT + 1)
            sync.dma_start(out=stats_t[0:1, :], in_=mrow[:]).then_inc(so, 16)
            sync.dma_start(out=stats_t[1:2, :], in_=rrow[:]).then_inc(so, 16)
            sync.dma_start(out=mloc_t, in_=sc[:]).then_inc(so, 16)
            sync.wait_ge(so, 48)

        SV_INIT = 2
        SV_A1 = SV_INIT + NT
        SV_A2 = SV_A1 + 3 * NT
        SV_B = SV_A2 + NT
        SV_FIN = SV_B + 1

        @blk.tensor
        def _(tensor):
            tensor.wait_ge(ld, 144)
            tensor.wait_ge(sv, 2)
            for t in range(NT):
                if t >= 1:
                    tensor.wait_ge(sv, SV_INIT + t)
                ts = slice(t * TT, (t + 1) * TT)
                for d in range(DC):
                    ins = tensor.matmul(out=psA[0:1, :], lhsT=onec[:],
                                        rhs=us[d][:, ts],
                                        start=(d == 0), stop=(d == DC - 1))
                ins.then_inc(sp, 1)
            nsq = 0
            for t in range(NT):
                for d in range(DC):
                    nsq += 1
                    tensor.wait_ge(sa, nsq)
                    if d == 0 and t >= 2:
                        tensor.wait_ge(sv, SV_A1 + 3 * (t - 1))
                    tensor.matmul(out=psB[t % 2][0:1, :], lhsT=onec[:],
                                  rhs=sq[nsq % 2][:], start=(d == 0),
                                  stop=(d == DC - 1)).then_inc(sp, 1)
            for t in range(NT):
                ts = slice(t * TT, (t + 1) * TT)
                tensor.wait_ge(sa, NT * DC + 2 * (t + 1))
                if t >= 2:
                    tensor.wait_ge(sv, SV_A2 + (t - 1))
                tensor.matmul(out=psM[t % 2][:], lhsT=oner[:], rhs=mrow[0:1, ts],
                              start=True, stop=True)
                tensor.matmul(out=psR[t % 2][:], lhsT=oner[:], rhs=rrow[0:1, ts],
                              start=True, stop=True).then_inc(sp, 1)
            tensor.wait_ge(sv, SV_B)
            tensor.transpose(out=psA[0:1, 0:128], in_=mx[:], identity=eye[:]
                             ).then_inc(sp, 1)

        @blk.scalar
        def _(scalar):
            scalar.wait_ge(ld, 144)
            n = 0
            for t in range(NT):
                ts = slice(t * TT, (t + 1) * TT)
                for d in range(DC):
                    n += 1
                    if n > 2:
                        scalar.wait_ge(sp, NT + n - 2)
                    scalar.activation(out=sq[n % 2][:], in_=us[d][:, ts],
                                      func=AF.Square).then_inc(sa, 1)
            for t in range(NT):
                ts = slice(t * TT, (t + 1) * TT)
                scalar.wait_ge(sv, 2 + NT + 3 * (t + 1))
                scalar.activation(out=lnt[:], in_=vrow[0:1, ts], func=AF.Ln
                                  ).then_inc(sa, 1)
                scalar.wait_ge(sa, NT * DC + 2 * t + 1)
                scalar.activation(out=rrow[0:1, ts], in_=lnt[:], func=AF.Exp,
                                  scale=-0.5).then_inc(sa, 1)

        @blk.vector
        def _(vector):
            vf = FE(vector, vf_s)
            vector.memset(onec[:], 1.0).then_inc(sv, 1)
            vector.memset(oner[:], 1.0)
            vector.memset(mx[:], 0.0).then_inc(sv, 1)
            vector.wait_ge(sv, SV_INIT)
            for t in range(NT):
                ts = slice(t * TT, (t + 1) * TT)
                vector.wait_ge(sp, t + 1)
                vector.tensor_scalar(out=mrow[0:1, ts], in0=psA[0:1, :],
                                     scalar1=INV_D, scalar2=None,
                                     op0=ALU.mult).then_inc(sv, 1)
            vector.wait_ge(sv, SV_A1)
            for t in range(NT):
                ts = slice(t * TT, (t + 1) * TT)
                vector.wait_ge(sp, NT + DC * (t + 1))
                vector.tensor_scalar(out=t0[:], in0=psB[t % 2][0:1, :],
                                     scalar1=INV_D, scalar2=None, op0=ALU.mult
                                     ).then_inc(sv, 1)
                vector.tensor_tensor(out=t1[:], in0=mrow[0:1, ts],
                                     in1=mrow[0:1, ts], op=ALU.mult).then_inc(sv, 1)
                vector.wait_ge(sv, SV_A1 + 3 * t + 2)
                vector.scalar_tensor_tensor(out=vrow[0:1, ts], in0=t0[:],
                                            scalar=EPS, in1=t1[:], op0=ALU.add,
                                            op1=ALU.subtract).then_inc(sv, 1)
                vector.wait_ge(sv, SV_A1 + 3 * t + 3)
            for t in range(NT):
                ts = slice(t * TT, (t + 1) * TT)
                vector.wait_ge(sp, NT + DC * NT + (t + 1))
                for d in range(DC):
                    ds = slice(d * TT, (d + 1) * TT)
                    ins = vector.tensor_sub(out=tmp8[:, ds], in0=us[d][:, ts],
                                            in1=psM[t % 2][:])
                vf(ins)
                for d in range(DC):
                    ds = slice(d * TT, (d + 1) * TT)
                    ins = vector.tensor_tensor(out=tmp8[:, ds], in0=tmp8[:, ds],
                                               in1=psR[t % 2][:], op=ALU.mult)
                vf(ins)
                for d in range(DC):
                    ds = slice(d * TT, (d + 1) * TT)
                    ins = vector.tensor_reduce(out=red8[:, d:d + 1], in_=tmp8[:, ds],
                                               axis=AX.X, op=ALU.max,
                                               apply_absolute_value=True)
                vf(ins)
                vf(vector.tensor_reduce(out=rt[:], in_=red8[:], axis=AX.X,
                                        op=ALU.max, apply_absolute_value=True))
                vector.tensor_max(out=mx[:], in0=mx[:], in1=rt[:]).then_inc(sv, 1)
                vector.wait_ge(sv, SV_A2 + t + 1)
            vector.wait_ge(sp, NT + DC * NT + NT + 1)
            vector.tensor_reduce(out=sc[:], in_=psA[0:1, 0:128], axis=AX.X,
                                 op=ALU.max, apply_absolute_value=True
                                 ).then_inc(sv, 1)

    return nc





def build_L2():
    nc = bass.Bass("TRN2", target_bir_lowering=False)
    u_t = nc.dram_tensor("u", [D, T], F32, kind="ExternalInput").ap()
    stats_t = nc.dram_tensor("stats", [2, T], F32, kind="ExternalInput").ap()
    mall_t = nc.dram_tensor("mall", [128, NCORE], F32, kind="ExternalInput").ap()
    w1T_t = nc.dram_tensor("w1T", [D, H], F32, kind="ExternalInput").ap()
    b1c_t = nc.dram_tensor("b1c", [128, HC], F32, kind="ExternalInput").ap()
    eye_t = nc.dram_tensor("eye", [128, 128], F32, kind="ExternalInput").ap()
    h1_t = nc.dram_tensor("h1", [H, T], F32, kind="ExternalOutput").ap()
    wq1_t = nc.dram_tensor("wq1", [D, H], BF16, kind="ExternalOutput").ap()
    sw_t = nc.dram_tensor("sw", [1, 1], F32, kind="ExternalOutput").ap()
    mloc_t = nc.dram_tensor("mloc", [1, 1], F32, kind="ExternalOutput").ap()

    NG = HC * NT
    NB = NG // 4
    with ExitStack() as st:
        S = st.enter_context
        mrow_t = S(nc.sbuf_tensor("mrow_t", [1, T], F32))
        rrow_t = S(nc.sbuf_tensor("rrow_t", [1, T], F32))
        mrow = mrow_t[0:1, :]
        rrow = rrow_t[0:1, :]
        ubuf = [S(nc.sbuf_tensor(f"ub{i}", [128, T], F32)) for i in range(2)]
        wbuf = [S(nc.sbuf_tensor(f"wb{i}", [128, H], F32)) for i in range(2)]
        xq = [S(nc.sbuf_tensor(f"xq{d}", [128, T], BF16)) for d in range(DC)]
        wqs = [S(nc.sbuf_tensor(f"wqs{i}", [128, DC * 128], BF16)) for i in range(2)]
        wqo = [S(nc.sbuf_tensor(f"wqo{i}", [128, H], BF16)) for i in range(2)]
        wint = S(nc.sbuf_tensor("wint", [128, H // 4], I32))
        xint = S(nc.sbuf_tensor("xint", [128, TT], I32))
        qf = S(nc.sbuf_tensor("qf", [128, TT], F32))
        red8 = S(nc.sbuf_tensor("red8", [128, DC], F32))
        rt2 = S(nc.sbuf_tensor("rt2", [128, 1], F32))
        eye = S(nc.sbuf_tensor("eye_sb", [128, 128], F32))
        oner = S(nc.sbuf_tensor("oner", [1, 128], F32))
        mallb = S(nc.sbuf_tensor("mallb", [128, NCORE], F32))
        b1c = S(nc.sbuf_tensor("b1c_sb", [128, HC], F32))
        b1s = S(nc.sbuf_tensor("b1s", [128, HC], F32))
        tmpq = S(nc.sbuf_tensor("tmpq", [128, T], F32))
        wmx = S(nc.sbuf_tensor("wmx", [128, 1], F32))
        gmx = S(nc.sbuf_tensor("gmx", [128, 1], F32))
        mxm = S(nc.sbuf_tensor("mxm", [128, 1], F32))
        sxb = S(nc.sbuf_tensor("sxb", [128, 1], F32))
        rxb = S(nc.sbuf_tensor("rxb", [128, 1], F32))
        s1b = S(nc.sbuf_tensor("s1b", [128, 1], F32))
        swb = S(nc.sbuf_tensor("swb", [128, 1], F32))
        rwb = S(nc.sbuf_tensor("rwb", [128, 1], F32))
        sc1 = S(nc.sbuf_tensor("sc1", [128, 1], F32))
        wsc = S(nc.sbuf_tensor("wsc", [1, 1], F32))
        sw_s = S(nc.sbuf_tensor("sw_s", [1, 1], F32))
        rw1 = S(nc.sbuf_tensor("rw1", [1, 1], F32))
        r1a = S(nc.sbuf_tensor("r1a", [1, 1], F32))
        srow = S(nc.sbuf_tensor("srow", [1, 2], F32))
        mgl = S(nc.sbuf_tensor("mgl", [1, 1], F32))
        pmm = [S(nc.psum_tensor(f"pmm{i}", [128, TT], F32)) for i in range(4)]
        paux = S(nc.psum_tensor("paux", [128, 128], F32))
        pbc = S(nc.psum_tensor("pbc", [128, 2], F32))
        lfix = S(nc.semaphore("lfix"))
        lu = [S(nc.semaphore(f"lu{i}")) for i in range(2)]
        gw = [S(nc.semaphore(f"gw{i}")) for i in range(2)]
        gq = [S(nc.semaphore(f"gq{i}")) for i in range(2)]
        sq = [S(nc.semaphore(f"sq{i}")) for i in range(2)]
        sh = [S(nc.semaphore(f"sh{i}")) for i in range(2)]
        sv = S(nc.semaphore("sv"))
        sp = S(nc.semaphore("sp"))
        sa = S(nc.semaphore("sa"))
        so = S(nc.semaphore("so"))
        vf_s = S(nc.semaphore("vf"))
        af_s = S(nc.semaphore("af"))
        blk = S(nc.Block())

        V_ONES = 1
        V_WCH = lambda d: 2 + d
        V_WMX = 10
        V_SROW = 11
        V_SCAL = 12
        V_WQ = lambda d: 13 + d
        V_XQ = lambda t: 21 + t
        V_G = lambda g: 29 + g
        V_GDONE = 29 + NG
        V_FIN = 30 + NG
        P_TR = 1
        P_BC = 2
        P_B = lambda t: 3 + t
        P_G = lambda g: 11 + g
        P_FINT = 11 + NG

        @blk.sync
        def _(sync):
            sync.dma_start(out=mrow_t[:], in_=stats_t[0:1, :]).then_inc(lfix, 16)
            sync.dma_start(out=rrow_t[:], in_=stats_t[1:2, :]).then_inc(lfix, 16)
            sync.dma_start(out=mallb[:], in_=mall_t).then_inc(lfix, 16)
            sync.dma_start(out=b1c[:], in_=b1c_t).then_inc(lfix, 16)
            sync.dma_start(out=eye[:], in_=eye_t).then_inc(lfix, 16)
            for t in range(NT):
                if t >= 2:
                    sync.wait_ge(sv, V_XQ(t - 2))
                sync.dma_start(
                    out=ubuf[t % 2][:].rearrange("p (c t) -> p c t", c=DC),
                    in_=u_t[:, t * TT:(t + 1) * TT].rearrange("(c p) t -> p c t", p=128),
                ).then_inc(lu[t % 2], 16)
            sync.wait_ge(sv, V_FIN)
            sync.dma_start(out=mloc_t, in_=mgl[:]).then_inc(so, 16)
            sync.dma_start(out=sw_t, in_=sw_s[:]).then_inc(so, 16)
            sync.wait_ge(so, 32)
            sync.wait_ge(sq[0], 16 * (DC // 2))
            sync.wait_ge(sq[1], 16 * (DC // 2))
            sync.wait_ge(sh[0], 16 * (NB // 2))
            sync.wait_ge(sh[1], 16 * (NB // 2))

        @blk.gpsimd
        def _(gpsimd):
            for d in range(DC):
                if d >= 2:
                    gpsimd.wait_ge(sv, V_WCH(d - 2))
                gpsimd.dma_start(out=wbuf[d % 2][:],
                                 in_=w1T_t[d * 128:(d + 1) * 128, :]
                                 ).then_inc(gw[d % 2], 16)
            gpsimd.wait_ge(sv, V_WCH(DC - 1))
            for d in range(DC):
                if d >= 2:
                    gpsimd.wait_ge(sv, V_WQ(d - 2))
                gpsimd.dma_start(out=wbuf[d % 2][:],
                                 in_=w1T_t[d * 128:(d + 1) * 128, :]
                                 ).then_inc(gw[d % 2], 16)
            gpsimd.wait_ge(sq[0], 16 * (DC // 2))
            gpsimd.wait_ge(sq[1], 16 * (DC // 2))
            for h in range(HC):
                if h >= 2:
                    gpsimd.wait_ge(sp, P_G((h - 1) * NT - 1))
                gpsimd.dma_start(
                    out=wqs[h % 2][:].rearrange("p (c m) -> p c m", c=DC),
                    in_=wq1_t[:, h * 128:(h + 1) * 128].rearrange("(c p) m -> p c m", p=128),
                ).then_inc(gq[h % 2], 16)

        @blk.vector
        def _(vector):
            vf = FE(vector, vf_s)
            vector.memset(oner[:], 1.0)
            vector.memset(wmx[:], 0.0)
            vector.memset(gmx[:], 0.0).then_inc(sv, 1)
            vector.wait_ge(sv, V_ONES)
            vector.wait_ge(lfix, 80)
            vf(vector.tensor_reduce(out=mxm[:], in_=mallb[:], axis=AX.X, op=ALU.max,
                                    apply_absolute_value=True))
            vf(vector.tensor_scalar(out=sxb[:], in0=mxm[:], scalar1=MINR,
                                    scalar2=1.0 / QP, op0=ALU.max, op1=ALU.mult))
            recip_newton_f(vector, vf, rxb[:], sxb[:], sc1[:])
            vf(vector.tensor_scalar(out=rrow, in0=rrow, scalar1=rxb[0:1, 0:1],
                                    scalar2=None, op0=ALU.mult))
            for d in range(DC):
                vector.wait_ge(gw[d % 2], 16 * (d // 2 + 1))
                vector.tensor_reduce(out=red8[:, d:d + 1], in_=wbuf[d % 2][:],
                                     axis=AX.X, op=ALU.max,
                                     apply_absolute_value=True).then_inc(sv, 1)
            vector.wait_ge(sv, V_WCH(DC - 1))
            vector.tensor_reduce(out=wmx[:], in_=red8[:], axis=AX.X, op=ALU.max,
                                 apply_absolute_value=True).then_inc(sv, 1)
            vector.wait_ge(sv, V_WMX)
            vector.wait_ge(sp, P_TR)
            vf(vector.tensor_reduce(out=wsc[:], in_=paux[0:1, 0:128], axis=AX.X,
                                    op=ALU.max, apply_absolute_value=True))
            vf(vector.tensor_scalar(out=sw_s[:], in0=wsc[:], scalar1=MINR,
                                    scalar2=1.0 / QP, op0=ALU.max, op1=ALU.mult))
            recip_newton_f(vector, vf, rw1[:], sw_s[:], r1a[:])
            vector.tensor_copy(out=srow[0:1, 0:1], in_=sw_s[:])
            vector.tensor_copy(out=srow[0:1, 1:2], in_=rw1[:]).then_inc(sv, 1)
            vector.wait_ge(sp, P_BC)
            vector.tensor_copy(out=swb[:], in_=pbc[:, 0:1])
            vf(vector.tensor_copy(out=rwb[:], in_=pbc[:, 1:2]))
            vf(vector.tensor_tensor(out=s1b[:], in0=swb[:], in1=sxb[:], op=ALU.mult))
            vector.tensor_scalar(out=b1s[:], in0=b1c[:], scalar1=s1b[:, 0:1],
                                 scalar2=None, op0=ALU.mult).then_inc(sv, 1)
            for d in range(DC):
                vector.wait_ge(gw[d % 2], 16 * (4 + d // 2 + 1))
                if d >= 2:
                    vector.wait_ge(sq[d % 2], 16 * (d // 2))
                for k in range(4):
                    hs = slice(k * (H // 4), (k + 1) * (H // 4))
                    wf = tmpq[:, 0:H // 4]
                    vf(vector.tensor_scalar(out=wf, in0=wbuf[d % 2][:, hs],
                                            scalar1=rwb[:, 0:1], scalar2=MAGIC,
                                            op0=ALU.mult, op1=ALU.add))
                    vf(vector.tensor_scalar(out=wf, in0=wf, scalar1=MAGIC,
                                            scalar2=None, op0=ALU.subtract))
                    vf(vector.tensor_copy(out=wint[:], in_=wf))
                    ins = vector.tensor_copy(out=wqo[d % 2][:, hs], in_=wint[:])
                    if k == 0:
                        vf(ins)
                ins.then_inc(sv, 1)
                vector.wait_ge(sv, V_WQ(d))
            for t in range(NT):
                ts_ = slice(t * TT, (t + 1) * TT)
                vector.wait_ge(lu[t % 2], 16 * (t // 2 + 1))
                vector.wait_ge(sp, P_B(t))
                for d in range(DC):
                    ds = slice(d * TT, (d + 1) * TT)
                    ins = vector.tensor_sub(out=tmpq[:, ds], in0=ubuf[t % 2][:, ds],
                                            in1=pmm[2 * (t % 2)][:])
                vf(ins)
                for d in range(DC):
                    ds = slice(d * TT, (d + 1) * TT)
                    vf(vector.tensor_tensor(out=qf[:], in0=tmpq[:, ds],
                                            in1=pmm[2 * (t % 2) + 1][:],
                                            op=ALU.mult))
                    vf(vector.tensor_scalar(out=qf[:], in0=qf[:], scalar1=MAGIC,
                                            scalar2=MAGIC, op0=ALU.add,
                                            op1=ALU.subtract))
                    vf(vector.tensor_copy(out=xint[:], in_=qf[:]))
                    ins = vector.tensor_copy(out=xq[d][:, ts_], in_=xint[:])
                    if d < DC - 1:
                        vf(ins)
                ins.then_inc(sv, 1)
                vector.wait_ge(sv, V_XQ(t))
            for g in range(NG):
                vector.wait_ge(sa, g + 1)
                stg = wbuf[(g // 4) % 2]
                col = (g % 4) * TT
                vector.tensor_reduce(out=red8[:, g % 8:g % 8 + 1],
                                     in_=stg[:, col:col + TT], axis=AX.X,
                                     op=ALU.max, apply_absolute_value=True
                                     ).then_inc(sv, 1)
                if g % 8 == 7:
                    vector.wait_ge(sv, V_G(g))
                    vf(vector.tensor_reduce(out=rt2[:], in_=red8[:], axis=AX.X,
                                            op=ALU.max, apply_absolute_value=True))
                    if g == NG - 1:
                        vector.tensor_max(out=gmx[:], in0=gmx[:], in1=rt2[:]
                                          ).then_inc(sv, 1)
                    else:
                        vf(vector.tensor_max(out=gmx[:], in0=gmx[:], in1=rt2[:]))
            vector.wait_ge(sp, P_FINT)
            vector.tensor_reduce(out=mgl[:], in_=paux[0:1, 0:128], axis=AX.X,
                                 op=ALU.max, apply_absolute_value=True).then_inc(sv, 1)

        @blk.tensor
        def _(tensor):
            tensor.wait_ge(lfix, 80)
            tensor.wait_ge(sv, V_WMX)
            tensor.transpose(out=paux[0:1, 0:128], in_=wmx[:], identity=eye[:]
                             ).then_inc(sp, 1)
            tensor.wait_ge(sv, V_SROW)
            tensor.matmul(out=pbc[:], lhsT=oner[:], rhs=srow[:], start=True,
                          stop=True).then_inc(sp, 1)
            for t in range(NT):
                ts_ = slice(t * TT, (t + 1) * TT)
                if t >= 2:
                    tensor.wait_ge(sv, V_XQ(t - 2))
                tensor.matmul(out=pmm[2 * (t % 2)][:], lhsT=oner[:],
                              rhs=mrow[:, ts_], start=True, stop=True)
                tensor.matmul(out=pmm[2 * (t % 2) + 1][:], lhsT=oner[:],
                              rhs=rrow[:, ts_], start=True, stop=True
                              ).then_inc(sp, 1)
            tensor.wait_ge(sv, V_XQ(NT - 1))
            g = 0
            for h in range(HC):
                tensor.wait_ge(gq[h % 2], 16 * (h // 2 + 1))
                for t in range(NT):
                    ts_ = slice(t * TT, (t + 1) * TT)
                    if g >= 4:
                        tensor.wait_ge(sa, g - 3)
                    for d in range(DC):
                        ins = tensor.matmul(out=pmm[g % 4][:],
                                            lhsT=wqs[h % 2][:, d * 128:(d + 1) * 128],
                                            rhs=xq[d][:, ts_], start=(d == 0),
                                            stop=(d == DC - 1))
                    ins.then_inc(sp, 1)
                    g += 1
            tensor.wait_ge(sv, V_GDONE)
            tensor.transpose(out=paux[0:1, 0:128], in_=gmx[:], identity=eye[:]
                             ).then_inc(sp, 1)

        @blk.scalar
        def _(scalar):
            for d in range(DC):
                scalar.wait_ge(sv, V_WQ(d))
                scalar.dma_start(out=wq1_t[d * 128:(d + 1) * 128, :],
                                 in_=wqo[d % 2][:]).then_inc(sq[d % 2], 16)
            scalar.wait_ge(sv, V_SCAL)
            scalar.wait_ge(sv, V_WQ(DC - 1))
            g = 0
            for h in range(HC):
                for t in range(NT):
                    b = g // 4
                    stg = wbuf[b % 2]
                    col = (g % 4) * TT
                    if g % 4 == 0 and b >= 2:
                        scalar.wait_ge(sh[b % 2], 16 * (b // 2))
                        scalar.wait_ge(sv, V_G(4 * (b - 1) - 1))
                    scalar.wait_ge(sp, P_G(g))
                    scalar.activation(out=stg[:, col:col + TT],
                                      in_=pmm[g % 4][:], func=AF.Gelu,
                                      bias=b1s[:, h:h + 1],
                                      scale=s1b[:, 0:1]).then_inc(sa, 1)
                    if g % 4 == 3:
                        scalar.wait_ge(sa, g + 1)
                        th = (t - 3) // 4
                        scalar.dma_start(
                            out=h1_t[h * 128:(h + 1) * 128,
                                     th * 2048:(th + 1) * 2048],
                            in_=stg[:, 0:2048]).then_inc(sh[b % 2], 16)
                    g += 1

    return nc





def build_L3a():
    nc = bass.Bass("TRN2", target_bir_lowering=False)
    x_t = nc.dram_tensor("x", [D, T], F32, kind="ExternalInput").ap()
    h1_t = nc.dram_tensor("h1", [H, T], F32, kind="ExternalInput").ap()
    mall_t = nc.dram_tensor("mall", [128, NCORE], F32, kind="ExternalInput").ap()
    w2T_t = nc.dram_tensor("w2T", [H, D], F32, kind="ExternalInput").ap()
    b2c_t = nc.dram_tensor("b2c", [128, DC], F32, kind="ExternalInput").ap()
    eye_t = nc.dram_tensor("eye", [128, 128], F32, kind="ExternalInput").ap()
    xo_t = nc.dram_tensor("xo", [D, T], F32, kind="ExternalOutput").ap()

    NW = HC
    NQ = 4
    with ExitStack() as st:
        S = st.enter_context
        wbuf = [S(nc.sbuf_tensor(f"wb{i}", [128, D], F32)) for i in range(2)]
        wq2r = [S(nc.sbuf_tensor(f"wq2r{h}", [128, D], BF16)) for h in range(NW)]
        hbuf = [S(nc.sbuf_tensor(f"hb{i}", [128, DC * TT], F32)) for i in range(2)]
        hq = [S(nc.sbuf_tensor(f"hq{i}", [128, DC * TT], BF16)) for i in range(2)]
        ubuf = [S(nc.sbuf_tensor(f"ub{i}", [128, T], F32)) for i in range(2)]
        xo = [S(nc.sbuf_tensor(f"xo_sb{i}", [128, T], F32)) for i in range(2)]
        wint = S(nc.sbuf_tensor("wint", [128, D], I32))
        wf = S(nc.sbuf_tensor("wf", [128, D], F32))
        xint = S(nc.sbuf_tensor("xint", [128, TT], I32))
        qf = S(nc.sbuf_tensor("qf", [128, TT], F32))
        eye = S(nc.sbuf_tensor("eye_sb", [128, 128], F32))
        oner = S(nc.sbuf_tensor("oner", [1, 128], F32))
        mallb = S(nc.sbuf_tensor("mallb", [128, NCORE], F32))
        b2c = S(nc.sbuf_tensor("b2c_sb", [128, DC], F32))
        b2s = S(nc.sbuf_tensor("b2s", [128, DC], F32))
        red8 = S(nc.sbuf_tensor("red8", [128, NW], F32))
        wmx = S(nc.sbuf_tensor("wmx", [128, 1], F32))
        mxm = S(nc.sbuf_tensor("mxm", [128, 1], F32))
        sxb = S(nc.sbuf_tensor("sxb", [128, 1], F32))
        rxb = S(nc.sbuf_tensor("rxb", [128, 1], F32))
        s5b = S(nc.sbuf_tensor("s5b", [128, 1], F32))
        swb = S(nc.sbuf_tensor("swb", [128, 1], F32))
        rwb = S(nc.sbuf_tensor("rwb", [128, 1], F32))
        sc1 = S(nc.sbuf_tensor("sc1", [128, 1], F32))
        wsc = S(nc.sbuf_tensor("wsc", [1, 1], F32))
        sw_s = S(nc.sbuf_tensor("sw_s", [1, 1], F32))
        rw1 = S(nc.sbuf_tensor("rw1", [1, 1], F32))
        r1a = S(nc.sbuf_tensor("r1a", [1, 1], F32))
        srow = S(nc.sbuf_tensor("srow", [1, 2], F32))
        pmm = [S(nc.psum_tensor(f"pmm{i}", [128, TT], F32)) for i in range(DC)]
        lfix = S(nc.semaphore("lfix"))
        lu = [S(nc.semaphore(f"lu{i}")) for i in range(2)]
        lh = [S(nc.semaphore(f"lh{i}")) for i in range(2)]
        gw = [S(nc.semaphore(f"gw{i}")) for i in range(2)]
        sx_ = [S(nc.semaphore(f"sx{i}")) for i in range(2)]
        sv = S(nc.semaphore("sv"))
        sp = S(nc.semaphore("sp"))
        sa = S(nc.semaphore("sa"))
        vf_s = S(nc.semaphore("vf"))
        blk = S(nc.Block())

        V_ONES = 1
        V_WCH = lambda h: 2 + h
        V_WMX = 2 + NW
        V_SROW = V_WMX + 1
        V_SCAL = V_SROW + 1
        V_WQ = lambda h: V_SCAL + 1 + h
        V_HQ = lambda t, q: V_WQ(NW - 1) + 1 + t * (NQ + 1) + q
        V_XO = lambda t: V_WQ(NW - 1) + 1 + t * (NQ + 1) + NQ
        V_FIN = V_XO(NT - 1)
        P_TR = 1
        P_BC = 2
        P_MM = lambda t, q: 3 + t * NQ + q
        P_T = lambda t: P_MM(t, NQ - 1)

        @blk.sync
        def _(sync):
            sync.dma_start(out=mallb[:], in_=mall_t).then_inc(lfix, 16)
            sync.dma_start(out=b2c[:], in_=b2c_t).then_inc(lfix, 16)
            sync.dma_start(out=eye[:], in_=eye_t).then_inc(lfix, 16)
            for t in range(NT):
                if t >= 2:
                    sync.wait_ge(sv, V_XO(t - 2))
                sync.dma_start(
                    out=ubuf[t % 2][:].rearrange("p (c t) -> p c t", c=DC),
                    in_=x_t[:, t * TT:(t + 1) * TT].rearrange("(c p) t -> p c t", p=128),
                ).then_inc(lu[t % 2], 16)
            sync.wait_ge(sv, V_FIN)
            sync.wait_ge(sx_[0], 16 * (NT // 2))
            sync.wait_ge(sx_[1], 16 * (NT // 2))

        @blk.gpsimd
        def _(gpsimd):
            for h in range(NW):
                if h >= 2:
                    gpsimd.wait_ge(sv, V_WCH(h - 2))
                gpsimd.dma_start(out=wbuf[h % 2][:],
                                 in_=w2T_t[h * 128:(h + 1) * 128, :]
                                 ).then_inc(gw[h % 2], 16)
            gpsimd.wait_ge(sv, V_WCH(NW - 1))
            for h in range(NW):
                if h >= 2:
                    gpsimd.wait_ge(sv, V_WQ(h - 2))
                gpsimd.dma_start(out=wbuf[h % 2][:],
                                 in_=w2T_t[h * 128:(h + 1) * 128, :]
                                 ).then_inc(gw[h % 2], 16)
            for t in range(NT):
                for q in range(NQ):
                    j = t * NQ + q
                    i = j % 2
                    if j >= 2:
                        gpsimd.wait_ge(sv, V_HQ(*divmod(j - 2, NQ)))
                    gpsimd.dma_start(
                        out=hbuf[i][:].rearrange("p (c t) -> p c t", c=DC),
                        in_=h1_t[q * (H // NQ):(q + 1) * (H // NQ),
                                 t * TT:(t + 1) * TT].rearrange("(c p) t -> p c t", p=128),
                    ).then_inc(lh[i], 16)

        @blk.vector
        def _(vector):
            vf = FE(vector, vf_s)
            vector.memset(oner[:], 1.0)
            vector.memset(wmx[:], 0.0).then_inc(sv, 1)
            vector.wait_ge(sv, V_ONES)
            vector.wait_ge(lfix, 48)
            vf(vector.tensor_reduce(out=mxm[:], in_=mallb[:], axis=AX.X, op=ALU.max,
                                    apply_absolute_value=True))
            vf(vector.tensor_scalar(out=sxb[:], in0=mxm[:], scalar1=MINR,
                                    scalar2=1.0 / QP, op0=ALU.max, op1=ALU.mult))
            recip_newton_f(vector, vf, rxb[:], sxb[:], sc1[:])
            for h in range(NW):
                vector.wait_ge(gw[h % 2], 16 * (h // 2 + 1))
                vector.tensor_reduce(out=red8[:, h:h + 1], in_=wbuf[h % 2][:],
                                     axis=AX.X, op=ALU.max,
                                     apply_absolute_value=True).then_inc(sv, 1)
            vector.wait_ge(sv, V_WCH(NW - 1))
            vector.tensor_reduce(out=wmx[:], in_=red8[:], axis=AX.X, op=ALU.max,
                                 apply_absolute_value=True).then_inc(sv, 1)
            vector.wait_ge(sv, V_WMX)
            vector.wait_ge(sp, P_TR)
            vf(vector.tensor_reduce(out=wsc[:], in_=pmm[0][0:1, 0:128],
                                    axis=AX.X, op=ALU.max, apply_absolute_value=True))
            vf(vector.tensor_scalar(out=sw_s[:], in0=wsc[:], scalar1=MINR,
                                    scalar2=1.0 / QP, op0=ALU.max, op1=ALU.mult))
            recip_newton_f(vector, vf, rw1[:], sw_s[:], r1a[:])
            vector.tensor_copy(out=srow[0:1, 0:1], in_=sw_s[:])
            vector.tensor_copy(out=srow[0:1, 1:2], in_=rw1[:]).then_inc(sv, 1)
            vector.wait_ge(sp, P_BC)
            vector.tensor_copy(out=swb[:], in_=pmm[0][:, 0:1])
            vf(vector.tensor_copy(out=rwb[:], in_=pmm[0][:, 1:2]))
            vf(vector.tensor_tensor(out=s5b[:], in0=swb[:], in1=sxb[:], op=ALU.mult))
            vf(vector.tensor_scalar(out=s5b[:], in0=s5b[:], scalar1=0.5,
                                    scalar2=None, op0=ALU.mult))
            vector.tensor_scalar(out=b2s[:], in0=b2c[:], scalar1=s5b[:, 0:1],
                                 scalar2=None, op0=ALU.mult).then_inc(sv, 1)
            for h in range(NW):
                vector.wait_ge(gw[h % 2], 16 * (NW // 2 + h // 2 + 1))
                vf(vector.tensor_scalar(out=wf[:], in0=wbuf[h % 2][:],
                                        scalar1=rwb[:, 0:1], scalar2=MAGIC,
                                        op0=ALU.mult, op1=ALU.add))
                vf(vector.tensor_scalar(out=wf[:], in0=wf[:], scalar1=MAGIC,
                                        scalar2=None, op0=ALU.subtract))
                vf(vector.tensor_copy(out=wint[:], in_=wf[:]))
                vector.tensor_copy(out=wq2r[h][:], in_=wint[:]).then_inc(sv, 1)
                vector.wait_ge(sv, V_WQ(h))
            for t in range(NT):
                for q in range(NQ):
                    j = t * NQ + q
                    i = j % 2
                    vector.wait_ge(lh[i], 16 * (j // 2 + 1))
                    if j >= 2:
                        vector.wait_ge(sp, P_MM(*divmod(j - 2, NQ)))
                    for c in range(DC):
                        cs = slice(c * TT, (c + 1) * TT)
                        vf(vector.tensor_scalar(out=qf[:], in0=hbuf[i][:, cs],
                                                scalar1=rxb[:, 0:1], scalar2=MAGIC,
                                                op0=ALU.mult, op1=ALU.add))
                        vf(vector.tensor_scalar(out=qf[:], in0=qf[:], scalar1=MAGIC,
                                                scalar2=None, op0=ALU.subtract))
                        vf(vector.tensor_copy(out=xint[:], in_=qf[:]))
                        ins = vector.tensor_copy(out=hq[i][:, cs], in_=xint[:])
                        if c < DC - 1:
                            vf(ins)
                    ins.then_inc(sv, 1)
                    vector.wait_ge(sv, V_HQ(t, q))
                vector.wait_ge(lu[t % 2], 16 * (t // 2 + 1))
                vector.wait_ge(sa, DC * (t + 1))
                for d in range(DC):
                    ds = slice(d * TT, (d + 1) * TT)
                    ins = vector.tensor_add(out=xo[t % 2][:, ds],
                                            in0=xo[t % 2][:, ds],
                                            in1=ubuf[t % 2][:, ds])
                ins.then_inc(sv, 1)
                vector.wait_ge(sv, V_XO(t))

        @blk.tensor
        def _(tensor):
            tensor.wait_ge(lfix, 48)
            tensor.wait_ge(sv, V_WMX)
            tensor.transpose(out=pmm[0][0:1, 0:128], in_=wmx[:],
                             identity=eye[:]).then_inc(sp, 1)
            tensor.wait_ge(sv, V_SROW)
            tensor.matmul(out=pmm[0][:, 0:2], lhsT=oner[:], rhs=srow[:],
                          start=True, stop=True).then_inc(sp, 1)
            tensor.wait_ge(sv, V_WQ(NW - 1))
            for t in range(NT):
                for q in range(NQ):
                    i = (t * NQ + q) % 2
                    tensor.wait_ge(sv, V_HQ(t, q))
                    if q == 0 and t >= 1:
                        tensor.wait_ge(sa, DC * t)
                    for hh in range(DC):
                        h = q * DC + hh
                        for d in range(DC):
                            ins = tensor.matmul(
                                out=pmm[d][:],
                                lhsT=wq2r[h][:, d * 128:(d + 1) * 128],
                                rhs=hq[i][:, hh * TT:(hh + 1) * TT],
                                start=(h == 0), stop=(h == NW - 1))
                    ins.then_inc(sp, 1)

        @blk.scalar
        def _(scalar):
            scalar.wait_ge(sv, V_SCAL)
            for t in range(NT):
                scalar.wait_ge(sp, P_T(t))
                if t >= 2:
                    scalar.wait_ge(sx_[t % 2], 16 * (t // 2))
                for d in range(DC):
                    ds = slice(d * TT, (d + 1) * TT)
                    scalar.activation(out=xo[t % 2][:, ds], in_=pmm[d][:],
                                      func=AF.Identity, bias=b2s[:, d:d + 1],
                                      scale=s5b[:, 0:1]).then_inc(sa, 1)
                scalar.wait_ge(sv, V_XO(t))
                scalar.dma_start(
                    out=xo_t[:, t * TT:(t + 1) * TT].rearrange("(c p) t -> p c t", p=128),
                    in_=xo[t % 2][:].rearrange("p (c t) -> p c t", c=DC),
                ).then_inc(sx_[t % 2], 16)

    return nc






def build_L3b():
    nc = bass.Bass("TRN2", target_bir_lowering=False)
    x_t = nc.dram_tensor("x", [D, T], F32, kind="ExternalInput").ap()
    waT_t = nc.dram_tensor("waT", [D, D], F32, kind="ExternalInput").ap()
    wbT_t = nc.dram_tensor("wbT", [D, D], F32, kind="ExternalInput").ap()
    bac_t = nc.dram_tensor("bac", [128, DC], F32, kind="ExternalInput").ap()
    bbc_t = nc.dram_tensor("bbc", [128, DC], F32, kind="ExternalInput").ap()
    stats_t = nc.dram_tensor("stats", [2, T], F32, kind="ExternalOutput").ap()
    hs0_t = nc.dram_tensor("hs0", [D, T], F32, kind="ExternalOutput").ap()
    P_t = nc.dram_tensor("P", [D, T], F32, kind="ExternalOutput").ap()
    hend_t = nc.dram_tensor("hend", [128, DC], F32, kind="ExternalOutput").ap()

    with ExitStack() as st:
        S = st.enter_context
        xbuf = S(nc.sbuf_tensor("xbuf", [128, T], F32))
        sq = [S(nc.sbuf_tensor(f"sq{i}", [128, TT], F32)) for i in range(2)]
        xh = [S(nc.sbuf_tensor(f"xh{d}", [128, T], BF16)) for d in range(DC)]
        wab = [S(nc.sbuf_tensor(f"wab{d}", [128, D], BF16)) for d in range(DC)]
        wbb = [S(nc.sbuf_tensor(f"wbb{d}", [128, D], BF16)) for d in range(DC)]
        wld = [S(nc.sbuf_tensor(f"wld{i}", [128, D], F32)) for i in range(2)]
        alpha = S(nc.sbuf_tensor("alpha", [128, T], F32))
        bu = S(nc.sbuf_tensor("bu", [128, T], F32))
        hs0 = S(nc.sbuf_tensor("hs0_sb", [128, T], F32))
        Pb = S(nc.sbuf_tensor("P_sb", [128, T], F32))
        bac = S(nc.sbuf_tensor("bac_sb", [128, DC], F32))
        bbc = S(nc.sbuf_tensor("bbc_sb", [128, DC], F32))
        hend = S(nc.sbuf_tensor("hend_sb", [128, DC], F32))
        onec = S(nc.sbuf_tensor("onec", [128, 1], F32))
        oner = S(nc.sbuf_tensor("oner", [1, 128], F32))
        t0 = S(nc.sbuf_tensor("t0", [1, TT], F32))
        t1 = S(nc.sbuf_tensor("t1", [1, TT], F32))
        vr = S(nc.sbuf_tensor("vr", [1, TT], F32))
        lnt = S(nc.sbuf_tensor("lnt", [1, TT], F32))
        mr = S(nc.sbuf_tensor("mr", [1, TT], F32))
        rr = S(nc.sbuf_tensor("rr", [1, TT], F32))
        tmpq = S(nc.sbuf_tensor("tmpq", [128, TT], F32))
        psS = [S(nc.psum_tensor(f"psS{i}", [128, TT], F32)) for i in range(2)]
        psM = [S(nc.psum_tensor(f"psM{i}", [128, TT], F32)) for i in range(2)]
        psA = [S(nc.psum_tensor(f"psA{i}", [128, TT], F32)) for i in range(2)]
        psB = [S(nc.psum_tensor(f"psB{i}", [128, TT], F32)) for i in range(2)]
        lfix = S(nc.semaphore("lfix"))
        lx = S(nc.semaphore("lx"))
        gw = [S(nc.semaphore(f"gw{i}")) for i in range(2)]
        sh = [S(nc.semaphore(f"sh{i}")) for i in range(2)]
        srl = S(nc.semaphore("srl"))
        she = S(nc.semaphore("she"))
        sv = S(nc.semaphore("sv"))
        sp = S(nc.semaphore("sp"))
        sa = S(nc.semaphore("sa"))
        vf_s = S(nc.semaphore("vf"))
        af_s = S(nc.semaphore("af"))
        blk = S(nc.Block())

        V_INIT = 1
        V_WB = lambda k: 2 + k
        V_ROW = lambda t: 18 + 3 * t
        V_VAR = lambda t: 19 + 3 * t
        V_XH = lambda t: 20 + 3 * t
        V_PHA = V_XH(NT - 1)
        V_Z = V_PHA + 1
        V_SC = lambda c: V_Z + 1 + 3 * c + 0
        V_PS = lambda c: V_Z + 2 + 3 * c
        V_HE = lambda c: V_Z + 3 + 3 * c
        V_FIN = V_HE(DC - 1)
        A_SQ = lambda t, d: 9 * t + d + 1
        A_RST = lambda t: 9 * t + 9
        A_EV = lambda c, tt: 9 * NT + 2 * (c * NT + tt) + 2
        P_S1 = lambda t: 10 * t + 1
        P_SQ = lambda t, d: 10 * t + 2 + d
        P_S2 = lambda t: 10 * t + 9
        P_BC = lambda t: 10 * t + 10
        P_AB = lambda c, tt: 10 * NT + c * NT + tt + 1

        @blk.sync
        def _(sync):
            sync.dma_start(out=bac[:], in_=bac_t).then_inc(lfix, 16)
            sync.dma_start(out=bbc[:], in_=bbc_t).then_inc(lfix, 16)
            for t in range(NT):
                if t >= 1:
                    sync.wait_ge(sv, V_XH(t - 1))
                sync.dma_start(
                    out=xbuf[:].rearrange("p (c t) -> p c t", c=DC),
                    in_=x_t[:, t * TT:(t + 1) * TT].rearrange("(c p) t -> p c t", p=128),
                ).then_inc(lx, 16)
            sync.wait_ge(sv, V_FIN)
            sync.dma_start(out=hend_t, in_=hend[:]).then_inc(she, 16)
            sync.wait_ge(she, 16)
            sync.wait_ge(srl, 16 * 2 * NT)
            sync.wait_ge(sh[0], 16 * DC)
            sync.wait_ge(sh[1], 16 * DC)

        @blk.gpsimd
        def _(gpsimd):
            for k in range(2 * DC):
                if k >= 2:
                    gpsimd.wait_ge(sv, V_WB(k - 2))
                src_t = waT_t if k < DC else wbT_t
                d = k % DC
                gpsimd.dma_start(out=wld[k % 2][:],
                                 in_=src_t[d * 128:(d + 1) * 128, :]
                                 ).then_inc(gw[k % 2], 16)

        @blk.vector
        def _(vector):
            vf = FE(vector, vf_s)
            vector.memset(onec[:], 1.0)
            vector.memset(oner[:], 1.0).then_inc(sv, 1)
            vector.wait_ge(sv, V_INIT)
            for k in range(2 * DC):
                vector.wait_ge(gw[k % 2], 16 * (k // 2 + 1))
                dst = wab[k] if k < DC else wbb[k - DC]
                vector.tensor_copy(out=dst[:], in_=wld[k % 2][:]).then_inc(sv, 1)
            for t in range(NT):
                ts_ = slice(t * TT, (t + 1) * TT)
                vector.wait_ge(sp, P_S1(t))
                if t >= 1:
                    vector.wait_ge(srl, 32 * t)
                vf(vector.tensor_scalar(out=mr[:], in0=psS[t % 2][0:1, :],
                                        scalar1=INV_D, scalar2=None, op0=ALU.mult))
                vector.wait_ge(sp, P_S2(t))
                vector.tensor_scalar(out=t0[:], in0=psM[t % 2][0:1, :],
                                     scalar1=INV_D, scalar2=None, op0=ALU.mult
                                     ).then_inc(sv, 1)
                vf(vector.tensor_tensor(out=t1[:], in0=mr[:], in1=mr[:],
                                        op=ALU.mult))
                vector.scalar_tensor_tensor(out=vr[:], in0=t0[:], scalar=EPS,
                                            in1=t1[:], op0=ALU.add,
                                            op1=ALU.subtract).then_inc(sv, 1)
                vector.wait_ge(sv, V_VAR(t))
                vector.wait_ge(sp, P_BC(t))
                for d in range(DC):
                    ds = slice(d * TT, (d + 1) * TT)
                    vf(vector.tensor_sub(out=tmpq[:], in0=xbuf[:, ds],
                                         in1=psA[t % 2][:]))
                    ins = vector.tensor_tensor(out=xh[d][:, ts_], in0=tmpq[:],
                                               in1=psB[t % 2][:], op=ALU.mult)
                    if d < DC - 1:
                        vf(ins)
                ins.then_inc(sv, 1)
                vector.wait_ge(sv, V_XH(t))
            vector.memset(xbuf[:], 0.0).then_inc(sv, 1)
            vector.wait_ge(sv, V_Z)
            for c in range(DC):
                vector.wait_ge(sa, A_EV(c, NT - 1))
                if c >= 1:
                    vector.wait_ge(sh[0], 16 * c)
                    vector.wait_ge(sh[1], 16 * c)
                vector.tensor_tensor_scan(out=hs0[:], data0=alpha[:], data1=bu[:],
                                          initial=0.0, op0=ALU.mult, op1=ALU.add
                                          ).then_inc(sv, 1)
                vector.tensor_tensor_scan(out=Pb[:], data0=alpha[:], data1=xbuf[:],
                                          initial=1.0, op0=ALU.mult, op1=ALU.add
                                          ).then_inc(sv, 1)
                vector.wait_ge(sv, V_SC(c))
                vector.tensor_copy(out=hend[:, c:c + 1], in_=hs0[:, T - 1:T]
                                   ).then_inc(sv, 1)
                vector.wait_ge(sv, V_HE(c))

        @blk.tensor
        def _(tensor):
            tensor.wait_ge(sv, V_INIT)
            for t in range(NT):
                ts_ = slice(t * TT, (t + 1) * TT)
                tensor.wait_ge(lx, 16 * (t + 1))
                if t >= 2:
                    tensor.wait_ge(sv, V_XH(t - 2))
                for d in range(DC):
                    ds = slice(d * TT, (d + 1) * TT)
                    ins = tensor.matmul(out=psS[t % 2][0:1, :], lhsT=onec[:],
                                        rhs=xbuf[:, ds], start=(d == 0),
                                        stop=(d == DC - 1))
                ins.then_inc(sp, 1)
                for d in range(DC):
                    tensor.wait_ge(sa, A_SQ(t, d))
                    tensor.matmul(out=psM[t % 2][0:1, :], lhsT=onec[:],
                                  rhs=sq[d % 2][:], start=(d == 0),
                                  stop=(d == DC - 1)).then_inc(sp, 1)
                tensor.wait_ge(sa, A_RST(t))
                tensor.matmul(out=psA[t % 2][:], lhsT=oner[:], rhs=mr[:],
                              start=True, stop=True)
                tensor.matmul(out=psB[t % 2][:], lhsT=oner[:], rhs=rr[:],
                              start=True, stop=True).then_inc(sp, 1)
            tensor.wait_ge(sv, V_WB(2 * DC - 1))
            tensor.wait_ge(sv, V_PHA)
            for c in range(DC):
                ns = slice(c * 128, (c + 1) * 128)
                for tt in range(NT):
                    ts_ = slice(tt * TT, (tt + 1) * TT)
                    j = c * NT + tt
                    if j >= 2:
                        tensor.wait_ge(sa, A_EV(*divmod(j - 2, NT)))
                    for d in range(DC):
                        tensor.matmul(out=psA[j % 2][:], lhsT=wab[d][:, ns],
                                      rhs=xh[d][:, ts_], start=(d == 0),
                                      stop=(d == DC - 1))
                    for d in range(DC):
                        ins = tensor.matmul(out=psB[j % 2][:], lhsT=wbb[d][:, ns],
                                            rhs=xh[d][:, ts_], start=(d == 0),
                                            stop=(d == DC - 1))
                    ins.then_inc(sp, 1)

        @blk.scalar
        def _(scalar):
            af = FE(scalar, af_s)
            scalar.wait_ge(lfix, 32)
            for t in range(NT):
                ts_ = slice(t * TT, (t + 1) * TT)
                scalar.wait_ge(lx, 16 * (t + 1))
                if t >= 1:
                    scalar.wait_ge(srl, 32 * t)
                for d in range(DC):
                    ds = slice(d * TT, (d + 1) * TT)
                    g = t * DC + d
                    if g >= 2:
                        scalar.wait_ge(sp, P_SQ((g - 2) // DC, (g - 2) % DC))
                    scalar.activation(out=sq[d % 2][:], in_=xbuf[:, ds],
                                      func=AF.Square).then_inc(sa, 1)
                scalar.wait_ge(sv, V_VAR(t))
                af(scalar.activation(out=lnt[:], in_=vr[:], func=AF.Ln))
                scalar.activation(out=rr[:], in_=lnt[:], func=AF.Exp,
                                  scale=-0.5).then_inc(sa, 1)
                scalar.wait_ge(sa, A_RST(t))
                scalar.dma_start(out=stats_t[0:1, ts_], in_=mr[:]).then_inc(srl, 16)
                scalar.dma_start(out=stats_t[1:2, ts_], in_=rr[:]).then_inc(srl, 16)
            for c in range(DC):
                if c >= 1:
                    scalar.wait_ge(sv, V_SC(c - 1))
                    scalar.dma_start(out=hs0_t[(c - 1) * 128:c * 128, :],
                                     in_=hs0[:]).then_inc(sh[0], 16)
                    scalar.wait_ge(sv, V_PS(c - 1))
                    scalar.dma_start(out=P_t[(c - 1) * 128:c * 128, :],
                                     in_=Pb[:]).then_inc(sh[1], 16)
                for tt in range(NT):
                    ts_ = slice(tt * TT, (tt + 1) * TT)
                    j = c * NT + tt
                    scalar.wait_ge(sp, P_AB(c, tt))
                    if tt == 0 and c >= 1:
                        scalar.wait_ge(sv, V_PS(c - 1))
                    scalar.activation(out=alpha[:, ts_], in_=psA[j % 2][:],
                                      func=AF.Sigmoid, bias=bac[:, c:c + 1]
                                      ).then_inc(sa, 1)
                    scalar.activation(out=bu[:, ts_], in_=psB[j % 2][:],
                                      func=AF.Identity, bias=bbc[:, c:c + 1]
                                      ).then_inc(sa, 1)
            scalar.wait_ge(sv, V_SC(DC - 1))
            scalar.dma_start(out=hs0_t[(DC - 1) * 128:DC * 128, :], in_=hs0[:]
                             ).then_inc(sh[0], 16)
            scalar.wait_ge(sv, V_PS(DC - 1))
            scalar.dma_start(out=P_t[(DC - 1) * 128:DC * 128, :], in_=Pb[:]
                             ).then_inc(sh[1], 16)

    return nc





def build_L4a():
    nc = bass.Bass("TRN2", target_bir_lowering=False)
    x_t = nc.dram_tensor("x", [D, T], F32, kind="ExternalInput").ap()
    hs0_t = nc.dram_tensor("hs0", [D, T], F32, kind="ExternalInput").ap()
    P_t = nc.dram_tensor("P", [D, T], F32, kind="ExternalInput").ap()
    hin_t = nc.dram_tensor("hin", [128, DC], F32, kind="ExternalInput").ap()
    stats_t = nc.dram_tensor("stats", [2, T], F32, kind="ExternalInput").ap()
    cT_t = nc.dram_tensor("cT", [D, D], F32, kind="ExternalInput").ap()
    Dc_t = nc.dram_tensor("Dc", [128, DC], F32, kind="ExternalInput").ap()
    x3_t = nc.dram_tensor("x3", [D, T], F32, kind="ExternalOutput").ap()

    TT2 = 256
    NT2 = T // TT2
    with ExitStack() as st:
        S = st.enter_context
        mrow_t = S(nc.sbuf_tensor("mrow_t", [1, T], F32))
        rrow_t = S(nc.sbuf_tensor("rrow_t", [1, T], F32))
        hsld = S(nc.sbuf_tensor("hsld", [128, T], F32))
        pld = S(nc.sbuf_tensor("pld", [128, T], F32))
        hsb = [S(nc.sbuf_tensor(f"hsb{c}", [128, T], BF16)) for c in range(DC)]
        ctb = [S(nc.sbuf_tensor(f"ctb{c}", [128, D], BF16)) for c in range(DC)]
        cld = [S(nc.sbuf_tensor(f"cld{i}", [128, D], F32)) for i in range(2)]
        x2b = [S(nc.sbuf_tensor(f"x2b{i}", [128, DC * TT2], F32)) for i in range(2)]
        ystg = [S(nc.sbuf_tensor(f"ystg{i}", [128, DC * TT2], F32)) for i in range(2)]
        xhst = S(nc.sbuf_tensor("xhst", [128, DC * TT2], F32))
        hin = S(nc.sbuf_tensor("hin_sb", [128, DC], F32))
        Dc = S(nc.sbuf_tensor("Dc_sb", [128, DC], F32))
        oner = S(nc.sbuf_tensor("oner", [1, 128], F32))
        pbank = [S(nc.psum_tensor(f"pbank{i}", [128, 2 * TT2], F32)) for i in range(4)]
        pmm = [pbank[i // 2][:, (i % 2) * TT2:(i % 2 + 1) * TT2] for i in range(DC)]
        psmr = [S(nc.psum_tensor(f"psmr{i}", [128, 2 * TT2], F32)) for i in range(2)]
        psM = [psmr[i][:, 0:TT2] for i in range(2)]
        psR = [psmr[i][:, TT2:2 * TT2] for i in range(2)]
        lfix = S(nc.semaphore("lfix"))
        lx = [S(nc.semaphore(f"lx{i}")) for i in range(2)]
        gc = [S(nc.semaphore(f"gc{i}")) for i in range(2)]
        glh = S(nc.semaphore("glh"))
        glp = S(nc.semaphore("glp"))
        sx_ = [S(nc.semaphore(f"sx{i}")) for i in range(2)]
        sv = S(nc.semaphore("sv"))
        sp = S(nc.semaphore("sp"))
        sa = S(nc.semaphore("sa"))
        vf_s = S(nc.semaphore("vf"))
        blk = S(nc.Block())

        V_INIT = 1
        V_CB = lambda k: 2 + k
        V_HSB = lambda c: 10 + c
        V_X3 = lambda tt: 18 + tt
        V_FIN = V_X3(NT2 - 1)
        P_Y = lambda tt: tt + 1
        A_EV = lambda tt: 8 * (tt + 1)

        @blk.sync
        def _(sync):
            sync.dma_start(out=mrow_t[:], in_=stats_t[0:1, :]).then_inc(lfix, 16)
            sync.dma_start(out=rrow_t[:], in_=stats_t[1:2, :]).then_inc(lfix, 16)
            sync.dma_start(out=hin[:], in_=hin_t).then_inc(lfix, 16)
            sync.dma_start(out=Dc[:], in_=Dc_t).then_inc(lfix, 16)
            for tt in range(NT2):
                if tt >= 2:
                    sync.wait_ge(sv, V_X3(tt - 2))
                    sync.wait_ge(sx_[tt % 2], 16 * (tt // 2))
                sync.dma_start(
                    out=x2b[tt % 2][:].rearrange("p (c t) -> p c t", c=DC),
                    in_=x_t[:, tt * TT2:(tt + 1) * TT2].rearrange("(c p) t -> p c t", p=128),
                ).then_inc(lx[tt % 2], 16)
            sync.wait_ge(sv, V_FIN)
            sync.wait_ge(sx_[0], 16 * (NT2 // 2))
            sync.wait_ge(sx_[1], 16 * (NT2 // 2))

        @blk.gpsimd
        def _(gpsimd):
            for k in range(DC):
                if k >= 2:
                    gpsimd.wait_ge(sv, V_CB(k - 2))
                gpsimd.dma_start(out=cld[k % 2][:],
                                 in_=cT_t[k * 128:(k + 1) * 128, :]
                                 ).then_inc(gc[k % 2], 16)
            for c in range(DC):
                if c >= 1:
                    gpsimd.wait_ge(sv, V_HSB(c - 1))
                gpsimd.dma_start(out=hsld[:], in_=hs0_t[c * 128:(c + 1) * 128, :]
                                 ).then_inc(glh, 16)
                gpsimd.dma_start(out=pld[:], in_=P_t[c * 128:(c + 1) * 128, :]
                                 ).then_inc(glp, 16)

        @blk.vector
        def _(vector):
            vf = FE(vector, vf_s)
            vector.memset(oner[:], 1.0).then_inc(sv, 1)
            vector.wait_ge(sv, V_INIT)
            vector.wait_ge(lfix, 64)
            for k in range(DC):
                vector.wait_ge(gc[k % 2], 16 * (k // 2 + 1))
                vector.tensor_copy(out=ctb[k][:], in_=cld[k % 2][:]).then_inc(sv, 1)
            for c in range(DC):
                vector.wait_ge(glh, 16 * (c + 1))
                vector.wait_ge(glp, 16 * (c + 1))
                vector.scalar_tensor_tensor(out=hsb[c][:], in0=pld[:],
                                            scalar=hin[:, c:c + 1], in1=hsld[:],
                                            op0=ALU.mult, op1=ALU.add
                                            ).then_inc(sv, 1)
                vector.wait_ge(sv, V_HSB(c))
            for tt in range(NT2):
                ds_all = slice(0, DC * TT2)
                vector.wait_ge(lx[tt % 2], 16 * (tt // 2 + 1))
                vector.wait_ge(sp, P_Y(tt))
                vector.wait_ge(sa, A_EV(tt))
                if tt >= 2:
                    vector.wait_ge(sx_[tt % 2], 16 * (tt // 2))
                for d in range(DC):
                    ds = slice(d * TT2, (d + 1) * TT2)
                    ins = vector.tensor_sub(out=xhst[:, ds], in0=x2b[tt % 2][:, ds],
                                            in1=psM[tt % 2])
                vf(ins)
                for d in range(DC):
                    ds = slice(d * TT2, (d + 1) * TT2)
                    ins = vector.tensor_tensor(out=xhst[:, ds], in0=xhst[:, ds],
                                               in1=psR[tt % 2], op=ALU.mult)
                vf(ins)
                for d in range(DC):
                    ds = slice(d * TT2, (d + 1) * TT2)
                    ins = vector.scalar_tensor_tensor(out=ystg[tt % 2][:, ds],
                                                      in0=xhst[:, ds],
                                                      scalar=Dc[:, d:d + 1],
                                                      in1=ystg[tt % 2][:, ds],
                                                      op0=ALU.mult, op1=ALU.add)
                vf(ins)
                for d in range(DC):
                    ds = slice(d * TT2, (d + 1) * TT2)
                    ins = vector.tensor_add(out=x2b[tt % 2][:, ds],
                                            in0=ystg[tt % 2][:, ds],
                                            in1=x2b[tt % 2][:, ds])
                ins.then_inc(sv, 1)
                vector.wait_ge(sv, V_X3(tt))

        @blk.tensor
        def _(tensor):
            tensor.wait_ge(lfix, 64)
            tensor.wait_ge(sv, V_HSB(DC - 1))
            for tt in range(NT2):
                ts_ = slice(tt * TT2, (tt + 1) * TT2)
                if tt >= 1:
                    tensor.wait_ge(sa, A_EV(tt - 1))
                if tt >= 2:
                    tensor.wait_ge(sv, V_X3(tt - 2))
                for d in range(DC):
                    for c in range(DC):
                        tensor.matmul(out=pmm[d],
                                      lhsT=ctb[c][:, d * 128:(d + 1) * 128],
                                      rhs=hsb[c][:, ts_], start=(c == 0),
                                      stop=(c == DC - 1))
                tensor.matmul(out=psM[tt % 2], lhsT=oner[:], rhs=mrow_t[0:1, ts_],
                              start=True, stop=True)
                tensor.matmul(out=psR[tt % 2], lhsT=oner[:], rhs=rrow_t[0:1, ts_],
                              start=True, stop=True).then_inc(sp, 1)

        @blk.scalar
        def _(scalar):
            for tt in range(NT2):
                scalar.wait_ge(sp, P_Y(tt))
                if tt >= 2:
                    scalar.wait_ge(sv, V_X3(tt - 2))
                for d in range(DC):
                    ds = slice(d * TT2, (d + 1) * TT2)
                    scalar.activation(out=ystg[tt % 2][:, ds], in_=pmm[d],
                                      func=AF.Identity).then_inc(sa, 1)
                scalar.wait_ge(sv, V_X3(tt))
                scalar.dma_start(
                    out=x3_t[:, tt * TT2:(tt + 1) * TT2].rearrange("(c p) t -> p c t", p=128),
                    in_=x2b[tt % 2][:].rearrange("p (c t) -> p c t", c=DC),
                ).then_inc(sx_[tt % 2], 16)

    return nc



def _asc(a):
    return np.ascontiguousarray(a)


def _shard_u(u):
    out = []
    for c in range(NCORE):
        b, h = c // 2, c % 2
        out.append(_asc(u[b, h * T:(h + 1) * T, :].T))
    return out


_EYE = np.eye(128, dtype=np.float32)


def _bcol(v):
    return np.ascontiguousarray(np.asarray(v).reshape(-1, 128).T)


def _gather_mall(results, key="mloc"):
    m = np.concatenate([np.asarray(r[key]).reshape(1, 1) for r in results], axis=1)
    return np.ascontiguousarray(np.broadcast_to(m, (128, NCORE)))


def kernel(**inputs):
    u = np.asarray(inputs["L0_attn_u"])
    w1T = _asc(np.asarray(inputs["L0_ffn_w1"]).T)
    w2T = _asc(np.asarray(inputs["L0_ffn_w2"]).T)
    waT = _asc(np.asarray(inputs["L0_attn_W_alpha_w"]).T)
    wbT = _asc(np.asarray(inputs["L0_attn_W_B_w"]).T)
    cT = _asc(np.asarray(inputs["L0_attn_C_w"]).T)
    b1c = _bcol(inputs["L0_ffn_b1"])
    b2c = _bcol(inputs["L0_ffn_b2"])
    bac = _bcol(inputs["L0_attn_W_alpha_b"])
    bbc = _bcol(inputs["L0_attn_W_B_b"])
    Dcc = _bcol(inputs["L0_attn_D"])
    shards = _shard_u(u)
    cores = list(range(NCORE))

    nc_ln = build_L1()
    nc_ffn1 = build_L2()
    nc_ffn2 = build_L3a()
    nc_ssm1 = build_L3b()
    nc_ssm2 = build_L4a()

    r1 = run_bass_kernel_spmd(
        nc_ln, [{"u": shards[c], "eye": _EYE} for c in cores], core_ids=cores)
    m1 = _gather_mall(r1.results)

    r2 = run_bass_kernel_spmd(
        nc_ffn1, [{"u": shards[c], "stats": r1.results[c]["stats"], "mall": m1,
                   "w1T": w1T, "b1c": b1c, "eye": _EYE} for c in cores],
        core_ids=cores)
    m2 = _gather_mall(r2.results)

    r3a = run_bass_kernel_spmd(
        nc_ffn2, [{"x": shards[c], "h1": r2.results[c]["h1"], "mall": m2,
                   "w2T": w2T, "b2c": b2c, "eye": _EYE} for c in cores],
        core_ids=cores)

    r3b = run_bass_kernel_spmd(
        nc_ssm1, [{"x": r3a.results[c]["xo"], "waT": waT, "wbT": wbT,
                   "bac": bac, "bbc": bbc} for c in cores], core_ids=cores)

    zeros_hin = np.zeros((128, DC), np.float32)
    hins = [np.asarray(r3b.results[c - 1]["hend"]) if c % 2 == 1 else zeros_hin
            for c in cores]
    r4a = run_bass_kernel_spmd(
        nc_ssm2, [{"x": r3a.results[c]["xo"], "hs0": r3b.results[c]["hs0"],
                   "P": r3b.results[c]["P"], "hin": hins[c],
                   "stats": r3b.results[c]["stats"], "cT": cT, "Dc": Dcc}
                  for c in cores], core_ids=cores)

    r4b = run_bass_kernel_spmd(
        nc_ln, [{"u": r4a.results[c]["x3"], "eye": _EYE} for c in cores],
        core_ids=cores)
    m3 = _gather_mall(r4b.results)

    r5 = run_bass_kernel_spmd(
        nc_ffn1, [{"u": r4a.results[c]["x3"], "stats": r4b.results[c]["stats"],
                   "mall": m3, "w1T": w1T, "b1c": b1c, "eye": _EYE}
                  for c in cores], core_ids=cores)
    m4 = _gather_mall(r5.results)

    r6 = run_bass_kernel_spmd(
        nc_ffn2, [{"x": r4a.results[c]["x3"], "h1": r5.results[c]["h1"],
                   "mall": m4, "w2T": w2T, "b2c": b2c, "eye": _EYE}
                  for c in cores], core_ids=cores)

    out = np.empty((4, 8192, D), np.float32)
    for c in cores:
        b, h = c // 2, c % 2
        out[b, h * T:(h + 1) * T, :] = np.asarray(r6.results[c]["xo"]).T
    return out


# version 03
# speedup vs baseline: 1.1307x; 1.1307x over previous
import numpy as np
import ml_dtypes
from contextlib import ExitStack

import concourse.bass as bass
import concourse.mybir as mybir
from concourse.bass_utils import run_bass_kernel_spmd

F32 = mybir.dt.float32
BF16 = mybir.dt.bfloat16
I32 = mybir.dt.int32
AF = mybir.ActivationFunctionType
ALU = mybir.AluOpType
AX = mybir.AxisListType

NCORE = 8
D = 1024
T = 4096
H = 4096
DC = D // 128
HC = H // 128
TT = 512
NT = T // TT
EPS = 1e-5
QP = 127.0
MAGIC = 12582912.0
MINR = 1e-6
INV_D = 1.0 / D



def recip_newton(vec, out, in_, scr):
    vec.reciprocal(out=out, in_=in_)
    vec.tensor_tensor(out=scr, in0=in_, in1=out, op=ALU.mult)
    vec.tensor_scalar(out=scr, in0=scr, scalar1=-1.0, scalar2=2.0,
                      op0=ALU.mult, op1=ALU.add)
    vec.tensor_tensor(out=out, in0=out, in1=scr, op=ALU.mult)


def quant_tiles(vec, out_bf, in_f32, r_ap, scr_i32):
    vec.tensor_scalar(out=scr_i32, in0=in_f32, scalar1=r_ap, scalar2=None,
                      op0=ALU.mult)
    vec.tensor_copy(out=out_bf, in_=scr_i32)




class FE:

    def __init__(self, eng, sem):
        self.eng, self.sem, self.n = eng, sem, 0

    def __call__(self, ins):
        self.n += 1
        ins.then_inc(self.sem, 1)
        self.eng.wait_ge(self.sem, self.n)


def recip_newton_f(vec, fe, out, in_, scr):
    fe(vec.reciprocal(out=out, in_=in_))
    fe(vec.tensor_tensor(out=scr, in0=in_, in1=out, op=ALU.mult))
    fe(vec.tensor_scalar(out=scr, in0=scr, scalar1=-1.0, scalar2=2.0,
                         op0=ALU.mult, op1=ALU.add))
    fe(vec.tensor_tensor(out=out, in0=out, in1=scr, op=ALU.mult))



def build_L1():
    nc = bass.Bass("TRN2", target_bir_lowering=False)
    u_t = nc.dram_tensor("u", [D, T], F32, kind="ExternalInput").ap()
    eye_t = nc.dram_tensor("eye", [128, 128], F32, kind="ExternalInput").ap()
    stats_t = nc.dram_tensor("stats", [2, T], F32, kind="ExternalOutput").ap()
    mloc_t = nc.dram_tensor("mloc", [1, 1], F32, kind="ExternalOutput").ap()

    with ExitStack() as st:
        S = st.enter_context
        us = [S(nc.sbuf_tensor(f"u{d}", [128, T], F32)) for d in range(DC)]
        eye = S(nc.sbuf_tensor("eye_sb", [128, 128], F32))
        onec = S(nc.sbuf_tensor("onec", [128, 1], F32))
        oner = S(nc.sbuf_tensor("oner", [1, 128], F32))
        sq = [S(nc.sbuf_tensor(f"sq{i}", [128, TT], F32)) for i in range(2)]
        mrow = S(nc.sbuf_tensor("mrow", [1, T], F32))
        rrow = S(nc.sbuf_tensor("rrow", [1, T], F32))
        t0 = S(nc.sbuf_tensor("t0", [1, TT], F32))
        t1 = S(nc.sbuf_tensor("t1", [1, TT], F32))
        vrow = S(nc.sbuf_tensor("vrow", [1, T], F32))
        lnt = S(nc.sbuf_tensor("lnt", [1, TT], F32))
        tmp8 = S(nc.sbuf_tensor("tmp8", [128, T], F32))
        red8 = S(nc.sbuf_tensor("red8", [128, DC], F32))
        rt = S(nc.sbuf_tensor("rt", [128, 1], F32))
        mx = S(nc.sbuf_tensor("mx", [128, 1], F32))
        sc = S(nc.sbuf_tensor("sc", [1, 1], F32))
        psA = S(nc.psum_tensor("psA", [128, TT], F32))
        psB = [S(nc.psum_tensor(f"psB{i}", [128, TT], F32)) for i in range(2)]
        psM = [S(nc.psum_tensor(f"psM{i}", [128, TT], F32)) for i in range(2)]
        psR = [S(nc.psum_tensor(f"psR{i}", [128, TT], F32)) for i in range(2)]
        ld = S(nc.semaphore("ld"))
        sa = S(nc.semaphore("sa"))
        sv = S(nc.semaphore("sv"))
        sp = S(nc.semaphore("sp"))
        so = S(nc.semaphore("so"))
        vf_s = S(nc.semaphore("vf"))
        blk = S(nc.Block())

        @blk.sync
        def _(sync):
            for d in range(DC):
                sync.dma_start(out=us[d][:], in_=u_t[d * 128:(d + 1) * 128, :]
                               ).then_inc(ld, 16)
            sync.dma_start(out=eye[:], in_=eye_t).then_inc(ld, 16)
            sync.wait_ge(sv, 2 + NT + 3 * NT + NT + 1)
            sync.dma_start(out=stats_t[0:1, :], in_=mrow[:]).then_inc(so, 16)
            sync.dma_start(out=stats_t[1:2, :], in_=rrow[:]).then_inc(so, 16)
            sync.dma_start(out=mloc_t, in_=sc[:]).then_inc(so, 16)
            sync.wait_ge(so, 48)

        SV_INIT = 2
        SV_A1 = SV_INIT + NT
        SV_A2 = SV_A1 + 3 * NT
        SV_B = SV_A2 + NT
        SV_FIN = SV_B + 1

        @blk.tensor
        def _(tensor):
            tensor.wait_ge(ld, 144)
            tensor.wait_ge(sv, 2)
            for t in range(NT):
                if t >= 1:
                    tensor.wait_ge(sv, SV_INIT + t)
                ts = slice(t * TT, (t + 1) * TT)
                for d in range(DC):
                    ins = tensor.matmul(out=psA[0:1, :], lhsT=onec[:],
                                        rhs=us[d][:, ts],
                                        start=(d == 0), stop=(d == DC - 1))
                ins.then_inc(sp, 1)
            nsq = 0
            for t in range(NT):
                for d in range(DC):
                    nsq += 1
                    tensor.wait_ge(sa, nsq)
                    if d == 0 and t >= 2:
                        tensor.wait_ge(sv, SV_A1 + 3 * (t - 1))
                    tensor.matmul(out=psB[t % 2][0:1, :], lhsT=onec[:],
                                  rhs=sq[nsq % 2][:], start=(d == 0),
                                  stop=(d == DC - 1)).then_inc(sp, 1)
            for t in range(NT):
                ts = slice(t * TT, (t + 1) * TT)
                tensor.wait_ge(sa, NT * DC + 2 * (t + 1))
                if t >= 2:
                    tensor.wait_ge(sv, SV_A2 + (t - 1))
                tensor.matmul(out=psM[t % 2][:], lhsT=oner[:], rhs=mrow[0:1, ts],
                              start=True, stop=True)
                tensor.matmul(out=psR[t % 2][:], lhsT=oner[:], rhs=rrow[0:1, ts],
                              start=True, stop=True).then_inc(sp, 1)
            tensor.wait_ge(sv, SV_B)
            tensor.transpose(out=psA[0:1, 0:128], in_=mx[:], identity=eye[:]
                             ).then_inc(sp, 1)

        @blk.scalar
        def _(scalar):
            scalar.wait_ge(ld, 144)
            n = 0
            for t in range(NT):
                ts = slice(t * TT, (t + 1) * TT)
                for d in range(DC):
                    n += 1
                    if n > 2:
                        scalar.wait_ge(sp, NT + n - 2)
                    scalar.activation(out=sq[n % 2][:], in_=us[d][:, ts],
                                      func=AF.Square).then_inc(sa, 1)
            for t in range(NT):
                ts = slice(t * TT, (t + 1) * TT)
                scalar.wait_ge(sv, 2 + NT + 3 * (t + 1))
                scalar.activation(out=lnt[:], in_=vrow[0:1, ts], func=AF.Ln
                                  ).then_inc(sa, 1)
                scalar.wait_ge(sa, NT * DC + 2 * t + 1)
                scalar.activation(out=rrow[0:1, ts], in_=lnt[:], func=AF.Exp,
                                  scale=-0.5).then_inc(sa, 1)

        @blk.vector
        def _(vector):
            vf = FE(vector, vf_s)
            vector.memset(onec[:], 1.0).then_inc(sv, 1)
            vector.memset(oner[:], 1.0)
            vector.memset(mx[:], 0.0).then_inc(sv, 1)
            vector.wait_ge(sv, SV_INIT)
            for t in range(NT):
                ts = slice(t * TT, (t + 1) * TT)
                vector.wait_ge(sp, t + 1)
                vector.tensor_scalar(out=mrow[0:1, ts], in0=psA[0:1, :],
                                     scalar1=INV_D, scalar2=None,
                                     op0=ALU.mult).then_inc(sv, 1)
            vector.wait_ge(sv, SV_A1)
            for t in range(NT):
                ts = slice(t * TT, (t + 1) * TT)
                vector.wait_ge(sp, NT + DC * (t + 1))
                vector.tensor_scalar(out=t0[:], in0=psB[t % 2][0:1, :],
                                     scalar1=INV_D, scalar2=None, op0=ALU.mult
                                     ).then_inc(sv, 1)
                vector.tensor_tensor(out=t1[:], in0=mrow[0:1, ts],
                                     in1=mrow[0:1, ts], op=ALU.mult).then_inc(sv, 1)
                vector.wait_ge(sv, SV_A1 + 3 * t + 2)
                vector.scalar_tensor_tensor(out=vrow[0:1, ts], in0=t0[:],
                                            scalar=EPS, in1=t1[:], op0=ALU.add,
                                            op1=ALU.subtract).then_inc(sv, 1)
                vector.wait_ge(sv, SV_A1 + 3 * t + 3)
            for t in range(NT):
                ts = slice(t * TT, (t + 1) * TT)
                vector.wait_ge(sp, NT + DC * NT + (t + 1))
                for d in range(DC):
                    ds = slice(d * TT, (d + 1) * TT)
                    ins = vector.tensor_sub(out=tmp8[:, ds], in0=us[d][:, ts],
                                            in1=psM[t % 2][:])
                vf(ins)
                for d in range(DC):
                    ds = slice(d * TT, (d + 1) * TT)
                    ins = vector.tensor_tensor(out=tmp8[:, ds], in0=tmp8[:, ds],
                                               in1=psR[t % 2][:], op=ALU.mult)
                vf(ins)
                for d in range(DC):
                    ds = slice(d * TT, (d + 1) * TT)
                    ins = vector.tensor_reduce(out=red8[:, d:d + 1], in_=tmp8[:, ds],
                                               axis=AX.X, op=ALU.max,
                                               apply_absolute_value=True)
                vf(ins)
                vf(vector.tensor_reduce(out=rt[:], in_=red8[:], axis=AX.X,
                                        op=ALU.max, apply_absolute_value=True))
                vector.tensor_max(out=mx[:], in0=mx[:], in1=rt[:]).then_inc(sv, 1)
                vector.wait_ge(sv, SV_A2 + t + 1)
            vector.wait_ge(sp, NT + DC * NT + NT + 1)
            vector.tensor_reduce(out=sc[:], in_=psA[0:1, 0:128], axis=AX.X,
                                 op=ALU.max, apply_absolute_value=True
                                 ).then_inc(sv, 1)

    return nc





def build_L2(prequant=False):
    nc = bass.Bass("TRN2", target_bir_lowering=False)
    u_t = nc.dram_tensor("u", [D, T], F32, kind="ExternalInput").ap()
    stats_t = nc.dram_tensor("stats", [2, T], F32, kind="ExternalInput").ap()
    mall_t = nc.dram_tensor("mall", [128, NCORE], F32, kind="ExternalInput").ap()
    if prequant:
        wq1i_t = nc.dram_tensor("wq1i", [D, H], BF16, kind="ExternalInput").ap()
        swi_t = nc.dram_tensor("swi", [1, 1], F32, kind="ExternalInput").ap()
    else:
        w1T_t = nc.dram_tensor("w1T", [D, H], F32, kind="ExternalInput").ap()
    b1c_t = nc.dram_tensor("b1c", [128, HC], F32, kind="ExternalInput").ap()
    eye_t = nc.dram_tensor("eye", [128, 128], F32, kind="ExternalInput").ap()
    h1_t = nc.dram_tensor("h1", [H, T], F32, kind="ExternalOutput").ap()
    if not prequant:
        wq1_t = nc.dram_tensor("wq1", [D, H], BF16, kind="ExternalOutput").ap()
        sw_t = nc.dram_tensor("sw", [1, 1], F32, kind="ExternalOutput").ap()
    mloc_t = nc.dram_tensor("mloc", [1, 1], F32, kind="ExternalOutput").ap()
    wq_src_t = wq1i_t if prequant else wq1_t

    NG = HC * NT
    NB = NG // 4
    with ExitStack() as st:
        S = st.enter_context
        mrow_t = S(nc.sbuf_tensor("mrow_t", [1, T], F32))
        rrow_t = S(nc.sbuf_tensor("rrow_t", [1, T], F32))
        mrow = mrow_t[0:1, :]
        rrow = rrow_t[0:1, :]
        ubuf = [S(nc.sbuf_tensor(f"ub{i}", [128, T], F32)) for i in range(2)]
        wbuf = [S(nc.sbuf_tensor(f"wb{i}", [128, H], F32)) for i in range(2)]
        xq = [S(nc.sbuf_tensor(f"xq{d}", [128, T], BF16)) for d in range(DC)]
        wqs = [S(nc.sbuf_tensor(f"wqs{i}", [128, DC * 128], BF16)) for i in range(2)]
        wqo = [S(nc.sbuf_tensor(f"wqo{i}", [128, H], BF16)) for i in range(2)]
        wint = S(nc.sbuf_tensor("wint", [128, H // 4], I32))
        xint = S(nc.sbuf_tensor("xint", [128, TT], I32))
        qf = S(nc.sbuf_tensor("qf", [128, TT], F32))
        red8 = S(nc.sbuf_tensor("red8", [128, DC], F32))
        rt2 = S(nc.sbuf_tensor("rt2", [128, 1], F32))
        eye = S(nc.sbuf_tensor("eye_sb", [128, 128], F32))
        oner = S(nc.sbuf_tensor("oner", [1, 128], F32))
        mallb = S(nc.sbuf_tensor("mallb", [128, NCORE], F32))
        b1c = S(nc.sbuf_tensor("b1c_sb", [128, HC], F32))
        b1s = S(nc.sbuf_tensor("b1s", [128, HC], F32))
        tmpq = S(nc.sbuf_tensor("tmpq", [128, T], F32))
        wmx = S(nc.sbuf_tensor("wmx", [128, 1], F32))
        gmx = S(nc.sbuf_tensor("gmx", [128, 1], F32))
        mxm = S(nc.sbuf_tensor("mxm", [128, 1], F32))
        sxb = S(nc.sbuf_tensor("sxb", [128, 1], F32))
        rxb = S(nc.sbuf_tensor("rxb", [128, 1], F32))
        s1b = S(nc.sbuf_tensor("s1b", [128, 1], F32))
        swb = S(nc.sbuf_tensor("swb", [128, 1], F32))
        rwb = S(nc.sbuf_tensor("rwb", [128, 1], F32))
        sc1 = S(nc.sbuf_tensor("sc1", [128, 1], F32))
        wsc = S(nc.sbuf_tensor("wsc", [1, 1], F32))
        sw_s = S(nc.sbuf_tensor("sw_s", [1, 1], F32))
        rw1 = S(nc.sbuf_tensor("rw1", [1, 1], F32))
        r1a = S(nc.sbuf_tensor("r1a", [1, 1], F32))
        srow = S(nc.sbuf_tensor("srow", [1, 2], F32))
        mgl = S(nc.sbuf_tensor("mgl", [1, 1], F32))
        pmm = [S(nc.psum_tensor(f"pmm{i}", [128, TT], F32)) for i in range(4)]
        paux = S(nc.psum_tensor("paux", [128, 128], F32))
        pbc = S(nc.psum_tensor("pbc", [128, 2], F32))
        lfix = S(nc.semaphore("lfix"))
        lu = [S(nc.semaphore(f"lu{i}")) for i in range(2)]
        gw = [S(nc.semaphore(f"gw{i}")) for i in range(2)]
        gq = [S(nc.semaphore(f"gq{i}")) for i in range(2)]
        sq = [S(nc.semaphore(f"sq{i}")) for i in range(2)]
        sh = [S(nc.semaphore(f"sh{i}")) for i in range(2)]
        sv = S(nc.semaphore("sv"))
        sp = S(nc.semaphore("sp"))
        sa = S(nc.semaphore("sa"))
        so = S(nc.semaphore("so"))
        vf_s = S(nc.semaphore("vf"))
        af_s = S(nc.semaphore("af"))
        blk = S(nc.Block())

        V_ONES = 1
        V_WCH = lambda d: 2 + d
        V_WMX = 10
        V_SROW = 11
        V_SCAL = 12
        V_WQ = lambda d: 13 + d
        V_XQ = lambda t: 21 + t
        V_G = lambda g: 29 + g
        V_GDONE = 29 + NG
        V_FIN = 30 + NG
        P_TR = 1
        P_BC = 2
        P_B = lambda t: 3 + t
        P_G = lambda g: 11 + g
        P_FINT = 11 + NG

        @blk.sync
        def _(sync):
            sync.dma_start(out=mrow_t[:], in_=stats_t[0:1, :]).then_inc(lfix, 16)
            sync.dma_start(out=rrow_t[:], in_=stats_t[1:2, :]).then_inc(lfix, 16)
            sync.dma_start(out=mallb[:], in_=mall_t).then_inc(lfix, 16)
            sync.dma_start(out=b1c[:], in_=b1c_t).then_inc(lfix, 16)
            sync.dma_start(out=eye[:], in_=eye_t).then_inc(lfix, 16)
            if prequant:
                sync.dma_start(out=sw_s[:], in_=swi_t).then_inc(lfix, 16)
            for t in range(NT):
                if t >= 2:
                    sync.wait_ge(sv, V_XQ(t - 2))
                sync.dma_start(
                    out=ubuf[t % 2][:].rearrange("p (c t) -> p c t", c=DC),
                    in_=u_t[:, t * TT:(t + 1) * TT].rearrange("(c p) t -> p c t", p=128),
                ).then_inc(lu[t % 2], 16)
            sync.wait_ge(sv, V_FIN)
            sync.dma_start(out=mloc_t, in_=mgl[:]).then_inc(so, 16)
            if not prequant:
                sync.dma_start(out=sw_t, in_=sw_s[:]).then_inc(so, 16)
                sync.wait_ge(so, 32)
                sync.wait_ge(sq[0], 16 * (DC // 2))
                sync.wait_ge(sq[1], 16 * (DC // 2))
            else:
                sync.wait_ge(so, 16)
            sync.wait_ge(sh[0], 16 * (NB // 2))
            sync.wait_ge(sh[1], 16 * (NB // 2))

        @blk.gpsimd
        def _(gpsimd):
            if not prequant:
                for d in range(DC):
                    if d >= 2:
                        gpsimd.wait_ge(sv, V_WCH(d - 2))
                    gpsimd.dma_start(out=wbuf[d % 2][:],
                                     in_=w1T_t[d * 128:(d + 1) * 128, :]
                                     ).then_inc(gw[d % 2], 16)
                gpsimd.wait_ge(sv, V_WCH(DC - 1))
                for d in range(DC):
                    if d >= 2:
                        gpsimd.wait_ge(sv, V_WQ(d - 2))
                    gpsimd.dma_start(out=wbuf[d % 2][:],
                                     in_=w1T_t[d * 128:(d + 1) * 128, :]
                                     ).then_inc(gw[d % 2], 16)
                gpsimd.wait_ge(sq[0], 16 * (DC // 2))
                gpsimd.wait_ge(sq[1], 16 * (DC // 2))
            for h in range(HC):
                if h >= 2:
                    gpsimd.wait_ge(sp, P_G((h - 1) * NT - 1))
                gpsimd.dma_start(
                    out=wqs[h % 2][:].rearrange("p (c m) -> p c m", c=DC),
                    in_=wq_src_t[:, h * 128:(h + 1) * 128].rearrange("(c p) m -> p c m", p=128),
                ).then_inc(gq[h % 2], 16)

        @blk.vector
        def _(vector):
            vf = FE(vector, vf_s)
            vector.memset(oner[:], 1.0)
            vector.memset(wmx[:], 0.0)
            vector.memset(gmx[:], 0.0).then_inc(sv, 1)
            vector.wait_ge(sv, V_ONES)
            vector.wait_ge(lfix, 16 * (6 if prequant else 5))
            vf(vector.tensor_reduce(out=mxm[:], in_=mallb[:], axis=AX.X, op=ALU.max,
                                    apply_absolute_value=True))
            vf(vector.tensor_scalar(out=sxb[:], in0=mxm[:], scalar1=MINR,
                                    scalar2=1.0 / QP, op0=ALU.max, op1=ALU.mult))
            recip_newton_f(vector, vf, rxb[:], sxb[:], sc1[:])
            vf(vector.tensor_scalar(out=rrow, in0=rrow, scalar1=rxb[0:1, 0:1],
                                    scalar2=None, op0=ALU.mult))
            if not prequant:
                for d in range(DC):
                    vector.wait_ge(gw[d % 2], 16 * (d // 2 + 1))
                    vector.tensor_reduce(out=red8[:, d:d + 1], in_=wbuf[d % 2][:],
                                         axis=AX.X, op=ALU.max,
                                         apply_absolute_value=True).then_inc(sv, 1)
                vector.wait_ge(sv, V_WCH(DC - 1))
                vector.tensor_reduce(out=wmx[:], in_=red8[:], axis=AX.X, op=ALU.max,
                                     apply_absolute_value=True).then_inc(sv, 1)
                vector.wait_ge(sv, V_WMX)
                vector.wait_ge(sp, P_TR)
                vf(vector.tensor_reduce(out=wsc[:], in_=paux[0:1, 0:128], axis=AX.X,
                                        op=ALU.max, apply_absolute_value=True))
                vf(vector.tensor_scalar(out=sw_s[:], in0=wsc[:], scalar1=MINR,
                                        scalar2=1.0 / QP, op0=ALU.max, op1=ALU.mult))
            else:
                for _ in range(DC + 1):
                    vector.nop().then_inc(sv, 1)
                vector.wait_ge(sv, V_WMX)
            recip_newton_f(vector, vf, rw1[:], sw_s[:], r1a[:])
            vector.tensor_copy(out=srow[0:1, 0:1], in_=sw_s[:])
            vector.tensor_copy(out=srow[0:1, 1:2], in_=rw1[:]).then_inc(sv, 1)
            vector.wait_ge(sp, P_BC)
            vector.tensor_copy(out=swb[:], in_=pbc[:, 0:1])
            vf(vector.tensor_copy(out=rwb[:], in_=pbc[:, 1:2]))
            vf(vector.tensor_tensor(out=s1b[:], in0=swb[:], in1=sxb[:], op=ALU.mult))
            vector.tensor_scalar(out=b1s[:], in0=b1c[:], scalar1=s1b[:, 0:1],
                                 scalar2=None, op0=ALU.mult).then_inc(sv, 1)
            for d in range(DC):
                if prequant:
                    vector.nop().then_inc(sv, 1)
                    continue
                vector.wait_ge(gw[d % 2], 16 * (4 + d // 2 + 1))
                if d >= 2:
                    vector.wait_ge(sq[d % 2], 16 * (d // 2))
                for k in range(4):
                    hs = slice(k * (H // 4), (k + 1) * (H // 4))
                    wf = tmpq[:, 0:H // 4]
                    vf(vector.tensor_scalar(out=wf, in0=wbuf[d % 2][:, hs],
                                            scalar1=rwb[:, 0:1], scalar2=MAGIC,
                                            op0=ALU.mult, op1=ALU.add))
                    vf(vector.tensor_scalar(out=wint[:], in0=wf, scalar1=MAGIC,
                                            scalar2=None, op0=ALU.subtract))
                    ins = vector.tensor_copy(out=wqo[d % 2][:, hs], in_=wint[:])
                    if k < 3:
                        vf(ins)
                ins.then_inc(sv, 1)
                vector.wait_ge(sv, V_WQ(d))
            for t in range(NT):
                ts_ = slice(t * TT, (t + 1) * TT)
                vector.wait_ge(lu[t % 2], 16 * (t // 2 + 1))
                vector.wait_ge(sp, P_B(t))
                for d in range(DC):
                    ds = slice(d * TT, (d + 1) * TT)
                    ins = vector.tensor_sub(out=tmpq[:, ds], in0=ubuf[t % 2][:, ds],
                                            in1=pmm[2 * (t % 2)][:])
                vf(ins)
                for d in range(DC):
                    ds = slice(d * TT, (d + 1) * TT)
                    vf(vector.tensor_tensor(out=qf[:], in0=tmpq[:, ds],
                                            in1=pmm[2 * (t % 2) + 1][:],
                                            op=ALU.mult))
                    vf(vector.tensor_scalar(out=xint[:], in0=qf[:], scalar1=MAGIC,
                                            scalar2=MAGIC, op0=ALU.add,
                                            op1=ALU.subtract))
                    ins = vector.tensor_copy(out=xq[d][:, ts_], in_=xint[:])
                    if d < DC - 1:
                        vf(ins)
                ins.then_inc(sv, 1)
                vector.wait_ge(sv, V_XQ(t))
            for g in range(NG):
                vector.wait_ge(sa, g + 1)
                stg = wbuf[(g // 4) % 2]
                col = (g % 4) * TT
                vector.tensor_reduce(out=red8[:, g % 8:g % 8 + 1],
                                     in_=stg[:, col:col + TT], axis=AX.X,
                                     op=ALU.max, apply_absolute_value=True
                                     ).then_inc(sv, 1)
                if g % 8 == 7:
                    vector.wait_ge(sv, V_G(g))
                    vf(vector.tensor_reduce(out=rt2[:], in_=red8[:], axis=AX.X,
                                            op=ALU.max, apply_absolute_value=True))
                    if g == NG - 1:
                        vector.tensor_max(out=gmx[:], in0=gmx[:], in1=rt2[:]
                                          ).then_inc(sv, 1)
                    else:
                        vf(vector.tensor_max(out=gmx[:], in0=gmx[:], in1=rt2[:]))
            vector.wait_ge(sp, P_FINT)
            vector.tensor_reduce(out=mgl[:], in_=paux[0:1, 0:128], axis=AX.X,
                                 op=ALU.max, apply_absolute_value=True).then_inc(sv, 1)

        @blk.tensor
        def _(tensor):
            tensor.wait_ge(lfix, 16 * (6 if prequant else 5))
            if not prequant:
                tensor.wait_ge(sv, V_WMX)
                tensor.transpose(out=paux[0:1, 0:128], in_=wmx[:], identity=eye[:]
                                 ).then_inc(sp, 1)
            else:
                tensor.nop().then_inc(sp, 1)
            tensor.wait_ge(sv, V_SROW)
            tensor.matmul(out=pbc[:], lhsT=oner[:], rhs=srow[:], start=True,
                          stop=True).then_inc(sp, 1)
            for t in range(NT):
                ts_ = slice(t * TT, (t + 1) * TT)
                if t >= 2:
                    tensor.wait_ge(sv, V_XQ(t - 2))
                tensor.matmul(out=pmm[2 * (t % 2)][:], lhsT=oner[:],
                              rhs=mrow[:, ts_], start=True, stop=True)
                tensor.matmul(out=pmm[2 * (t % 2) + 1][:], lhsT=oner[:],
                              rhs=rrow[:, ts_], start=True, stop=True
                              ).then_inc(sp, 1)
            tensor.wait_ge(sv, V_XQ(NT - 1))
            g = 0
            for h in range(HC):
                tensor.wait_ge(gq[h % 2], 16 * (h // 2 + 1))
                for t in range(NT):
                    ts_ = slice(t * TT, (t + 1) * TT)
                    if g >= 4:
                        tensor.wait_ge(sa, g - 3)
                    for d in range(DC):
                        ins = tensor.matmul(out=pmm[g % 4][:],
                                            lhsT=wqs[h % 2][:, d * 128:(d + 1) * 128],
                                            rhs=xq[d][:, ts_], start=(d == 0),
                                            stop=(d == DC - 1))
                    ins.then_inc(sp, 1)
                    g += 1
            tensor.wait_ge(sv, V_GDONE)
            tensor.transpose(out=paux[0:1, 0:128], in_=gmx[:], identity=eye[:]
                             ).then_inc(sp, 1)

        @blk.scalar
        def _(scalar):
            if not prequant:
                for d in range(DC):
                    scalar.wait_ge(sv, V_WQ(d))
                    scalar.dma_start(out=wq1_t[d * 128:(d + 1) * 128, :],
                                     in_=wqo[d % 2][:]).then_inc(sq[d % 2], 16)
            scalar.wait_ge(sv, V_SCAL)
            scalar.wait_ge(sv, V_WQ(DC - 1))
            g = 0
            for h in range(HC):
                for t in range(NT):
                    b = g // 4
                    stg = wbuf[b % 2]
                    col = (g % 4) * TT
                    if g % 4 == 0 and b >= 2:
                        scalar.wait_ge(sh[b % 2], 16 * (b // 2))
                        scalar.wait_ge(sv, V_G(4 * (b - 1) - 1))
                    scalar.wait_ge(sp, P_G(g))
                    scalar.activation(out=stg[:, col:col + TT],
                                      in_=pmm[g % 4][:], func=AF.Gelu,
                                      bias=b1s[:, h:h + 1],
                                      scale=s1b[:, 0:1]).then_inc(sa, 1)
                    if g % 4 == 3:
                        scalar.wait_ge(sa, g + 1)
                        th = (t - 3) // 4
                        scalar.dma_start(
                            out=h1_t[h * 128:(h + 1) * 128,
                                     th * 2048:(th + 1) * 2048],
                            in_=stg[:, 0:2048]).then_inc(sh[b % 2], 16)
                    g += 1

    return nc





def build_L3a(prequant=False):
    nc = bass.Bass("TRN2", target_bir_lowering=False)
    x_t = nc.dram_tensor("x", [D, T], F32, kind="ExternalInput").ap()
    h1_t = nc.dram_tensor("h1", [H, T], F32, kind="ExternalInput").ap()
    mall_t = nc.dram_tensor("mall", [128, NCORE], F32, kind="ExternalInput").ap()
    if prequant:
        wq2i_t = nc.dram_tensor("wq2i", [H, D], BF16, kind="ExternalInput").ap()
        swi_t = nc.dram_tensor("swi", [1, 1], F32, kind="ExternalInput").ap()
    else:
        w2T_t = nc.dram_tensor("w2T", [H, D], F32, kind="ExternalInput").ap()
    b2c_t = nc.dram_tensor("b2c", [128, DC], F32, kind="ExternalInput").ap()
    eye_t = nc.dram_tensor("eye", [128, 128], F32, kind="ExternalInput").ap()
    xo_t = nc.dram_tensor("xo", [D, T], F32, kind="ExternalOutput").ap()
    if not prequant:
        wq2_t = nc.dram_tensor("wq2", [H, D], BF16, kind="ExternalOutput").ap()
        sw_t = nc.dram_tensor("sw", [1, 1], F32, kind="ExternalOutput").ap()

    NW = HC
    NQ = 4
    with ExitStack() as st:
        S = st.enter_context
        wbuf = [S(nc.sbuf_tensor(f"wb{i}", [128, D], F32)) for i in range(2)]
        wq2r = [S(nc.sbuf_tensor(f"wq2r{h}", [128, D], BF16)) for h in range(NW)]
        hbuf = [S(nc.sbuf_tensor(f"hb{i}", [128, DC * TT], F32)) for i in range(2)]
        hq = [S(nc.sbuf_tensor(f"hq{i}", [128, DC * TT], BF16)) for i in range(2)]
        ubuf = [S(nc.sbuf_tensor(f"ub{i}", [128, T], F32)) for i in range(2)]
        xo = [S(nc.sbuf_tensor(f"xo_sb{i}", [128, T], F32)) for i in range(2)]
        wint = S(nc.sbuf_tensor("wint", [128, D], I32))
        wf = S(nc.sbuf_tensor("wf", [128, D], F32))
        xint = S(nc.sbuf_tensor("xint", [128, TT], I32))
        qf = S(nc.sbuf_tensor("qf", [128, TT], F32))
        eye = S(nc.sbuf_tensor("eye_sb", [128, 128], F32))
        oner = S(nc.sbuf_tensor("oner", [1, 128], F32))
        mallb = S(nc.sbuf_tensor("mallb", [128, NCORE], F32))
        b2c = S(nc.sbuf_tensor("b2c_sb", [128, DC], F32))
        b2s = S(nc.sbuf_tensor("b2s", [128, DC], F32))
        red8 = S(nc.sbuf_tensor("red8", [128, NW], F32))
        wmx = S(nc.sbuf_tensor("wmx", [128, 1], F32))
        mxm = S(nc.sbuf_tensor("mxm", [128, 1], F32))
        sxb = S(nc.sbuf_tensor("sxb", [128, 1], F32))
        rxb = S(nc.sbuf_tensor("rxb", [128, 1], F32))
        s5b = S(nc.sbuf_tensor("s5b", [128, 1], F32))
        swb = S(nc.sbuf_tensor("swb", [128, 1], F32))
        rwb = S(nc.sbuf_tensor("rwb", [128, 1], F32))
        sc1 = S(nc.sbuf_tensor("sc1", [128, 1], F32))
        wsc = S(nc.sbuf_tensor("wsc", [1, 1], F32))
        sw_s = S(nc.sbuf_tensor("sw_s", [1, 1], F32))
        rw1 = S(nc.sbuf_tensor("rw1", [1, 1], F32))
        r1a = S(nc.sbuf_tensor("r1a", [1, 1], F32))
        srow = S(nc.sbuf_tensor("srow", [1, 2], F32))
        pmm = [S(nc.psum_tensor(f"pmm{i}", [128, TT], F32)) for i in range(DC)]
        lfix = S(nc.semaphore("lfix"))
        lu = [S(nc.semaphore(f"lu{i}")) for i in range(2)]
        lh = [S(nc.semaphore(f"lh{i}")) for i in range(2)]
        gw = [S(nc.semaphore(f"gw{i}")) for i in range(2)]
        sx_ = [S(nc.semaphore(f"sx{i}")) for i in range(2)]
        swq = [S(nc.semaphore(f"swq{i}")) for i in range(2)]
        so_ = S(nc.semaphore("so_"))
        sv = S(nc.semaphore("sv"))
        sp = S(nc.semaphore("sp"))
        sa = S(nc.semaphore("sa"))
        vf_s = S(nc.semaphore("vf"))
        blk = S(nc.Block())

        V_ONES = 1
        V_WCH = lambda h: 2 + h
        V_WMX = 2 + NW
        V_SROW = V_WMX + 1
        V_SCAL = V_SROW + 1
        V_WQ = lambda h: V_SCAL + 1 + h
        V_HQ = lambda t, q: V_WQ(NW - 1) + 1 + t * (NQ + 1) + q
        V_XO = lambda t: V_WQ(NW - 1) + 1 + t * (NQ + 1) + NQ
        V_FIN = V_XO(NT - 1)
        V_SWST = V_SROW
        P_TR = 1
        P_BC = 2
        P_MM = lambda t, q: 3 + t * NQ + q
        P_T = lambda t: P_MM(t, NQ - 1)

        @blk.sync
        def _(sync):
            sync.dma_start(out=mallb[:], in_=mall_t).then_inc(lfix, 16)
            sync.dma_start(out=b2c[:], in_=b2c_t).then_inc(lfix, 16)
            sync.dma_start(out=eye[:], in_=eye_t).then_inc(lfix, 16)
            if prequant:
                sync.dma_start(out=sw_s[:], in_=swi_t).then_inc(lfix, 16)
            for t in range(NT):
                if t >= 2:
                    sync.wait_ge(sv, V_XO(t - 2))
                sync.dma_start(
                    out=ubuf[t % 2][:].rearrange("p (c t) -> p c t", c=DC),
                    in_=x_t[:, t * TT:(t + 1) * TT].rearrange("(c p) t -> p c t", p=128),
                ).then_inc(lu[t % 2], 16)
            sync.wait_ge(sv, V_FIN)
            if not prequant:
                sync.wait_ge(sv, V_SWST)
                sync.dma_start(out=sw_t, in_=sw_s[:]).then_inc(so_, 16)
                sync.wait_ge(so_, 16)
                sync.wait_ge(swq[0], 16 * (NW // 2))
                sync.wait_ge(swq[1], 16 * (NW // 2))
            sync.wait_ge(sx_[0], 16 * (NT // 2))
            sync.wait_ge(sx_[1], 16 * (NT // 2))

        @blk.gpsimd
        def _(gpsimd):
            if prequant:
                for h in range(NW):
                    if h >= 2:
                        gpsimd.wait_ge(sv, V_WQ(h - 2))
                    gpsimd.dma_start(out=wq2r[h][:],
                                     in_=wq2i_t[h * 128:(h + 1) * 128, :]
                                     ).then_inc(gw[h % 2], 16)
            else:
                for h in range(NW):
                    if h >= 2:
                        gpsimd.wait_ge(sv, V_WCH(h - 2))
                    gpsimd.dma_start(out=wbuf[h % 2][:],
                                     in_=w2T_t[h * 128:(h + 1) * 128, :]
                                     ).then_inc(gw[h % 2], 16)
                gpsimd.wait_ge(sv, V_WCH(NW - 1))
                for h in range(NW):
                    if h >= 2:
                        gpsimd.wait_ge(sv, V_WQ(h - 2))
                    gpsimd.dma_start(out=wbuf[h % 2][:],
                                     in_=w2T_t[h * 128:(h + 1) * 128, :]
                                     ).then_inc(gw[h % 2], 16)
            for t in range(NT):
                for q in range(NQ):
                    j = t * NQ + q
                    i = j % 2
                    if j >= 2:
                        gpsimd.wait_ge(sv, V_HQ(*divmod(j - 2, NQ)))
                    gpsimd.dma_start(
                        out=hbuf[i][:].rearrange("p (c t) -> p c t", c=DC),
                        in_=h1_t[q * (H // NQ):(q + 1) * (H // NQ),
                                 t * TT:(t + 1) * TT].rearrange("(c p) t -> p c t", p=128),
                    ).then_inc(lh[i], 16)

        @blk.vector
        def _(vector):
            vf = FE(vector, vf_s)
            vector.memset(oner[:], 1.0)
            vector.memset(wmx[:], 0.0).then_inc(sv, 1)
            vector.wait_ge(sv, V_ONES)
            vector.wait_ge(lfix, 16 * (4 if prequant else 3))
            vf(vector.tensor_reduce(out=mxm[:], in_=mallb[:], axis=AX.X, op=ALU.max,
                                    apply_absolute_value=True))
            vf(vector.tensor_scalar(out=sxb[:], in0=mxm[:], scalar1=MINR,
                                    scalar2=1.0 / QP, op0=ALU.max, op1=ALU.mult))
            recip_newton_f(vector, vf, rxb[:], sxb[:], sc1[:])
            if not prequant:
                for h in range(NW):
                    vector.wait_ge(gw[h % 2], 16 * (h // 2 + 1))
                    vector.tensor_reduce(out=red8[:, h:h + 1], in_=wbuf[h % 2][:],
                                         axis=AX.X, op=ALU.max,
                                         apply_absolute_value=True).then_inc(sv, 1)
                vector.wait_ge(sv, V_WCH(NW - 1))
                vector.tensor_reduce(out=wmx[:], in_=red8[:], axis=AX.X, op=ALU.max,
                                     apply_absolute_value=True).then_inc(sv, 1)
                vector.wait_ge(sv, V_WMX)
                vector.wait_ge(sp, P_TR)
                vf(vector.tensor_reduce(out=wsc[:], in_=pmm[0][0:1, 0:128],
                                        axis=AX.X, op=ALU.max,
                                        apply_absolute_value=True))
                vf(vector.tensor_scalar(out=sw_s[:], in0=wsc[:], scalar1=MINR,
                                        scalar2=1.0 / QP, op0=ALU.max, op1=ALU.mult))
            else:
                for _ in range(NW + 1):
                    vector.nop().then_inc(sv, 1)
                vector.wait_ge(sv, V_WMX)
            recip_newton_f(vector, vf, rw1[:], sw_s[:], r1a[:])
            vector.tensor_copy(out=srow[0:1, 0:1], in_=sw_s[:])
            vector.tensor_copy(out=srow[0:1, 1:2], in_=rw1[:]).then_inc(sv, 1)
            vector.wait_ge(sp, P_BC)
            vector.tensor_copy(out=swb[:], in_=pmm[0][:, 0:1])
            vf(vector.tensor_copy(out=rwb[:], in_=pmm[0][:, 1:2]))
            vf(vector.tensor_tensor(out=s5b[:], in0=swb[:], in1=sxb[:], op=ALU.mult))
            vf(vector.tensor_scalar(out=s5b[:], in0=s5b[:], scalar1=0.5,
                                    scalar2=None, op0=ALU.mult))
            vector.tensor_scalar(out=b2s[:], in0=b2c[:], scalar1=s5b[:, 0:1],
                                 scalar2=None, op0=ALU.mult).then_inc(sv, 1)
            for h in range(NW):
                if prequant:
                    vector.wait_ge(gw[h % 2], 16 * (h // 2 + 1))
                    vector.nop().then_inc(sv, 1)
                    continue
                vector.wait_ge(gw[h % 2], 16 * (NW // 2 + h // 2 + 1))
                vf(vector.tensor_scalar(out=wf[:], in0=wbuf[h % 2][:],
                                        scalar1=rwb[:, 0:1], scalar2=MAGIC,
                                        op0=ALU.mult, op1=ALU.add))
                vf(vector.tensor_scalar(out=wint[:], in0=wf[:], scalar1=MAGIC,
                                        scalar2=None, op0=ALU.subtract))
                vector.tensor_copy(out=wq2r[h][:], in_=wint[:]).then_inc(sv, 1)
                vector.wait_ge(sv, V_WQ(h))
            for t in range(NT):
                for q in range(NQ):
                    j = t * NQ + q
                    i = j % 2
                    vector.wait_ge(lh[i], 16 * (j // 2 + 1))
                    if j >= 2:
                        vector.wait_ge(sp, P_MM(*divmod(j - 2, NQ)))
                    for c in range(DC):
                        cs = slice(c * TT, (c + 1) * TT)
                        vf(vector.tensor_scalar(out=qf[:], in0=hbuf[i][:, cs],
                                                scalar1=rxb[:, 0:1], scalar2=MAGIC,
                                                op0=ALU.mult, op1=ALU.add))
                        vf(vector.tensor_scalar(out=xint[:], in0=qf[:], scalar1=MAGIC,
                                                scalar2=None, op0=ALU.subtract))
                        ins = vector.tensor_copy(out=hq[i][:, cs], in_=xint[:])
                        if c < DC - 1:
                            vf(ins)
                    ins.then_inc(sv, 1)
                    vector.wait_ge(sv, V_HQ(t, q))
                vector.wait_ge(lu[t % 2], 16 * (t // 2 + 1))
                vector.wait_ge(sa, DC * (t + 1))
                for d in range(DC):
                    ds = slice(d * TT, (d + 1) * TT)
                    ins = vector.tensor_add(out=xo[t % 2][:, ds],
                                            in0=xo[t % 2][:, ds],
                                            in1=ubuf[t % 2][:, ds])
                ins.then_inc(sv, 1)
                vector.wait_ge(sv, V_XO(t))

        @blk.tensor
        def _(tensor):
            tensor.wait_ge(lfix, 16 * (4 if prequant else 3))
            if not prequant:
                tensor.wait_ge(sv, V_WMX)
                tensor.transpose(out=pmm[0][0:1, 0:128], in_=wmx[:],
                                 identity=eye[:]).then_inc(sp, 1)
            else:
                tensor.nop().then_inc(sp, 1)
            tensor.wait_ge(sv, V_SROW)
            tensor.matmul(out=pmm[0][:, 0:2], lhsT=oner[:], rhs=srow[:],
                          start=True, stop=True).then_inc(sp, 1)
            tensor.wait_ge(sv, V_WQ(NW - 1))
            for t in range(NT):
                for q in range(NQ):
                    i = (t * NQ + q) % 2
                    tensor.wait_ge(sv, V_HQ(t, q))
                    if q == 0 and t >= 1:
                        tensor.wait_ge(sa, DC * t)
                    for hh in range(DC):
                        h = q * DC + hh
                        for d in range(DC):
                            ins = tensor.matmul(
                                out=pmm[d][:],
                                lhsT=wq2r[h][:, d * 128:(d + 1) * 128],
                                rhs=hq[i][:, hh * TT:(hh + 1) * TT],
                                start=(h == 0), stop=(h == NW - 1))
                    ins.then_inc(sp, 1)

        @blk.scalar
        def _(scalar):
            if not prequant:
                for h in range(NW):
                    scalar.wait_ge(sv, V_WQ(h))
                    scalar.dma_start(out=wq2_t[h * 128:(h + 1) * 128, :],
                                     in_=wq2r[h][:]).then_inc(swq[h % 2], 16)
            scalar.wait_ge(sv, V_SCAL)
            for t in range(NT):
                scalar.wait_ge(sp, P_T(t))
                if t >= 2:
                    scalar.wait_ge(sx_[t % 2], 16 * (t // 2))
                for d in range(DC):
                    ds = slice(d * TT, (d + 1) * TT)
                    scalar.activation(out=xo[t % 2][:, ds], in_=pmm[d][:],
                                      func=AF.Identity, bias=b2s[:, d:d + 1],
                                      scale=s5b[:, 0:1]).then_inc(sa, 1)
                scalar.wait_ge(sv, V_XO(t))
                scalar.dma_start(
                    out=xo_t[:, t * TT:(t + 1) * TT].rearrange("(c p) t -> p c t", p=128),
                    in_=xo[t % 2][:].rearrange("p (c t) -> p c t", c=DC),
                ).then_inc(sx_[t % 2], 16)

    return nc






def build_L3b():
    nc = bass.Bass("TRN2", target_bir_lowering=False)
    x_t = nc.dram_tensor("x", [D, T], F32, kind="ExternalInput").ap()
    waT_t = nc.dram_tensor("waT", [D, D], F32, kind="ExternalInput").ap()
    wbT_t = nc.dram_tensor("wbT", [D, D], F32, kind="ExternalInput").ap()
    bac_t = nc.dram_tensor("bac", [128, DC], F32, kind="ExternalInput").ap()
    bbc_t = nc.dram_tensor("bbc", [128, DC], F32, kind="ExternalInput").ap()
    stats_t = nc.dram_tensor("stats", [2, T], F32, kind="ExternalOutput").ap()
    hs0_t = nc.dram_tensor("hs0", [D, T], F32, kind="ExternalOutput").ap()
    P_t = nc.dram_tensor("P", [D, T], F32, kind="ExternalOutput").ap()
    hend_t = nc.dram_tensor("hend", [128, DC], F32, kind="ExternalOutput").ap()

    with ExitStack() as st:
        S = st.enter_context
        xbuf = S(nc.sbuf_tensor("xbuf", [128, T], F32))
        sq = [S(nc.sbuf_tensor(f"sq{i}", [128, TT], F32)) for i in range(2)]
        xh = [S(nc.sbuf_tensor(f"xh{d}", [128, T], BF16)) for d in range(DC)]
        wab = [S(nc.sbuf_tensor(f"wab{d}", [128, D], BF16)) for d in range(DC)]
        wbb = [S(nc.sbuf_tensor(f"wbb{d}", [128, D], BF16)) for d in range(DC)]
        wld = [S(nc.sbuf_tensor(f"wld{i}", [128, D], F32)) for i in range(2)]
        alpha = S(nc.sbuf_tensor("alpha", [128, T], F32))
        bu = S(nc.sbuf_tensor("bu", [128, T], F32))
        hs0 = S(nc.sbuf_tensor("hs0_sb", [128, T], F32))
        Pb = S(nc.sbuf_tensor("P_sb", [128, T], F32))
        bac = S(nc.sbuf_tensor("bac_sb", [128, DC], F32))
        bbc = S(nc.sbuf_tensor("bbc_sb", [128, DC], F32))
        hend = S(nc.sbuf_tensor("hend_sb", [128, DC], F32))
        onec = S(nc.sbuf_tensor("onec", [128, 1], F32))
        oner = S(nc.sbuf_tensor("oner", [1, 128], F32))
        t0 = S(nc.sbuf_tensor("t0", [1, TT], F32))
        t1 = S(nc.sbuf_tensor("t1", [1, TT], F32))
        vr = S(nc.sbuf_tensor("vr", [1, TT], F32))
        lnt = S(nc.sbuf_tensor("lnt", [1, TT], F32))
        mr = S(nc.sbuf_tensor("mr", [1, TT], F32))
        rr = S(nc.sbuf_tensor("rr", [1, TT], F32))
        tmpq = S(nc.sbuf_tensor("tmpq", [128, TT], F32))
        psS = [S(nc.psum_tensor(f"psS{i}", [128, TT], F32)) for i in range(2)]
        psM = [S(nc.psum_tensor(f"psM{i}", [128, TT], F32)) for i in range(2)]
        psA = [S(nc.psum_tensor(f"psA{i}", [128, TT], F32)) for i in range(2)]
        psB = [S(nc.psum_tensor(f"psB{i}", [128, TT], F32)) for i in range(2)]
        lfix = S(nc.semaphore("lfix"))
        lx = S(nc.semaphore("lx"))
        gw = [S(nc.semaphore(f"gw{i}")) for i in range(2)]
        sh = [S(nc.semaphore(f"sh{i}")) for i in range(2)]
        srl = S(nc.semaphore("srl"))
        she = S(nc.semaphore("she"))
        sv = S(nc.semaphore("sv"))
        sp = S(nc.semaphore("sp"))
        sa = S(nc.semaphore("sa"))
        vf_s = S(nc.semaphore("vf"))
        af_s = S(nc.semaphore("af"))
        blk = S(nc.Block())

        V_INIT = 1
        V_WB = lambda k: 2 + k
        V_ROW = lambda t: 18 + 3 * t
        V_VAR = lambda t: 19 + 3 * t
        V_XH = lambda t: 20 + 3 * t
        V_PHA = V_XH(NT - 1)
        V_Z = V_PHA + 1
        V_SC = lambda c: V_Z + 1 + 3 * c + 0
        V_PS = lambda c: V_Z + 2 + 3 * c
        V_HE = lambda c: V_Z + 3 + 3 * c
        V_FIN = V_HE(DC - 1)
        A_SQ = lambda t, d: 9 * t + d + 1
        A_RST = lambda t: 9 * t + 9
        A_EV = lambda c, tt: 9 * NT + 2 * (c * NT + tt) + 2
        P_S1 = lambda t: 10 * t + 1
        P_SQ = lambda t, d: 10 * t + 2 + d
        P_S2 = lambda t: 10 * t + 9
        P_BC = lambda t: 10 * t + 10
        P_AB = lambda c, tt: 10 * NT + c * NT + tt + 1

        @blk.sync
        def _(sync):
            sync.dma_start(out=bac[:], in_=bac_t).then_inc(lfix, 16)
            sync.dma_start(out=bbc[:], in_=bbc_t).then_inc(lfix, 16)
            for t in range(NT):
                if t >= 1:
                    sync.wait_ge(sv, V_XH(t - 1))
                sync.dma_start(
                    out=xbuf[:].rearrange("p (c t) -> p c t", c=DC),
                    in_=x_t[:, t * TT:(t + 1) * TT].rearrange("(c p) t -> p c t", p=128),
                ).then_inc(lx, 16)
            sync.wait_ge(sv, V_FIN)
            sync.dma_start(out=hend_t, in_=hend[:]).then_inc(she, 16)
            sync.wait_ge(she, 16)
            sync.wait_ge(srl, 16 * 2 * NT)
            sync.wait_ge(sh[0], 16 * DC)
            sync.wait_ge(sh[1], 16 * DC)

        @blk.gpsimd
        def _(gpsimd):
            for k in range(2 * DC):
                if k >= 2:
                    gpsimd.wait_ge(sv, V_WB(k - 2))
                src_t = waT_t if k < DC else wbT_t
                d = k % DC
                gpsimd.dma_start(out=wld[k % 2][:],
                                 in_=src_t[d * 128:(d + 1) * 128, :]
                                 ).then_inc(gw[k % 2], 16)

        @blk.vector
        def _(vector):
            vf = FE(vector, vf_s)
            vector.memset(onec[:], 1.0)
            vector.memset(oner[:], 1.0).then_inc(sv, 1)
            vector.wait_ge(sv, V_INIT)
            for k in range(2 * DC):
                vector.wait_ge(gw[k % 2], 16 * (k // 2 + 1))
                dst = wab[k] if k < DC else wbb[k - DC]
                vector.tensor_copy(out=dst[:], in_=wld[k % 2][:]).then_inc(sv, 1)
            for t in range(NT):
                ts_ = slice(t * TT, (t + 1) * TT)
                vector.wait_ge(sp, P_S1(t))
                if t >= 1:
                    vector.wait_ge(srl, 32 * t)
                vf(vector.tensor_scalar(out=mr[:], in0=psS[t % 2][0:1, :],
                                        scalar1=INV_D, scalar2=None, op0=ALU.mult))
                vector.wait_ge(sp, P_S2(t))
                vector.tensor_scalar(out=t0[:], in0=psM[t % 2][0:1, :],
                                     scalar1=INV_D, scalar2=None, op0=ALU.mult
                                     ).then_inc(sv, 1)
                vf(vector.tensor_tensor(out=t1[:], in0=mr[:], in1=mr[:],
                                        op=ALU.mult))
                vector.scalar_tensor_tensor(out=vr[:], in0=t0[:], scalar=EPS,
                                            in1=t1[:], op0=ALU.add,
                                            op1=ALU.subtract).then_inc(sv, 1)
                vector.wait_ge(sv, V_VAR(t))
                vector.wait_ge(sp, P_BC(t))
                for d in range(DC):
                    ds = slice(d * TT, (d + 1) * TT)
                    vf(vector.tensor_sub(out=tmpq[:], in0=xbuf[:, ds],
                                         in1=psA[t % 2][:]))
                    ins = vector.tensor_tensor(out=xh[d][:, ts_], in0=tmpq[:],
                                               in1=psB[t % 2][:], op=ALU.mult)
                    if d < DC - 1:
                        vf(ins)
                ins.then_inc(sv, 1)
                vector.wait_ge(sv, V_XH(t))
            vector.memset(xbuf[:], 0.0).then_inc(sv, 1)
            vector.wait_ge(sv, V_Z)
            for c in range(DC):
                vector.wait_ge(sa, A_EV(c, NT - 1))
                if c >= 1:
                    vector.wait_ge(sh[0], 16 * c)
                    vector.wait_ge(sh[1], 16 * c)
                vector.tensor_tensor_scan(out=hs0[:], data0=alpha[:], data1=bu[:],
                                          initial=0.0, op0=ALU.mult, op1=ALU.add
                                          ).then_inc(sv, 1)
                vector.tensor_tensor_scan(out=Pb[:], data0=alpha[:], data1=xbuf[:],
                                          initial=1.0, op0=ALU.mult, op1=ALU.add
                                          ).then_inc(sv, 1)
                vector.wait_ge(sv, V_SC(c))
                vector.tensor_copy(out=hend[:, c:c + 1], in_=hs0[:, T - 1:T]
                                   ).then_inc(sv, 1)
                vector.wait_ge(sv, V_HE(c))

        @blk.tensor
        def _(tensor):
            tensor.wait_ge(sv, V_INIT)
            for t in range(NT):
                ts_ = slice(t * TT, (t + 1) * TT)
                tensor.wait_ge(lx, 16 * (t + 1))
                if t >= 2:
                    tensor.wait_ge(sv, V_XH(t - 2))
                for d in range(DC):
                    ds = slice(d * TT, (d + 1) * TT)
                    ins = tensor.matmul(out=psS[t % 2][0:1, :], lhsT=onec[:],
                                        rhs=xbuf[:, ds], start=(d == 0),
                                        stop=(d == DC - 1))
                ins.then_inc(sp, 1)
                for d in range(DC):
                    tensor.wait_ge(sa, A_SQ(t, d))
                    tensor.matmul(out=psM[t % 2][0:1, :], lhsT=onec[:],
                                  rhs=sq[d % 2][:], start=(d == 0),
                                  stop=(d == DC - 1)).then_inc(sp, 1)
                tensor.wait_ge(sa, A_RST(t))
                tensor.matmul(out=psA[t % 2][:], lhsT=oner[:], rhs=mr[:],
                              start=True, stop=True)
                tensor.matmul(out=psB[t % 2][:], lhsT=oner[:], rhs=rr[:],
                              start=True, stop=True).then_inc(sp, 1)
            tensor.wait_ge(sv, V_WB(2 * DC - 1))
            tensor.wait_ge(sv, V_PHA)
            for c in range(DC):
                ns = slice(c * 128, (c + 1) * 128)
                for tt in range(NT):
                    ts_ = slice(tt * TT, (tt + 1) * TT)
                    j = c * NT + tt
                    if j >= 2:
                        tensor.wait_ge(sa, A_EV(*divmod(j - 2, NT)))
                    for d in range(DC):
                        tensor.matmul(out=psA[j % 2][:], lhsT=wab[d][:, ns],
                                      rhs=xh[d][:, ts_], start=(d == 0),
                                      stop=(d == DC - 1))
                    for d in range(DC):
                        ins = tensor.matmul(out=psB[j % 2][:], lhsT=wbb[d][:, ns],
                                            rhs=xh[d][:, ts_], start=(d == 0),
                                            stop=(d == DC - 1))
                    ins.then_inc(sp, 1)

        @blk.scalar
        def _(scalar):
            af = FE(scalar, af_s)
            scalar.wait_ge(lfix, 32)
            for t in range(NT):
                ts_ = slice(t * TT, (t + 1) * TT)
                scalar.wait_ge(lx, 16 * (t + 1))
                if t >= 1:
                    scalar.wait_ge(srl, 32 * t)
                for d in range(DC):
                    ds = slice(d * TT, (d + 1) * TT)
                    g = t * DC + d
                    if g >= 2:
                        scalar.wait_ge(sp, P_SQ((g - 2) // DC, (g - 2) % DC))
                    scalar.activation(out=sq[d % 2][:], in_=xbuf[:, ds],
                                      func=AF.Square).then_inc(sa, 1)
                scalar.wait_ge(sv, V_VAR(t))
                af(scalar.activation(out=lnt[:], in_=vr[:], func=AF.Ln))
                scalar.activation(out=rr[:], in_=lnt[:], func=AF.Exp,
                                  scale=-0.5).then_inc(sa, 1)
                scalar.wait_ge(sa, A_RST(t))
                scalar.dma_start(out=stats_t[0:1, ts_], in_=mr[:]).then_inc(srl, 16)
                scalar.dma_start(out=stats_t[1:2, ts_], in_=rr[:]).then_inc(srl, 16)
            for c in range(DC):
                if c >= 1:
                    scalar.wait_ge(sv, V_SC(c - 1))
                    scalar.dma_start(out=hs0_t[(c - 1) * 128:c * 128, :],
                                     in_=hs0[:]).then_inc(sh[0], 16)
                    scalar.wait_ge(sv, V_PS(c - 1))
                    scalar.dma_start(out=P_t[(c - 1) * 128:c * 128, :],
                                     in_=Pb[:]).then_inc(sh[1], 16)
                for tt in range(NT):
                    ts_ = slice(tt * TT, (tt + 1) * TT)
                    j = c * NT + tt
                    scalar.wait_ge(sp, P_AB(c, tt))
                    if tt == 0 and c >= 1:
                        scalar.wait_ge(sv, V_PS(c - 1))
                    scalar.activation(out=alpha[:, ts_], in_=psA[j % 2][:],
                                      func=AF.Sigmoid, bias=bac[:, c:c + 1]
                                      ).then_inc(sa, 1)
                    scalar.activation(out=bu[:, ts_], in_=psB[j % 2][:],
                                      func=AF.Identity, bias=bbc[:, c:c + 1]
                                      ).then_inc(sa, 1)
            scalar.wait_ge(sv, V_SC(DC - 1))
            scalar.dma_start(out=hs0_t[(DC - 1) * 128:DC * 128, :], in_=hs0[:]
                             ).then_inc(sh[0], 16)
            scalar.wait_ge(sv, V_PS(DC - 1))
            scalar.dma_start(out=P_t[(DC - 1) * 128:DC * 128, :], in_=Pb[:]
                             ).then_inc(sh[1], 16)

    return nc





def build_L4a():
    nc = bass.Bass("TRN2", target_bir_lowering=False)
    x_t = nc.dram_tensor("x", [D, T], F32, kind="ExternalInput").ap()
    hs0_t = nc.dram_tensor("hs0", [D, T], F32, kind="ExternalInput").ap()
    P_t = nc.dram_tensor("P", [D, T], F32, kind="ExternalInput").ap()
    hin_t = nc.dram_tensor("hin", [128, DC], F32, kind="ExternalInput").ap()
    stats_t = nc.dram_tensor("stats", [2, T], F32, kind="ExternalInput").ap()
    cT_t = nc.dram_tensor("cT", [D, D], F32, kind="ExternalInput").ap()
    Dc_t = nc.dram_tensor("Dc", [128, DC], F32, kind="ExternalInput").ap()
    x3_t = nc.dram_tensor("x3", [D, T], F32, kind="ExternalOutput").ap()

    TT2 = 256
    NT2 = T // TT2
    with ExitStack() as st:
        S = st.enter_context
        mrow_t = S(nc.sbuf_tensor("mrow_t", [1, T], F32))
        rrow_t = S(nc.sbuf_tensor("rrow_t", [1, T], F32))
        hsld = S(nc.sbuf_tensor("hsld", [128, T], F32))
        pld = S(nc.sbuf_tensor("pld", [128, T], F32))
        hsb = [S(nc.sbuf_tensor(f"hsb{c}", [128, T], BF16)) for c in range(DC)]
        ctb = [S(nc.sbuf_tensor(f"ctb{c}", [128, D], BF16)) for c in range(DC)]
        cld = [S(nc.sbuf_tensor(f"cld{i}", [128, D], F32)) for i in range(2)]
        x2b = [S(nc.sbuf_tensor(f"x2b{i}", [128, DC * TT2], F32)) for i in range(2)]
        ystg = [S(nc.sbuf_tensor(f"ystg{i}", [128, DC * TT2], F32)) for i in range(2)]
        xhst = S(nc.sbuf_tensor("xhst", [128, DC * TT2], F32))
        hin = S(nc.sbuf_tensor("hin_sb", [128, DC], F32))
        Dc = S(nc.sbuf_tensor("Dc_sb", [128, DC], F32))
        oner = S(nc.sbuf_tensor("oner", [1, 128], F32))
        pbank = [S(nc.psum_tensor(f"pbank{i}", [128, 2 * TT2], F32)) for i in range(4)]
        pmm = [pbank[i // 2][:, (i % 2) * TT2:(i % 2 + 1) * TT2] for i in range(DC)]
        psmr = [S(nc.psum_tensor(f"psmr{i}", [128, 2 * TT2], F32)) for i in range(2)]
        psM = [psmr[i][:, 0:TT2] for i in range(2)]
        psR = [psmr[i][:, TT2:2 * TT2] for i in range(2)]
        lfix = S(nc.semaphore("lfix"))
        lx = [S(nc.semaphore(f"lx{i}")) for i in range(2)]
        gc = [S(nc.semaphore(f"gc{i}")) for i in range(2)]
        glh = S(nc.semaphore("glh"))
        glp = S(nc.semaphore("glp"))
        sx_ = [S(nc.semaphore(f"sx{i}")) for i in range(2)]
        sv = S(nc.semaphore("sv"))
        sp = S(nc.semaphore("sp"))
        sa = S(nc.semaphore("sa"))
        vf_s = S(nc.semaphore("vf"))
        blk = S(nc.Block())

        V_INIT = 1
        V_CB = lambda k: 2 + k
        V_HSB = lambda c: 10 + c
        V_X3 = lambda tt: 18 + tt
        V_FIN = V_X3(NT2 - 1)
        P_Y = lambda tt: tt + 1
        A_EV = lambda tt: 8 * (tt + 1)

        @blk.sync
        def _(sync):
            sync.dma_start(out=mrow_t[:], in_=stats_t[0:1, :]).then_inc(lfix, 16)
            sync.dma_start(out=rrow_t[:], in_=stats_t[1:2, :]).then_inc(lfix, 16)
            sync.dma_start(out=hin[:], in_=hin_t).then_inc(lfix, 16)
            sync.dma_start(out=Dc[:], in_=Dc_t).then_inc(lfix, 16)
            for tt in range(NT2):
                if tt >= 2:
                    sync.wait_ge(sv, V_X3(tt - 2))
                    sync.wait_ge(sx_[tt % 2], 16 * (tt // 2))
                sync.dma_start(
                    out=x2b[tt % 2][:].rearrange("p (c t) -> p c t", c=DC),
                    in_=x_t[:, tt * TT2:(tt + 1) * TT2].rearrange("(c p) t -> p c t", p=128),
                ).then_inc(lx[tt % 2], 16)
            sync.wait_ge(sv, V_FIN)
            sync.wait_ge(sx_[0], 16 * (NT2 // 2))
            sync.wait_ge(sx_[1], 16 * (NT2 // 2))

        @blk.gpsimd
        def _(gpsimd):
            for k in range(DC):
                if k >= 2:
                    gpsimd.wait_ge(sv, V_CB(k - 2))
                gpsimd.dma_start(out=cld[k % 2][:],
                                 in_=cT_t[k * 128:(k + 1) * 128, :]
                                 ).then_inc(gc[k % 2], 16)
            for c in range(DC):
                if c >= 1:
                    gpsimd.wait_ge(sv, V_HSB(c - 1))
                gpsimd.dma_start(out=hsld[:], in_=hs0_t[c * 128:(c + 1) * 128, :]
                                 ).then_inc(glh, 16)
                gpsimd.dma_start(out=pld[:], in_=P_t[c * 128:(c + 1) * 128, :]
                                 ).then_inc(glp, 16)

        @blk.vector
        def _(vector):
            vf = FE(vector, vf_s)
            vector.memset(oner[:], 1.0).then_inc(sv, 1)
            vector.wait_ge(sv, V_INIT)
            vector.wait_ge(lfix, 64)
            for k in range(DC):
                vector.wait_ge(gc[k % 2], 16 * (k // 2 + 1))
                vector.tensor_copy(out=ctb[k][:], in_=cld[k % 2][:]).then_inc(sv, 1)
            for c in range(DC):
                vector.wait_ge(glh, 16 * (c + 1))
                vector.wait_ge(glp, 16 * (c + 1))
                vector.scalar_tensor_tensor(out=hsb[c][:], in0=pld[:],
                                            scalar=hin[:, c:c + 1], in1=hsld[:],
                                            op0=ALU.mult, op1=ALU.add
                                            ).then_inc(sv, 1)
                vector.wait_ge(sv, V_HSB(c))
            for tt in range(NT2):
                ds_all = slice(0, DC * TT2)
                vector.wait_ge(lx[tt % 2], 16 * (tt // 2 + 1))
                vector.wait_ge(sp, P_Y(tt))
                vector.wait_ge(sa, A_EV(tt))
                if tt >= 2:
                    vector.wait_ge(sx_[tt % 2], 16 * (tt // 2))
                for d in range(DC):
                    ds = slice(d * TT2, (d + 1) * TT2)
                    ins = vector.tensor_sub(out=xhst[:, ds], in0=x2b[tt % 2][:, ds],
                                            in1=psM[tt % 2])
                vf(ins)
                for d in range(DC):
                    ds = slice(d * TT2, (d + 1) * TT2)
                    ins = vector.tensor_tensor(out=xhst[:, ds], in0=xhst[:, ds],
                                               in1=psR[tt % 2], op=ALU.mult)
                vf(ins)
                for d in range(DC):
                    ds = slice(d * TT2, (d + 1) * TT2)
                    ins = vector.scalar_tensor_tensor(out=ystg[tt % 2][:, ds],
                                                      in0=xhst[:, ds],
                                                      scalar=Dc[:, d:d + 1],
                                                      in1=ystg[tt % 2][:, ds],
                                                      op0=ALU.mult, op1=ALU.add)
                vf(ins)
                for d in range(DC):
                    ds = slice(d * TT2, (d + 1) * TT2)
                    ins = vector.tensor_add(out=x2b[tt % 2][:, ds],
                                            in0=ystg[tt % 2][:, ds],
                                            in1=x2b[tt % 2][:, ds])
                ins.then_inc(sv, 1)
                vector.wait_ge(sv, V_X3(tt))

        @blk.tensor
        def _(tensor):
            tensor.wait_ge(lfix, 64)
            tensor.wait_ge(sv, V_HSB(DC - 1))
            for tt in range(NT2):
                ts_ = slice(tt * TT2, (tt + 1) * TT2)
                if tt >= 1:
                    tensor.wait_ge(sa, A_EV(tt - 1))
                if tt >= 2:
                    tensor.wait_ge(sv, V_X3(tt - 2))
                for d in range(DC):
                    for c in range(DC):
                        tensor.matmul(out=pmm[d],
                                      lhsT=ctb[c][:, d * 128:(d + 1) * 128],
                                      rhs=hsb[c][:, ts_], start=(c == 0),
                                      stop=(c == DC - 1))
                tensor.matmul(out=psM[tt % 2], lhsT=oner[:], rhs=mrow_t[0:1, ts_],
                              start=True, stop=True)
                tensor.matmul(out=psR[tt % 2], lhsT=oner[:], rhs=rrow_t[0:1, ts_],
                              start=True, stop=True).then_inc(sp, 1)

        @blk.scalar
        def _(scalar):
            for tt in range(NT2):
                scalar.wait_ge(sp, P_Y(tt))
                if tt >= 2:
                    scalar.wait_ge(sv, V_X3(tt - 2))
                for d in range(DC):
                    ds = slice(d * TT2, (d + 1) * TT2)
                    scalar.activation(out=ystg[tt % 2][:, ds], in_=pmm[d],
                                      func=AF.Identity).then_inc(sa, 1)
                scalar.wait_ge(sv, V_X3(tt))
                scalar.dma_start(
                    out=x3_t[:, tt * TT2:(tt + 1) * TT2].rearrange("(c p) t -> p c t", p=128),
                    in_=x2b[tt % 2][:].rearrange("p (c t) -> p c t", c=DC),
                ).then_inc(sx_[tt % 2], 16)

    return nc



def _asc(a):
    return np.ascontiguousarray(a)


def _shard_u(u):
    out = []
    for c in range(NCORE):
        b, h = c // 2, c % 2
        out.append(_asc(u[b, h * T:(h + 1) * T, :].T))
    return out


_EYE = np.eye(128, dtype=np.float32)


def _bcol(v):
    return np.ascontiguousarray(np.asarray(v).reshape(-1, 128).T)


def _gather_mall(results, key="mloc"):
    m = np.concatenate([np.asarray(r[key]).reshape(1, 1) for r in results], axis=1)
    return np.ascontiguousarray(np.broadcast_to(m, (128, NCORE)))


def kernel(**inputs):
    u = np.asarray(inputs["L0_attn_u"])
    w1T = _asc(np.asarray(inputs["L0_ffn_w1"]).T)
    w2T = _asc(np.asarray(inputs["L0_ffn_w2"]).T)
    waT = _asc(np.asarray(inputs["L0_attn_W_alpha_w"]).T)
    wbT = _asc(np.asarray(inputs["L0_attn_W_B_w"]).T)
    cT = _asc(np.asarray(inputs["L0_attn_C_w"]).T)
    b1c = _bcol(inputs["L0_ffn_b1"])
    b2c = _bcol(inputs["L0_ffn_b2"])
    bac = _bcol(inputs["L0_attn_W_alpha_b"])
    bbc = _bcol(inputs["L0_attn_W_B_b"])
    Dcc = _bcol(inputs["L0_attn_D"])
    shards = _shard_u(u)
    cores = list(range(NCORE))

    nc_ln = build_L1()
    nc_ffn1 = build_L2()
    nc_ffn1p = build_L2(prequant=True)
    nc_ffn2 = build_L3a()
    nc_ffn2p = build_L3a(prequant=True)
    nc_ssm1 = build_L3b()
    nc_ssm2 = build_L4a()

    r1 = run_bass_kernel_spmd(
        nc_ln, [{"u": shards[c], "eye": _EYE} for c in cores], core_ids=cores)
    m1 = _gather_mall(r1.results)

    r2 = run_bass_kernel_spmd(
        nc_ffn1, [{"u": shards[c], "stats": r1.results[c]["stats"], "mall": m1,
                   "w1T": w1T, "b1c": b1c, "eye": _EYE} for c in cores],
        core_ids=cores)
    m2 = _gather_mall(r2.results)

    r3a = run_bass_kernel_spmd(
        nc_ffn2, [{"x": shards[c], "h1": r2.results[c]["h1"], "mall": m2,
                   "w2T": w2T, "b2c": b2c, "eye": _EYE} for c in cores],
        core_ids=cores)

    r3b = run_bass_kernel_spmd(
        nc_ssm1, [{"x": r3a.results[c]["xo"], "waT": waT, "wbT": wbT,
                   "bac": bac, "bbc": bbc} for c in cores], core_ids=cores)

    zeros_hin = np.zeros((128, DC), np.float32)
    hins = [np.asarray(r3b.results[c - 1]["hend"]) if c % 2 == 1 else zeros_hin
            for c in cores]
    r4a = run_bass_kernel_spmd(
        nc_ssm2, [{"x": r3a.results[c]["xo"], "hs0": r3b.results[c]["hs0"],
                   "P": r3b.results[c]["P"], "hin": hins[c],
                   "stats": r3b.results[c]["stats"], "cT": cT, "Dc": Dcc}
                  for c in cores], core_ids=cores)

    r4b = run_bass_kernel_spmd(
        nc_ln, [{"u": r4a.results[c]["x3"], "eye": _EYE} for c in cores],
        core_ids=cores)
    m3 = _gather_mall(r4b.results)

    r5 = run_bass_kernel_spmd(
        nc_ffn1p, [{"u": r4a.results[c]["x3"], "stats": r4b.results[c]["stats"],
                    "mall": m3, "wq1i": r2.results[c]["wq1"],
                    "swi": r2.results[c]["sw"], "b1c": b1c, "eye": _EYE}
                   for c in cores], core_ids=cores)
    m4 = _gather_mall(r5.results)

    r6 = run_bass_kernel_spmd(
        nc_ffn2p, [{"x": r4a.results[c]["x3"], "h1": r5.results[c]["h1"],
                    "mall": m4, "wq2i": r3a.results[c]["wq2"],
                    "swi": r3a.results[c]["sw"], "b2c": b2c, "eye": _EYE}
                   for c in cores], core_ids=cores)

    out = np.empty((4, 8192, D), np.float32)
    for c in cores:
        b, h = c // 2, c % 2
        out[b, h * T:(h + 1) * T, :] = np.asarray(r6.results[c]["xo"]).T
    return out


# version 06
# speedup vs baseline: 1.1718x; 1.0364x over previous
import numpy as np
import ml_dtypes
from contextlib import ExitStack

import concourse.bass as bass
import concourse.mybir as mybir
from concourse.bass_utils import run_bass_kernel_spmd

F32 = mybir.dt.float32
BF16 = mybir.dt.bfloat16
I32 = mybir.dt.int32
AF = mybir.ActivationFunctionType
ALU = mybir.AluOpType
AX = mybir.AxisListType

NCORE = 8
D = 1024
T = 4096
H = 4096
DC = D // 128
HC = H // 128
TT = 512
NT = T // TT
EPS = 1e-5
QP = 127.0
MAGIC = 12582912.0
MINR = 1e-6
INV_D = 1.0 / D



def recip_newton(vec, out, in_, scr):
    vec.reciprocal(out=out, in_=in_)
    vec.tensor_tensor(out=scr, in0=in_, in1=out, op=ALU.mult)
    vec.tensor_scalar(out=scr, in0=scr, scalar1=-1.0, scalar2=2.0,
                      op0=ALU.mult, op1=ALU.add)
    vec.tensor_tensor(out=out, in0=out, in1=scr, op=ALU.mult)


def quant_tiles(vec, out_bf, in_f32, r_ap, scr_i32):
    vec.tensor_scalar(out=scr_i32, in0=in_f32, scalar1=r_ap, scalar2=None,
                      op0=ALU.mult)
    vec.tensor_copy(out=out_bf, in_=scr_i32)




class FE:

    def __init__(self, eng, sem):
        self.eng, self.sem, self.n = eng, sem, 0

    def __call__(self, ins):
        self.n += 1
        ins.then_inc(self.sem, 1)
        self.eng.wait_ge(self.sem, self.n)


def recip_newton_f(vec, fe, out, in_, scr):
    fe(vec.reciprocal(out=out, in_=in_))
    fe(vec.tensor_tensor(out=scr, in0=in_, in1=out, op=ALU.mult))
    fe(vec.tensor_scalar(out=scr, in0=scr, scalar1=-1.0, scalar2=2.0,
                         op0=ALU.mult, op1=ALU.add))
    fe(vec.tensor_tensor(out=out, in0=out, in1=scr, op=ALU.mult))



def build_L1():
    nc = bass.Bass("TRN2", target_bir_lowering=False)
    u_t = nc.dram_tensor("u", [D, T], F32, kind="ExternalInput").ap()
    eye_t = nc.dram_tensor("eye", [128, 128], F32, kind="ExternalInput").ap()
    stats_t = nc.dram_tensor("stats", [2, T], F32, kind="ExternalOutput").ap()
    mloc_t = nc.dram_tensor("mloc", [1, 1], F32, kind="ExternalOutput").ap()

    with ExitStack() as st:
        S = st.enter_context
        us = [S(nc.sbuf_tensor(f"u{d}", [128, T], F32)) for d in range(DC)]
        eye = S(nc.sbuf_tensor("eye_sb", [128, 128], F32))
        onec = S(nc.sbuf_tensor("onec", [128, 1], F32))
        oner = S(nc.sbuf_tensor("oner", [1, 128], F32))
        sq = [S(nc.sbuf_tensor(f"sq{i}", [128, TT], F32)) for i in range(2)]
        mrow = S(nc.sbuf_tensor("mrow", [1, T], F32))
        rrow = S(nc.sbuf_tensor("rrow", [1, T], F32))
        t0 = S(nc.sbuf_tensor("t0", [1, TT], F32))
        t1 = S(nc.sbuf_tensor("t1", [1, TT], F32))
        vrow = S(nc.sbuf_tensor("vrow", [1, T], F32))
        lnt = S(nc.sbuf_tensor("lnt", [1, TT], F32))
        tmp8 = S(nc.sbuf_tensor("tmp8", [128, T], F32))
        red8 = S(nc.sbuf_tensor("red8", [128, DC], F32))
        rt = S(nc.sbuf_tensor("rt", [128, 1], F32))
        mx = S(nc.sbuf_tensor("mx", [128, 1], F32))
        sc = S(nc.sbuf_tensor("sc", [1, 1], F32))
        psA = S(nc.psum_tensor("psA", [128, TT], F32))
        psB = [S(nc.psum_tensor(f"psB{i}", [128, TT], F32)) for i in range(2)]
        psM = [S(nc.psum_tensor(f"psM{i}", [128, TT], F32)) for i in range(2)]
        psR = [S(nc.psum_tensor(f"psR{i}", [128, TT], F32)) for i in range(2)]
        ld = S(nc.semaphore("ld"))
        sa = S(nc.semaphore("sa"))
        sv = S(nc.semaphore("sv"))
        sp = S(nc.semaphore("sp"))
        so = S(nc.semaphore("so"))
        vf_s = S(nc.semaphore("vf"))
        blk = S(nc.Block())

        @blk.sync
        def _(sync):
            for d in range(DC):
                sync.dma_start(out=us[d][:], in_=u_t[d * 128:(d + 1) * 128, :]
                               ).then_inc(ld, 16)
            sync.dma_start(out=eye[:], in_=eye_t).then_inc(ld, 16)
            sync.wait_ge(sv, 2 + NT + 3 * NT + NT + 1)
            sync.dma_start(out=stats_t[0:1, :], in_=mrow[:]).then_inc(so, 16)
            sync.dma_start(out=stats_t[1:2, :], in_=rrow[:]).then_inc(so, 16)
            sync.dma_start(out=mloc_t, in_=sc[:]).then_inc(so, 16)
            sync.wait_ge(so, 48)

        SV_INIT = 2
        SV_A1 = SV_INIT + NT
        SV_A2 = SV_A1 + 3 * NT
        SV_B = SV_A2 + NT
        SV_FIN = SV_B + 1

        @blk.tensor
        def _(tensor):
            tensor.wait_ge(ld, 144)
            tensor.wait_ge(sv, 2)
            for t in range(NT):
                if t >= 1:
                    tensor.wait_ge(sv, SV_INIT + t)
                ts = slice(t * TT, (t + 1) * TT)
                for d in range(DC):
                    ins = tensor.matmul(out=psA[0:1, :], lhsT=onec[:],
                                        rhs=us[d][:, ts],
                                        start=(d == 0), stop=(d == DC - 1))
                ins.then_inc(sp, 1)
            nsq = 0
            for t in range(NT):
                for d in range(DC):
                    nsq += 1
                    tensor.wait_ge(sa, nsq)
                    if d == 0 and t >= 2:
                        tensor.wait_ge(sv, SV_A1 + 3 * (t - 1))
                    tensor.matmul(out=psB[t % 2][0:1, :], lhsT=onec[:],
                                  rhs=sq[nsq % 2][:], start=(d == 0),
                                  stop=(d == DC - 1)).then_inc(sp, 1)
            for t in range(NT):
                ts = slice(t * TT, (t + 1) * TT)
                tensor.wait_ge(sa, NT * DC + 2 * (t + 1))
                if t >= 2:
                    tensor.wait_ge(sv, SV_A2 + (t - 1))
                tensor.matmul(out=psM[t % 2][:], lhsT=oner[:], rhs=mrow[0:1, ts],
                              start=True, stop=True)
                tensor.matmul(out=psR[t % 2][:], lhsT=oner[:], rhs=rrow[0:1, ts],
                              start=True, stop=True).then_inc(sp, 1)
            tensor.wait_ge(sv, SV_B)
            tensor.transpose(out=psA[0:1, 0:128], in_=mx[:], identity=eye[:]
                             ).then_inc(sp, 1)

        @blk.scalar
        def _(scalar):
            scalar.wait_ge(ld, 144)
            n = 0
            for t in range(NT):
                ts = slice(t * TT, (t + 1) * TT)
                for d in range(DC):
                    n += 1
                    if n > 2:
                        scalar.wait_ge(sp, NT + n - 2)
                    scalar.activation(out=sq[n % 2][:], in_=us[d][:, ts],
                                      func=AF.Square).then_inc(sa, 1)
            for t in range(NT):
                ts = slice(t * TT, (t + 1) * TT)
                scalar.wait_ge(sv, 2 + NT + 3 * (t + 1))
                scalar.activation(out=lnt[:], in_=vrow[0:1, ts], func=AF.Ln
                                  ).then_inc(sa, 1)
                scalar.wait_ge(sa, NT * DC + 2 * t + 1)
                scalar.activation(out=rrow[0:1, ts], in_=lnt[:], func=AF.Exp,
                                  scale=-0.5).then_inc(sa, 1)

        @blk.vector
        def _(vector):
            vf = FE(vector, vf_s)
            vector.memset(onec[:], 1.0).then_inc(sv, 1)
            vector.memset(oner[:], 1.0)
            vector.memset(mx[:], 0.0).then_inc(sv, 1)
            vector.wait_ge(sv, SV_INIT)
            for t in range(NT):
                ts = slice(t * TT, (t + 1) * TT)
                vector.wait_ge(sp, t + 1)
                vector.tensor_scalar(out=mrow[0:1, ts], in0=psA[0:1, :],
                                     scalar1=INV_D, scalar2=None,
                                     op0=ALU.mult).then_inc(sv, 1)
            vector.wait_ge(sv, SV_A1)
            for t in range(NT):
                ts = slice(t * TT, (t + 1) * TT)
                vector.wait_ge(sp, NT + DC * (t + 1))
                vector.tensor_scalar(out=t0[:], in0=psB[t % 2][0:1, :],
                                     scalar1=INV_D, scalar2=None, op0=ALU.mult
                                     ).then_inc(sv, 1)
                vector.tensor_tensor(out=t1[:], in0=mrow[0:1, ts],
                                     in1=mrow[0:1, ts], op=ALU.mult).then_inc(sv, 1)
                vector.wait_ge(sv, SV_A1 + 3 * t + 2)
                vector.scalar_tensor_tensor(out=vrow[0:1, ts], in0=t0[:],
                                            scalar=EPS, in1=t1[:], op0=ALU.add,
                                            op1=ALU.subtract).then_inc(sv, 1)
                vector.wait_ge(sv, SV_A1 + 3 * t + 3)
            for t in range(NT):
                ts = slice(t * TT, (t + 1) * TT)
                vector.wait_ge(sp, NT + DC * NT + (t + 1))
                for d in range(DC):
                    ds = slice(d * TT, (d + 1) * TT)
                    ins = vector.tensor_sub(out=tmp8[:, ds], in0=us[d][:, ts],
                                            in1=psM[t % 2][:])
                vf(ins)
                for d in range(DC):
                    ds = slice(d * TT, (d + 1) * TT)
                    ins = vector.tensor_tensor(out=tmp8[:, ds], in0=tmp8[:, ds],
                                               in1=psR[t % 2][:], op=ALU.mult)
                vf(ins)
                for d in range(DC):
                    ds = slice(d * TT, (d + 1) * TT)
                    ins = vector.tensor_reduce(out=red8[:, d:d + 1], in_=tmp8[:, ds],
                                               axis=AX.X, op=ALU.max,
                                               apply_absolute_value=True)
                vf(ins)
                vf(vector.tensor_reduce(out=rt[:], in_=red8[:], axis=AX.X,
                                        op=ALU.max, apply_absolute_value=True))
                vector.tensor_max(out=mx[:], in0=mx[:], in1=rt[:]).then_inc(sv, 1)
                vector.wait_ge(sv, SV_A2 + t + 1)
            vector.wait_ge(sp, NT + DC * NT + NT + 1)
            vector.tensor_reduce(out=sc[:], in_=psA[0:1, 0:128], axis=AX.X,
                                 op=ALU.max, apply_absolute_value=True
                                 ).then_inc(sv, 1)

    return nc





def build_L2(prequant=False):
    nc = bass.Bass("TRN2", target_bir_lowering=False)
    u_t = nc.dram_tensor("u", [D, T], F32, kind="ExternalInput").ap()
    stats_t = nc.dram_tensor("stats", [2, T], F32, kind="ExternalInput").ap()
    mall_t = nc.dram_tensor("mall", [128, NCORE], F32, kind="ExternalInput").ap()
    if prequant:
        wq1i_t = nc.dram_tensor("wq1i", [D, H], BF16, kind="ExternalInput").ap()
        swi_t = nc.dram_tensor("swi", [1, 1], F32, kind="ExternalInput").ap()
    else:
        w1T_t = nc.dram_tensor("w1T", [D, H], F32, kind="ExternalInput").ap()
    b1c_t = nc.dram_tensor("b1c", [128, HC], F32, kind="ExternalInput").ap()
    eye_t = nc.dram_tensor("eye", [128, 128], F32, kind="ExternalInput").ap()
    h1_t = nc.dram_tensor("h1", [H, T], F32, kind="ExternalOutput").ap()
    if not prequant:
        wq1_t = nc.dram_tensor("wq1", [D, H], BF16, kind="ExternalOutput").ap()
        sw_t = nc.dram_tensor("sw", [1, 1], F32, kind="ExternalOutput").ap()
    mloc_t = nc.dram_tensor("mloc", [1, 1], F32, kind="ExternalOutput").ap()
    wq_src_t = wq1i_t if prequant else wq1_t

    NG = HC * NT
    NB = NG // 4
    with ExitStack() as st:
        S = st.enter_context
        mrow_t = S(nc.sbuf_tensor("mrow_t", [1, T], F32))
        rrow_t = S(nc.sbuf_tensor("rrow_t", [1, T], F32))
        mrow = mrow_t[0:1, :]
        rrow = rrow_t[0:1, :]
        ubuf = [S(nc.sbuf_tensor(f"ub{i}", [128, T], F32)) for i in range(2)]
        wbuf = [S(nc.sbuf_tensor(f"wb{i}", [128, H], F32)) for i in range(2)]
        xq = [S(nc.sbuf_tensor(f"xq{d}", [128, T], BF16)) for d in range(DC)]
        wqs = [S(nc.sbuf_tensor(f"wqs{i}", [128, DC * 128], BF16)) for i in range(2)]
        wqo = [S(nc.sbuf_tensor(f"wqo{i}", [128, H], BF16)) for i in range(2)]
        wint = S(nc.sbuf_tensor("wint", [128, H // 4], I32))
        xint = S(nc.sbuf_tensor("xint", [128, TT], I32))
        qf = S(nc.sbuf_tensor("qf", [128, TT], F32))
        red8 = S(nc.sbuf_tensor("red8", [128, DC], F32))
        rt2 = S(nc.sbuf_tensor("rt2", [128, 1], F32))
        eye = S(nc.sbuf_tensor("eye_sb", [128, 128], F32))
        oner = S(nc.sbuf_tensor("oner", [1, 128], F32))
        mallb = S(nc.sbuf_tensor("mallb", [128, NCORE], F32))
        b1c = S(nc.sbuf_tensor("b1c_sb", [128, HC], F32))
        b1s = S(nc.sbuf_tensor("b1s", [128, HC], F32))
        tmpq = S(nc.sbuf_tensor("tmpq", [128, T], F32))
        wmx = S(nc.sbuf_tensor("wmx", [128, 1], F32))
        gmx = S(nc.sbuf_tensor("gmx", [128, 1], F32))
        mxm = S(nc.sbuf_tensor("mxm", [128, 1], F32))
        sxb = S(nc.sbuf_tensor("sxb", [128, 1], F32))
        rxb = S(nc.sbuf_tensor("rxb", [128, 1], F32))
        s1b = S(nc.sbuf_tensor("s1b", [128, 1], F32))
        swb = S(nc.sbuf_tensor("swb", [128, 1], F32))
        rwb = S(nc.sbuf_tensor("rwb", [128, 1], F32))
        sc1 = S(nc.sbuf_tensor("sc1", [128, 1], F32))
        wsc = S(nc.sbuf_tensor("wsc", [1, 1], F32))
        sw_s = S(nc.sbuf_tensor("sw_s", [1, 1], F32))
        rw1 = S(nc.sbuf_tensor("rw1", [1, 1], F32))
        r1a = S(nc.sbuf_tensor("r1a", [1, 1], F32))
        srow = S(nc.sbuf_tensor("srow", [1, 2], F32))
        mgl = S(nc.sbuf_tensor("mgl", [1, 1], F32))
        pmm = [S(nc.psum_tensor(f"pmm{i}", [128, TT], F32)) for i in range(4)]
        psE = S(nc.psum_tensor("psE", [128, TT], F32))
        psF = S(nc.psum_tensor("psF", [128, TT], F32))
        paux = S(nc.psum_tensor("paux", [128, 128], F32))
        pbc = S(nc.psum_tensor("pbc", [128, 2], F32))
        lfix = S(nc.semaphore("lfix"))
        lu = [S(nc.semaphore(f"lu{i}")) for i in range(2)]
        gw = [S(nc.semaphore(f"gw{i}")) for i in range(2)]
        gq = [S(nc.semaphore(f"gq{i}")) for i in range(2)]
        sq = [S(nc.semaphore(f"sq{i}")) for i in range(2)]
        sh = [S(nc.semaphore(f"sh{i}")) for i in range(2)]
        sv = S(nc.semaphore("sv"))
        sp = S(nc.semaphore("sp"))
        sa = S(nc.semaphore("sa"))
        so = S(nc.semaphore("so"))
        vf_s = S(nc.semaphore("vf"))
        af_s = S(nc.semaphore("af"))
        blk = S(nc.Block())

        V_ONES = 1
        V_WCH = lambda d: 2 + d
        V_WMX = 10
        V_SROW = 11
        V_SCAL = 12
        V_WQ = lambda d: 13 + d
        V_XQ = lambda t: 21 + t
        V_G = lambda g: 29 + g
        V_GDONE = 29 + NG
        V_FIN = 30 + NG
        P_TR = 1
        P_BC = 2
        P_B = lambda t: 3 + t
        P_G = lambda g: 11 + g
        P_FINT = 11 + NG

        @blk.sync
        def _(sync):
            sync.dma_start(out=mrow_t[:], in_=stats_t[0:1, :]).then_inc(lfix, 16)
            sync.dma_start(out=rrow_t[:], in_=stats_t[1:2, :]).then_inc(lfix, 16)
            sync.dma_start(out=mallb[:], in_=mall_t).then_inc(lfix, 16)
            sync.dma_start(out=b1c[:], in_=b1c_t).then_inc(lfix, 16)
            sync.dma_start(out=eye[:], in_=eye_t).then_inc(lfix, 16)
            if prequant:
                sync.dma_start(out=sw_s[:], in_=swi_t).then_inc(lfix, 16)
            for t in range(NT):
                if t >= 2:
                    sync.wait_ge(sv, V_XQ(t - 2))
                sync.dma_start(
                    out=ubuf[t % 2][:].rearrange("p (c t) -> p c t", c=DC),
                    in_=u_t[:, t * TT:(t + 1) * TT].rearrange("(c p) t -> p c t", p=128),
                ).then_inc(lu[t % 2], 16)
            sync.wait_ge(sv, V_FIN)
            sync.dma_start(out=mloc_t, in_=mgl[:]).then_inc(so, 16)
            if not prequant:
                sync.dma_start(out=sw_t, in_=sw_s[:]).then_inc(so, 16)
                sync.wait_ge(so, 32)
                sync.wait_ge(sq[0], 16 * (DC // 2))
                sync.wait_ge(sq[1], 16 * (DC // 2))
            else:
                sync.wait_ge(so, 16)
            sync.wait_ge(sh[0], 16 * (NB // 2))
            sync.wait_ge(sh[1], 16 * (NB // 2))

        @blk.gpsimd
        def _(gpsimd):
            if not prequant:
                for d in range(DC):
                    if d >= 2:
                        gpsimd.wait_ge(sv, V_WCH(d - 2))
                    gpsimd.dma_start(out=wbuf[d % 2][:],
                                     in_=w1T_t[d * 128:(d + 1) * 128, :]
                                     ).then_inc(gw[d % 2], 16)
                gpsimd.wait_ge(sv, V_WCH(DC - 1))
                for d in range(DC):
                    if d >= 2:
                        gpsimd.wait_ge(sv, V_WQ(d - 2))
                    gpsimd.dma_start(out=wbuf[d % 2][:],
                                     in_=w1T_t[d * 128:(d + 1) * 128, :]
                                     ).then_inc(gw[d % 2], 16)
                gpsimd.wait_ge(sq[0], 16 * (DC // 2))
                gpsimd.wait_ge(sq[1], 16 * (DC // 2))
            for h in range(HC):
                if h >= 2:
                    gpsimd.wait_ge(sp, P_G((h - 1) * NT - 1))
                gpsimd.dma_start(
                    out=wqs[h % 2][:].rearrange("p (c m) -> p c m", c=DC),
                    in_=wq_src_t[:, h * 128:(h + 1) * 128].rearrange("(c p) m -> p c m", p=128),
                ).then_inc(gq[h % 2], 16)

        @blk.vector
        def _(vector):
            vf = FE(vector, vf_s)
            vector.memset(oner[:], 1.0)
            vector.memset(wmx[:], 0.0)
            vector.memset(gmx[:], 0.0).then_inc(sv, 1)
            vector.wait_ge(sv, V_ONES)
            vector.wait_ge(lfix, 16 * (6 if prequant else 5))
            vf(vector.tensor_reduce(out=mxm[:], in_=mallb[:], axis=AX.X, op=ALU.max,
                                    apply_absolute_value=True))
            vf(vector.tensor_scalar(out=sxb[:], in0=mxm[:], scalar1=MINR,
                                    scalar2=1.0 / QP, op0=ALU.max, op1=ALU.mult))
            recip_newton_f(vector, vf, rxb[:], sxb[:], sc1[:])
            vf(vector.tensor_scalar(out=rrow, in0=rrow, scalar1=rxb[0:1, 0:1],
                                    scalar2=None, op0=ALU.mult))
            if not prequant:
                for d in range(DC):
                    vector.wait_ge(gw[d % 2], 16 * (d // 2 + 1))
                    vector.tensor_reduce(out=red8[:, d:d + 1], in_=wbuf[d % 2][:],
                                         axis=AX.X, op=ALU.max,
                                         apply_absolute_value=True).then_inc(sv, 1)
                vector.wait_ge(sv, V_WCH(DC - 1))
                vector.tensor_reduce(out=wmx[:], in_=red8[:], axis=AX.X, op=ALU.max,
                                     apply_absolute_value=True).then_inc(sv, 1)
                vector.wait_ge(sv, V_WMX)
                vector.wait_ge(sp, P_TR)
                vf(vector.tensor_reduce(out=wsc[:], in_=paux[0:1, 0:128], axis=AX.X,
                                        op=ALU.max, apply_absolute_value=True))
                vf(vector.tensor_scalar(out=sw_s[:], in0=wsc[:], scalar1=MINR,
                                        scalar2=1.0 / QP, op0=ALU.max, op1=ALU.mult))
            else:
                for _ in range(DC + 1):
                    vector.nop().then_inc(sv, 1)
                vector.wait_ge(sv, V_WMX)
            recip_newton_f(vector, vf, rw1[:], sw_s[:], r1a[:])
            vector.tensor_copy(out=srow[0:1, 0:1], in_=sw_s[:])
            vector.tensor_copy(out=srow[0:1, 1:2], in_=rw1[:]).then_inc(sv, 1)
            vector.wait_ge(sp, P_BC)
            vector.tensor_copy(out=swb[:], in_=pbc[:, 0:1])
            vf(vector.tensor_copy(out=rwb[:], in_=pbc[:, 1:2]))
            vf(vector.tensor_tensor(out=s1b[:], in0=swb[:], in1=sxb[:], op=ALU.mult))
            vector.tensor_scalar(out=b1s[:], in0=b1c[:], scalar1=s1b[:, 0:1],
                                 scalar2=None, op0=ALU.mult).then_inc(sv, 1)
            for d in range(DC):
                if prequant:
                    vector.nop().then_inc(sv, 1)
                    continue
                vector.wait_ge(gw[d % 2], 16 * (4 + d // 2 + 1))
                if d >= 2:
                    vector.wait_ge(sq[d % 2], 16 * (d // 2))
                for k in range(4):
                    hs = slice(k * (H // 4), (k + 1) * (H // 4))
                    wf = tmpq[:, 0:H // 4]
                    vf(vector.tensor_scalar(out=wf, in0=wbuf[d % 2][:, hs],
                                            scalar1=rwb[:, 0:1], scalar2=MAGIC,
                                            op0=ALU.mult, op1=ALU.add))
                    vf(vector.tensor_scalar(out=wint[:], in0=wf, scalar1=MAGIC,
                                            scalar2=None, op0=ALU.subtract))
                    ins = vector.tensor_copy(out=wqo[d % 2][:, hs], in_=wint[:])
                    if k < 3:
                        vf(ins)
                ins.then_inc(sv, 1)
                vector.wait_ge(sv, V_WQ(d))
            for t in range(NT):
                ts_ = slice(t * TT, (t + 1) * TT)
                vector.wait_ge(lu[t % 2], 16 * (t // 2 + 1))
                vector.wait_ge(sp, P_B(t))
                for d in range(DC):
                    ds = slice(d * TT, (d + 1) * TT)
                    ins = vector.tensor_sub(out=tmpq[:, ds], in0=ubuf[t % 2][:, ds],
                                            in1=psE[:])
                vf(ins)
                for d in range(DC):
                    ds = slice(d * TT, (d + 1) * TT)
                    vf(vector.tensor_tensor(out=qf[:], in0=tmpq[:, ds],
                                            in1=psF[:],
                                            op=ALU.mult))
                    vf(vector.tensor_scalar(out=xint[:], in0=qf[:], scalar1=MAGIC,
                                            scalar2=MAGIC, op0=ALU.add,
                                            op1=ALU.subtract))
                    ins = vector.tensor_copy(out=xq[d][:, ts_], in_=xint[:])
                    if d < DC - 1:
                        vf(ins)
                ins.then_inc(sv, 1)
                vector.wait_ge(sv, V_XQ(t))
            for g in range(NG):
                vector.wait_ge(sa, g + 1)
                stg = wbuf[(g // 4) % 2]
                col = (g % 4) * TT
                vector.tensor_reduce(out=red8[:, g % 8:g % 8 + 1],
                                     in_=stg[:, col:col + TT], axis=AX.X,
                                     op=ALU.max, apply_absolute_value=True
                                     ).then_inc(sv, 1)
                if g % 8 == 7:
                    vector.wait_ge(sv, V_G(g))
                    vf(vector.tensor_reduce(out=rt2[:], in_=red8[:], axis=AX.X,
                                            op=ALU.max, apply_absolute_value=True))
                    if g == NG - 1:
                        vector.tensor_max(out=gmx[:], in0=gmx[:], in1=rt2[:]
                                          ).then_inc(sv, 1)
                    else:
                        vf(vector.tensor_max(out=gmx[:], in0=gmx[:], in1=rt2[:]))
            vector.wait_ge(sp, P_FINT)
            vector.tensor_reduce(out=mgl[:], in_=paux[0:1, 0:128], axis=AX.X,
                                 op=ALU.max, apply_absolute_value=True).then_inc(sv, 1)

        @blk.tensor
        def _(tensor):
            tensor.wait_ge(lfix, 16 * (6 if prequant else 5))
            if not prequant:
                tensor.wait_ge(sv, V_WMX)
                tensor.transpose(out=paux[0:1, 0:128], in_=wmx[:], identity=eye[:]
                                 ).then_inc(sp, 1)
            else:
                tensor.nop().then_inc(sp, 1)
            tensor.wait_ge(sv, V_SROW)
            tensor.matmul(out=pbc[:], lhsT=oner[:], rhs=srow[:], start=True,
                          stop=True).then_inc(sp, 1)
            for t in range(NT):
                ts_ = slice(t * TT, (t + 1) * TT)
                if t >= 1:
                    tensor.wait_ge(sv, V_XQ(t - 1))
                tensor.matmul(out=psE[:], lhsT=oner[:],
                              rhs=mrow[:, ts_], start=True, stop=True)
                tensor.matmul(out=psF[:], lhsT=oner[:],
                              rhs=rrow[:, ts_], start=True, stop=True
                              ).then_inc(sp, 1)
            g = 0
            for h in range(HC):
                tensor.wait_ge(gq[h % 2], 16 * (h // 2 + 1))
                for t in range(NT):
                    ts_ = slice(t * TT, (t + 1) * TT)
                    if h == 0:
                        tensor.wait_ge(sv, V_XQ(t))
                    if g >= 4:
                        tensor.wait_ge(sa, g - 3)
                    for d in range(DC):
                        ins = tensor.matmul(out=pmm[g % 4][:],
                                            lhsT=wqs[h % 2][:, d * 128:(d + 1) * 128],
                                            rhs=xq[d][:, ts_], start=(d == 0),
                                            stop=(d == DC - 1))
                    ins.then_inc(sp, 1)
                    g += 1
            tensor.wait_ge(sv, V_GDONE)
            tensor.transpose(out=paux[0:1, 0:128], in_=gmx[:], identity=eye[:]
                             ).then_inc(sp, 1)

        @blk.scalar
        def _(scalar):
            if not prequant:
                for d in range(DC):
                    scalar.wait_ge(sv, V_WQ(d))
                    scalar.dma_start(out=wq1_t[d * 128:(d + 1) * 128, :],
                                     in_=wqo[d % 2][:]).then_inc(sq[d % 2], 16)
            scalar.wait_ge(sv, V_SCAL)
            scalar.wait_ge(sv, V_WQ(DC - 1))
            g = 0
            for h in range(HC):
                for t in range(NT):
                    b = g // 4
                    stg = wbuf[b % 2]
                    col = (g % 4) * TT
                    if g % 4 == 0 and b >= 2:
                        scalar.wait_ge(sh[b % 2], 16 * (b // 2))
                        scalar.wait_ge(sv, V_G(4 * (b - 1) - 1))
                    scalar.wait_ge(sp, P_G(g))
                    scalar.activation(out=stg[:, col:col + TT],
                                      in_=pmm[g % 4][:], func=AF.Gelu,
                                      bias=b1s[:, h:h + 1],
                                      scale=s1b[:, 0:1]).then_inc(sa, 1)
                    if g % 4 == 3:
                        scalar.wait_ge(sa, g + 1)
                        th = (t - 3) // 4
                        scalar.dma_start(
                            out=h1_t[h * 128:(h + 1) * 128,
                                     th * 2048:(th + 1) * 2048],
                            in_=stg[:, 0:2048]).then_inc(sh[b % 2], 16)
                    g += 1

    return nc





def build_L3a(prequant=False):
    nc = bass.Bass("TRN2", target_bir_lowering=False)
    x_t = nc.dram_tensor("x", [D, T], F32, kind="ExternalInput").ap()
    h1_t = nc.dram_tensor("h1", [H, T], F32, kind="ExternalInput").ap()
    mall_t = nc.dram_tensor("mall", [128, NCORE], F32, kind="ExternalInput").ap()
    if prequant:
        wq2i_t = nc.dram_tensor("wq2i", [H, D], BF16, kind="ExternalInput").ap()
        swi_t = nc.dram_tensor("swi", [1, 1], F32, kind="ExternalInput").ap()
    else:
        w2T_t = nc.dram_tensor("w2T", [H, D], F32, kind="ExternalInput").ap()
    b2c_t = nc.dram_tensor("b2c", [128, DC], F32, kind="ExternalInput").ap()
    eye_t = nc.dram_tensor("eye", [128, 128], F32, kind="ExternalInput").ap()
    xo_t = nc.dram_tensor("xo", [D, T], F32, kind="ExternalOutput").ap()
    if not prequant:
        wq2_t = nc.dram_tensor("wq2", [H, D], BF16, kind="ExternalOutput").ap()
        sw_t = nc.dram_tensor("sw", [1, 1], F32, kind="ExternalOutput").ap()

    NW = HC
    NQ = 4
    with ExitStack() as st:
        S = st.enter_context
        wbuf = [S(nc.sbuf_tensor(f"wb{i}", [128, D], F32)) for i in range(2)]
        wq2r = [S(nc.sbuf_tensor(f"wq2r{h}", [128, D], BF16)) for h in range(NW)]
        hbuf = [S(nc.sbuf_tensor(f"hb{i}", [128, DC * TT], F32)) for i in range(2)]
        hq = [S(nc.sbuf_tensor(f"hq{i}", [128, DC * TT], BF16)) for i in range(2)]
        ubuf = [S(nc.sbuf_tensor(f"ub{i}", [128, T], F32)) for i in range(2)]
        xo = [S(nc.sbuf_tensor(f"xo_sb{i}", [128, T], F32)) for i in range(2)]
        wint = S(nc.sbuf_tensor("wint", [128, D], I32))
        wf = S(nc.sbuf_tensor("wf", [128, D], F32))
        xint = S(nc.sbuf_tensor("xint", [128, TT], I32))
        qf = S(nc.sbuf_tensor("qf", [128, TT], F32))
        eye = S(nc.sbuf_tensor("eye_sb", [128, 128], F32))
        oner = S(nc.sbuf_tensor("oner", [1, 128], F32))
        mallb = S(nc.sbuf_tensor("mallb", [128, NCORE], F32))
        b2c = S(nc.sbuf_tensor("b2c_sb", [128, DC], F32))
        b2s = S(nc.sbuf_tensor("b2s", [128, DC], F32))
        red8 = S(nc.sbuf_tensor("red8", [128, NW], F32))
        wmx = S(nc.sbuf_tensor("wmx", [128, 1], F32))
        mxm = S(nc.sbuf_tensor("mxm", [128, 1], F32))
        sxb = S(nc.sbuf_tensor("sxb", [128, 1], F32))
        rxb = S(nc.sbuf_tensor("rxb", [128, 1], F32))
        s5b = S(nc.sbuf_tensor("s5b", [128, 1], F32))
        swb = S(nc.sbuf_tensor("swb", [128, 1], F32))
        rwb = S(nc.sbuf_tensor("rwb", [128, 1], F32))
        sc1 = S(nc.sbuf_tensor("sc1", [128, 1], F32))
        wsc = S(nc.sbuf_tensor("wsc", [1, 1], F32))
        sw_s = S(nc.sbuf_tensor("sw_s", [1, 1], F32))
        rw1 = S(nc.sbuf_tensor("rw1", [1, 1], F32))
        r1a = S(nc.sbuf_tensor("r1a", [1, 1], F32))
        srow = S(nc.sbuf_tensor("srow", [1, 2], F32))
        pmm = [S(nc.psum_tensor(f"pmm{i}", [128, TT], F32)) for i in range(DC)]
        lfix = S(nc.semaphore("lfix"))
        lu = [S(nc.semaphore(f"lu{i}")) for i in range(2)]
        lh = [S(nc.semaphore(f"lh{i}")) for i in range(2)]
        gw = [S(nc.semaphore(f"gw{i}")) for i in range(2)]
        sx_ = [S(nc.semaphore(f"sx{i}")) for i in range(2)]
        swq = [S(nc.semaphore(f"swq{i}")) for i in range(2)]
        so_ = S(nc.semaphore("so_"))
        sv = S(nc.semaphore("sv"))
        sp = S(nc.semaphore("sp"))
        sa = S(nc.semaphore("sa"))
        vf_s = S(nc.semaphore("vf"))
        blk = S(nc.Block())

        V_ONES = 1
        V_WCH = lambda h: 2 + h
        V_WMX = 2 + NW
        V_SROW = V_WMX + 1
        V_SCAL = V_SROW + 1
        V_WQ = lambda h: V_SCAL + 1 + h
        V_HQ = lambda t, q: V_WQ(NW - 1) + 1 + t * (NQ + 1) + q
        V_XO = lambda t: V_WQ(NW - 1) + 1 + t * (NQ + 1) + NQ
        V_FIN = V_XO(NT - 1)
        V_SWST = V_SROW
        P_TR = 1
        P_BC = 2
        P_MM = lambda t, q: 3 + t * NQ + q
        P_T = lambda t: P_MM(t, NQ - 1)

        @blk.sync
        def _(sync):
            sync.dma_start(out=mallb[:], in_=mall_t).then_inc(lfix, 16)
            sync.dma_start(out=b2c[:], in_=b2c_t).then_inc(lfix, 16)
            sync.dma_start(out=eye[:], in_=eye_t).then_inc(lfix, 16)
            if prequant:
                sync.dma_start(out=sw_s[:], in_=swi_t).then_inc(lfix, 16)
            for t in range(NT):
                if t >= 2:
                    sync.wait_ge(sv, V_XO(t - 2))
                sync.dma_start(
                    out=ubuf[t % 2][:].rearrange("p (c t) -> p c t", c=DC),
                    in_=x_t[:, t * TT:(t + 1) * TT].rearrange("(c p) t -> p c t", p=128),
                ).then_inc(lu[t % 2], 16)
                for j in (4 * t + 1, 4 * t + 3):
                    tq, qq = divmod(j, NQ)
                    if j >= 2:
                        sync.wait_ge(sv, V_HQ(*divmod(j - 2, NQ)))
                    sync.dma_start(
                        out=hbuf[1][:].rearrange("p (c t) -> p c t", c=DC),
                        in_=h1_t[qq * (H // NQ):(qq + 1) * (H // NQ),
                                 tq * TT:(tq + 1) * TT].rearrange("(c p) t -> p c t", p=128),
                    ).then_inc(lh[1], 16)
            sync.wait_ge(sv, V_FIN)
            if not prequant:
                sync.wait_ge(sv, V_SWST)
                sync.dma_start(out=sw_t, in_=sw_s[:]).then_inc(so_, 16)
                sync.wait_ge(so_, 16)
                sync.wait_ge(swq[0], 16 * (NW // 2))
                sync.wait_ge(swq[1], 16 * (NW // 2))
            sync.wait_ge(sx_[0], 16 * (NT // 2))
            sync.wait_ge(sx_[1], 16 * (NT // 2))

        @blk.gpsimd
        def _(gpsimd):
            if prequant:
                for h in range(NW):
                    if h >= 2:
                        gpsimd.wait_ge(sv, V_WQ(h - 2))
                    gpsimd.dma_start(out=wq2r[h][:],
                                     in_=wq2i_t[h * 128:(h + 1) * 128, :]
                                     ).then_inc(gw[h % 2], 16)
            else:
                for h in range(NW):
                    if h >= 2:
                        gpsimd.wait_ge(sv, V_WCH(h - 2))
                    gpsimd.dma_start(out=wbuf[h % 2][:],
                                     in_=w2T_t[h * 128:(h + 1) * 128, :]
                                     ).then_inc(gw[h % 2], 16)
                gpsimd.wait_ge(sv, V_WCH(NW - 1))
                for h in range(NW):
                    if h >= 2:
                        gpsimd.wait_ge(sv, V_WQ(h - 2))
                    gpsimd.dma_start(out=wbuf[h % 2][:],
                                     in_=w2T_t[h * 128:(h + 1) * 128, :]
                                     ).then_inc(gw[h % 2], 16)
            for t in range(NT):
                for q in range(NQ):
                    j = t * NQ + q
                    if j % 2 != 0:
                        continue
                    if j >= 2:
                        gpsimd.wait_ge(sv, V_HQ(*divmod(j - 2, NQ)))
                    gpsimd.dma_start(
                        out=hbuf[0][:].rearrange("p (c t) -> p c t", c=DC),
                        in_=h1_t[q * (H // NQ):(q + 1) * (H // NQ),
                                 t * TT:(t + 1) * TT].rearrange("(c p) t -> p c t", p=128),
                    ).then_inc(lh[0], 16)

        @blk.vector
        def _(vector):
            vf = FE(vector, vf_s)
            vector.memset(oner[:], 1.0)
            vector.memset(wmx[:], 0.0).then_inc(sv, 1)
            vector.wait_ge(sv, V_ONES)
            vector.wait_ge(lfix, 16 * (4 if prequant else 3))
            vf(vector.tensor_reduce(out=mxm[:], in_=mallb[:], axis=AX.X, op=ALU.max,
                                    apply_absolute_value=True))
            vf(vector.tensor_scalar(out=sxb[:], in0=mxm[:], scalar1=MINR,
                                    scalar2=1.0 / QP, op0=ALU.max, op1=ALU.mult))
            recip_newton_f(vector, vf, rxb[:], sxb[:], sc1[:])
            if not prequant:
                for h in range(NW):
                    vector.wait_ge(gw[h % 2], 16 * (h // 2 + 1))
                    vector.tensor_reduce(out=red8[:, h:h + 1], in_=wbuf[h % 2][:],
                                         axis=AX.X, op=ALU.max,
                                         apply_absolute_value=True).then_inc(sv, 1)
                vector.wait_ge(sv, V_WCH(NW - 1))
                vector.tensor_reduce(out=wmx[:], in_=red8[:], axis=AX.X, op=ALU.max,
                                     apply_absolute_value=True).then_inc(sv, 1)
                vector.wait_ge(sv, V_WMX)
                vector.wait_ge(sp, P_TR)
                vf(vector.tensor_reduce(out=wsc[:], in_=pmm[0][0:1, 0:128],
                                        axis=AX.X, op=ALU.max,
                                        apply_absolute_value=True))
                vf(vector.tensor_scalar(out=sw_s[:], in0=wsc[:], scalar1=MINR,
                                        scalar2=1.0 / QP, op0=ALU.max, op1=ALU.mult))
            else:
                for _ in range(NW + 1):
                    vector.nop().then_inc(sv, 1)
                vector.wait_ge(sv, V_WMX)
            recip_newton_f(vector, vf, rw1[:], sw_s[:], r1a[:])
            vector.tensor_copy(out=srow[0:1, 0:1], in_=sw_s[:])
            vector.tensor_copy(out=srow[0:1, 1:2], in_=rw1[:]).then_inc(sv, 1)
            vector.wait_ge(sp, P_BC)
            vector.tensor_copy(out=swb[:], in_=pmm[0][:, 0:1])
            vf(vector.tensor_copy(out=rwb[:], in_=pmm[0][:, 1:2]))
            vf(vector.tensor_tensor(out=s5b[:], in0=swb[:], in1=sxb[:], op=ALU.mult))
            vf(vector.tensor_scalar(out=s5b[:], in0=s5b[:], scalar1=0.5,
                                    scalar2=None, op0=ALU.mult))
            vector.tensor_scalar(out=b2s[:], in0=b2c[:], scalar1=s5b[:, 0:1],
                                 scalar2=None, op0=ALU.mult).then_inc(sv, 1)
            for h in range(NW):
                if prequant:
                    vector.wait_ge(gw[h % 2], 16 * (h // 2 + 1))
                    vector.nop().then_inc(sv, 1)
                    continue
                vector.wait_ge(gw[h % 2], 16 * (NW // 2 + h // 2 + 1))
                vf(vector.tensor_scalar(out=wf[:], in0=wbuf[h % 2][:],
                                        scalar1=rwb[:, 0:1], scalar2=MAGIC,
                                        op0=ALU.mult, op1=ALU.add))
                vf(vector.tensor_scalar(out=wint[:], in0=wf[:], scalar1=MAGIC,
                                        scalar2=None, op0=ALU.subtract))
                vector.tensor_copy(out=wq2r[h][:], in_=wint[:]).then_inc(sv, 1)
                vector.wait_ge(sv, V_WQ(h))
            for t in range(NT):
                for q in range(NQ):
                    j = t * NQ + q
                    i = j % 2
                    vector.wait_ge(lh[i], 16 * (j // 2 + 1))
                    if j >= 2:
                        vector.wait_ge(sp, P_MM(*divmod(j - 2, NQ)))
                    for c in range(DC):
                        cs = slice(c * TT, (c + 1) * TT)
                        vf(vector.tensor_scalar(out=qf[:], in0=hbuf[i][:, cs],
                                                scalar1=rxb[:, 0:1], scalar2=MAGIC,
                                                op0=ALU.mult, op1=ALU.add))
                        vf(vector.tensor_scalar(out=xint[:], in0=qf[:], scalar1=MAGIC,
                                                scalar2=None, op0=ALU.subtract))
                        ins = vector.tensor_copy(out=hq[i][:, cs], in_=xint[:])
                        if c < DC - 1:
                            vf(ins)
                    ins.then_inc(sv, 1)
                    vector.wait_ge(sv, V_HQ(t, q))
                vector.wait_ge(lu[t % 2], 16 * (t // 2 + 1))
                vector.wait_ge(sa, DC * (t + 1))
                for d in range(DC):
                    ds = slice(d * TT, (d + 1) * TT)
                    ins = vector.tensor_add(out=xo[t % 2][:, ds],
                                            in0=xo[t % 2][:, ds],
                                            in1=ubuf[t % 2][:, ds])
                ins.then_inc(sv, 1)
                vector.wait_ge(sv, V_XO(t))

        @blk.tensor
        def _(tensor):
            tensor.wait_ge(lfix, 16 * (4 if prequant else 3))
            if not prequant:
                tensor.wait_ge(sv, V_WMX)
                tensor.transpose(out=pmm[0][0:1, 0:128], in_=wmx[:],
                                 identity=eye[:]).then_inc(sp, 1)
            else:
                tensor.nop().then_inc(sp, 1)
            tensor.wait_ge(sv, V_SROW)
            tensor.matmul(out=pmm[0][:, 0:2], lhsT=oner[:], rhs=srow[:],
                          start=True, stop=True).then_inc(sp, 1)
            for t in range(NT):
                for q in range(NQ):
                    i = (t * NQ + q) % 2
                    if t == 0:
                        tensor.wait_ge(sv, V_WQ(q * DC + DC - 1))
                    tensor.wait_ge(sv, V_HQ(t, q))
                    if q == 0 and t >= 1:
                        tensor.wait_ge(sa, DC * t)
                    for hh in range(DC):
                        h = q * DC + hh
                        for d in range(DC):
                            ins = tensor.matmul(
                                out=pmm[d][:],
                                lhsT=wq2r[h][:, d * 128:(d + 1) * 128],
                                rhs=hq[i][:, hh * TT:(hh + 1) * TT],
                                start=(h == 0), stop=(h == NW - 1))
                    ins.then_inc(sp, 1)

        @blk.scalar
        def _(scalar):
            if not prequant:
                for h in range(NW):
                    scalar.wait_ge(sv, V_WQ(h))
                    scalar.dma_start(out=wq2_t[h * 128:(h + 1) * 128, :],
                                     in_=wq2r[h][:]).then_inc(swq[h % 2], 16)
            scalar.wait_ge(sv, V_SCAL)
            for t in range(NT):
                scalar.wait_ge(sp, P_T(t))
                if t >= 2:
                    scalar.wait_ge(sx_[t % 2], 16 * (t // 2))
                for d in range(DC):
                    ds = slice(d * TT, (d + 1) * TT)
                    scalar.activation(out=xo[t % 2][:, ds], in_=pmm[d][:],
                                      func=AF.Identity, bias=b2s[:, d:d + 1],
                                      scale=s5b[:, 0:1]).then_inc(sa, 1)
                scalar.wait_ge(sv, V_XO(t))
                scalar.dma_start(
                    out=xo_t[:, t * TT:(t + 1) * TT].rearrange("(c p) t -> p c t", p=128),
                    in_=xo[t % 2][:].rearrange("p (c t) -> p c t", c=DC),
                ).then_inc(sx_[t % 2], 16)

    return nc






def build_L3b():
    nc = bass.Bass("TRN2", target_bir_lowering=False)
    x_t = nc.dram_tensor("x", [D, T], F32, kind="ExternalInput").ap()
    waT_t = nc.dram_tensor("waT", [D, D], F32, kind="ExternalInput").ap()
    wbT_t = nc.dram_tensor("wbT", [D, D], F32, kind="ExternalInput").ap()
    bac_t = nc.dram_tensor("bac", [128, DC], F32, kind="ExternalInput").ap()
    bbc_t = nc.dram_tensor("bbc", [128, DC], F32, kind="ExternalInput").ap()
    stats_t = nc.dram_tensor("stats", [2, T], F32, kind="ExternalOutput").ap()
    hs0_t = nc.dram_tensor("hs0", [D, T], F32, kind="ExternalOutput").ap()
    P_t = nc.dram_tensor("P", [D, T], F32, kind="ExternalOutput").ap()
    hend_t = nc.dram_tensor("hend", [128, DC], F32, kind="ExternalOutput").ap()

    with ExitStack() as st:
        S = st.enter_context
        xbuf = [S(nc.sbuf_tensor(f"xbuf{i}", [128, T], F32)) for i in range(2)]
        sq = [S(nc.sbuf_tensor(f"sq{i}", [128, TT], F32)) for i in range(2)]
        xh = [S(nc.sbuf_tensor(f"xh{d}", [128, T], BF16)) for d in range(DC)]
        wab = [S(nc.sbuf_tensor(f"wab{d}", [128, D], BF16)) for d in range(DC)]
        wbb = [S(nc.sbuf_tensor(f"wbb{d}", [128, D], BF16)) for d in range(DC)]
        wld = [S(nc.sbuf_tensor(f"wld{i}", [128, D], F32)) for i in range(2)]
        alpha = S(nc.sbuf_tensor("alpha", [128, T], F32))
        bu = S(nc.sbuf_tensor("bu", [128, T], F32))
        hs0 = S(nc.sbuf_tensor("hs0_sb", [128, T], F32))
        Pb = xbuf[1]
        bac = S(nc.sbuf_tensor("bac_sb", [128, DC], F32))
        bbc = S(nc.sbuf_tensor("bbc_sb", [128, DC], F32))
        hend = S(nc.sbuf_tensor("hend_sb", [128, DC], F32))
        onec = S(nc.sbuf_tensor("onec", [128, 1], F32))
        oner = S(nc.sbuf_tensor("oner", [1, 128], F32))
        t0 = S(nc.sbuf_tensor("t0", [1, TT], F32))
        t1 = S(nc.sbuf_tensor("t1", [1, TT], F32))
        vr = S(nc.sbuf_tensor("vr", [1, TT], F32))
        lnt = S(nc.sbuf_tensor("lnt", [1, TT], F32))
        mr = S(nc.sbuf_tensor("mr", [1, TT], F32))
        rr = S(nc.sbuf_tensor("rr", [1, TT], F32))
        tmpq = S(nc.sbuf_tensor("tmpq", [128, TT], F32))
        psS = [S(nc.psum_tensor(f"psS{i}", [128, TT], F32)) for i in range(2)]
        psM = [S(nc.psum_tensor(f"psM{i}", [128, TT], F32)) for i in range(2)]
        psA = [S(nc.psum_tensor(f"psA{i}", [128, TT], F32)) for i in range(2)]
        psB = [S(nc.psum_tensor(f"psB{i}", [128, TT], F32)) for i in range(2)]
        pA = [psA[0], psA[1], psS[0], psS[1]]
        pB = [psB[0], psB[1], psM[0], psM[1]]
        lfix = S(nc.semaphore("lfix"))
        lx2 = [S(nc.semaphore(f"lx{i}")) for i in range(2)]
        gw = [S(nc.semaphore(f"gw{i}")) for i in range(2)]
        sh = [S(nc.semaphore(f"sh{i}")) for i in range(2)]
        srl = S(nc.semaphore("srl"))
        she = S(nc.semaphore("she"))
        sv = S(nc.semaphore("sv"))
        sp = S(nc.semaphore("sp"))
        sa = S(nc.semaphore("sa"))
        vf_s = S(nc.semaphore("vf"))
        af_s = S(nc.semaphore("af"))
        blk = S(nc.Block())

        V_INIT = 1
        V_WB = lambda k: 2 + k
        V_ROW = lambda t: 18 + 3 * t
        V_VAR = lambda t: 19 + 3 * t
        V_XH = lambda t: 20 + 3 * t
        V_PHA = V_XH(NT - 1)
        V_Z = V_PHA + 1
        V_SC = lambda c: V_Z + 1 + 3 * c + 0
        V_PS = lambda c: V_Z + 2 + 3 * c
        V_HE = lambda c: V_Z + 3 + 3 * c
        V_FIN = V_HE(DC - 1)
        A_SQ = lambda t, d: 9 * t + d + 1
        A_RST = lambda t: 9 * t + 9
        A_EV = lambda c, tt: 9 * NT + 2 * (c * NT + tt) + 2
        P_S1 = lambda t: 10 * t + 1
        P_SQ = lambda t, d: 10 * t + 2 + d
        P_S2 = lambda t: 10 * t + 9
        P_BC = lambda t: 10 * t + 10
        P_AB = lambda c, tt: 10 * NT + c * NT + tt + 1

        @blk.sync
        def _(sync):
            sync.dma_start(out=bac[:], in_=bac_t).then_inc(lfix, 16)
            sync.dma_start(out=bbc[:], in_=bbc_t).then_inc(lfix, 16)
            for t in range(NT):
                if t >= 2:
                    sync.wait_ge(sv, V_XH(t - 2))
                sync.dma_start(
                    out=xbuf[t % 2][:].rearrange("p (c t) -> p c t", c=DC),
                    in_=x_t[:, t * TT:(t + 1) * TT].rearrange("(c p) t -> p c t", p=128),
                ).then_inc(lx2[t % 2], 16)
            sync.wait_ge(sv, V_FIN)
            sync.dma_start(out=hend_t, in_=hend[:]).then_inc(she, 16)
            sync.wait_ge(she, 16)
            sync.wait_ge(srl, 16 * 2 * NT)
            sync.wait_ge(sh[0], 16 * DC)
            sync.wait_ge(sh[1], 16 * DC)

        @blk.gpsimd
        def _(gpsimd):
            for k in range(2 * DC):
                if k >= 2:
                    gpsimd.wait_ge(sv, V_WB(k - 2))
                src_t = waT_t if k < DC else wbT_t
                d = k % DC
                gpsimd.dma_start(out=wld[k % 2][:],
                                 in_=src_t[d * 128:(d + 1) * 128, :]
                                 ).then_inc(gw[k % 2], 16)

        @blk.vector
        def _(vector):
            vf = FE(vector, vf_s)
            vector.memset(onec[:], 1.0)
            vector.memset(oner[:], 1.0).then_inc(sv, 1)
            vector.wait_ge(sv, V_INIT)
            for k in range(2 * DC):
                vector.wait_ge(gw[k % 2], 16 * (k // 2 + 1))
                dst = wab[k] if k < DC else wbb[k - DC]
                vector.tensor_copy(out=dst[:], in_=wld[k % 2][:]).then_inc(sv, 1)
            for t in range(NT):
                ts_ = slice(t * TT, (t + 1) * TT)
                vector.wait_ge(sp, P_S1(t))
                if t >= 1:
                    vector.wait_ge(srl, 32 * t)
                vf(vector.tensor_scalar(out=mr[:], in0=psS[t % 2][0:1, :],
                                        scalar1=INV_D, scalar2=None, op0=ALU.mult))
                vector.wait_ge(sp, P_S2(t))
                vector.tensor_scalar(out=t0[:], in0=psM[t % 2][0:1, :],
                                     scalar1=INV_D, scalar2=None, op0=ALU.mult
                                     ).then_inc(sv, 1)
                vf(vector.tensor_tensor(out=t1[:], in0=mr[:], in1=mr[:],
                                        op=ALU.mult))
                vector.scalar_tensor_tensor(out=vr[:], in0=t0[:], scalar=EPS,
                                            in1=t1[:], op0=ALU.add,
                                            op1=ALU.subtract).then_inc(sv, 1)
                vector.wait_ge(sv, V_VAR(t))
                vector.wait_ge(sp, P_BC(t))
                for d in range(DC):
                    ds = slice(d * TT, (d + 1) * TT)
                    vf(vector.tensor_sub(out=tmpq[:], in0=xbuf[t % 2][:, ds],
                                         in1=psA[t % 2][:]))
                    ins = vector.tensor_tensor(out=xh[d][:, ts_], in0=tmpq[:],
                                               in1=psB[t % 2][:], op=ALU.mult)
                    if d < DC - 1:
                        vf(ins)
                ins.then_inc(sv, 1)
                vector.wait_ge(sv, V_XH(t))
            vector.memset(xbuf[0][:], 0.0).then_inc(sv, 1)
            vector.wait_ge(sv, V_Z)
            for c in range(DC):
                vector.wait_ge(sa, A_EV(c, NT - 1))
                if c >= 1:
                    vector.wait_ge(sh[0], 16 * c)
                    vector.wait_ge(sh[1], 16 * c)
                vector.tensor_tensor_scan(out=hs0[:], data0=alpha[:], data1=bu[:],
                                          initial=0.0, op0=ALU.mult, op1=ALU.add
                                          ).then_inc(sv, 1)
                vector.tensor_tensor_scan(out=Pb[:], data0=alpha[:], data1=xbuf[0][:],
                                          initial=1.0, op0=ALU.mult, op1=ALU.add
                                          ).then_inc(sv, 1)
                vector.wait_ge(sv, V_SC(c))
                vector.tensor_copy(out=hend[:, c:c + 1], in_=hs0[:, T - 1:T]
                                   ).then_inc(sv, 1)
                vector.wait_ge(sv, V_HE(c))

        @blk.tensor
        def _(tensor):
            tensor.wait_ge(sv, V_INIT)
            for t in range(NT):
                ts_ = slice(t * TT, (t + 1) * TT)
                tensor.wait_ge(lx2[t % 2], 16 * (t // 2 + 1))
                if t >= 2:
                    tensor.wait_ge(sv, V_XH(t - 2))
                for d in range(DC):
                    ds = slice(d * TT, (d + 1) * TT)
                    ins = tensor.matmul(out=psS[t % 2][0:1, :], lhsT=onec[:],
                                        rhs=xbuf[t % 2][:, ds], start=(d == 0),
                                        stop=(d == DC - 1))
                ins.then_inc(sp, 1)
                for d in range(DC):
                    tensor.wait_ge(sa, A_SQ(t, d))
                    tensor.matmul(out=psM[t % 2][0:1, :], lhsT=onec[:],
                                  rhs=sq[d % 2][:], start=(d == 0),
                                  stop=(d == DC - 1)).then_inc(sp, 1)
                tensor.wait_ge(sa, A_RST(t))
                tensor.matmul(out=psA[t % 2][:], lhsT=oner[:], rhs=mr[:],
                              start=True, stop=True)
                tensor.matmul(out=psB[t % 2][:], lhsT=oner[:], rhs=rr[:],
                              start=True, stop=True).then_inc(sp, 1)
            tensor.wait_ge(sv, V_WB(2 * DC - 1))
            tensor.wait_ge(sv, V_PHA)
            for c in range(DC):
                ns = slice(c * 128, (c + 1) * 128)
                for tt in range(NT):
                    ts_ = slice(tt * TT, (tt + 1) * TT)
                    j = c * NT + tt
                    if j >= 4:
                        tensor.wait_ge(sa, A_EV(*divmod(j - 4, NT)))
                    for d in range(DC):
                        tensor.matmul(out=pA[j % 4][:], lhsT=wab[d][:, ns],
                                      rhs=xh[d][:, ts_], start=(d == 0),
                                      stop=(d == DC - 1))
                    for d in range(DC):
                        ins = tensor.matmul(out=pB[j % 4][:], lhsT=wbb[d][:, ns],
                                            rhs=xh[d][:, ts_], start=(d == 0),
                                            stop=(d == DC - 1))
                    ins.then_inc(sp, 1)

        @blk.scalar
        def _(scalar):
            af = FE(scalar, af_s)
            scalar.wait_ge(lfix, 32)
            for t in range(NT):
                ts_ = slice(t * TT, (t + 1) * TT)
                scalar.wait_ge(lx2[t % 2], 16 * (t // 2 + 1))
                if t >= 1:
                    scalar.wait_ge(srl, 32 * t)
                for d in range(DC):
                    ds = slice(d * TT, (d + 1) * TT)
                    g = t * DC + d
                    if g >= 2:
                        scalar.wait_ge(sp, P_SQ((g - 2) // DC, (g - 2) % DC))
                    scalar.activation(out=sq[d % 2][:], in_=xbuf[t % 2][:, ds],
                                      func=AF.Square).then_inc(sa, 1)
                scalar.wait_ge(sv, V_VAR(t))
                af(scalar.activation(out=lnt[:], in_=vr[:], func=AF.Ln))
                scalar.activation(out=rr[:], in_=lnt[:], func=AF.Exp,
                                  scale=-0.5).then_inc(sa, 1)
                scalar.wait_ge(sa, A_RST(t))
                scalar.dma_start(out=stats_t[0:1, ts_], in_=mr[:]).then_inc(srl, 16)
                scalar.dma_start(out=stats_t[1:2, ts_], in_=rr[:]).then_inc(srl, 16)
            for c in range(DC):
                if c >= 1:
                    scalar.wait_ge(sv, V_SC(c - 1))
                    scalar.dma_start(out=hs0_t[(c - 1) * 128:c * 128, :],
                                     in_=hs0[:]).then_inc(sh[0], 16)
                    scalar.wait_ge(sv, V_PS(c - 1))
                    scalar.dma_start(out=P_t[(c - 1) * 128:c * 128, :],
                                     in_=Pb[:]).then_inc(sh[1], 16)
                for tt in range(NT):
                    ts_ = slice(tt * TT, (tt + 1) * TT)
                    j = c * NT + tt
                    scalar.wait_ge(sp, P_AB(c, tt))
                    if tt == 0 and c >= 1:
                        scalar.wait_ge(sv, V_PS(c - 1))
                    scalar.activation(out=alpha[:, ts_], in_=pA[j % 4][:],
                                      func=AF.Sigmoid, bias=bac[:, c:c + 1]
                                      ).then_inc(sa, 1)
                    scalar.activation(out=bu[:, ts_], in_=pB[j % 4][:],
                                      func=AF.Identity, bias=bbc[:, c:c + 1]
                                      ).then_inc(sa, 1)
            scalar.wait_ge(sv, V_SC(DC - 1))
            scalar.dma_start(out=hs0_t[(DC - 1) * 128:DC * 128, :], in_=hs0[:]
                             ).then_inc(sh[0], 16)
            scalar.wait_ge(sv, V_PS(DC - 1))
            scalar.dma_start(out=P_t[(DC - 1) * 128:DC * 128, :], in_=Pb[:]
                             ).then_inc(sh[1], 16)

    return nc





def build_L4a():
    nc = bass.Bass("TRN2", target_bir_lowering=False)
    x_t = nc.dram_tensor("x", [D, T], F32, kind="ExternalInput").ap()
    hs0_t = nc.dram_tensor("hs0", [D, T], F32, kind="ExternalInput").ap()
    P_t = nc.dram_tensor("P", [D, T], F32, kind="ExternalInput").ap()
    hin_t = nc.dram_tensor("hin", [128, DC], F32, kind="ExternalInput").ap()
    stats_t = nc.dram_tensor("stats", [2, T], F32, kind="ExternalInput").ap()
    cT_t = nc.dram_tensor("cT", [D, D], F32, kind="ExternalInput").ap()
    Dc_t = nc.dram_tensor("Dc", [128, DC], F32, kind="ExternalInput").ap()
    x3_t = nc.dram_tensor("x3", [D, T], F32, kind="ExternalOutput").ap()

    TT2 = 256
    NT2 = T // TT2
    with ExitStack() as st:
        S = st.enter_context
        mrl = [S(nc.sbuf_tensor(f"mrl{i}", [1, TT2], F32)) for i in range(2)]
        rrl = [S(nc.sbuf_tensor(f"rrl{i}", [1, TT2], F32)) for i in range(2)]
        HT = T // 2
        hsld = [S(nc.sbuf_tensor(f"hsld{i}", [128, HT], F32)) for i in range(2)]
        pld = [S(nc.sbuf_tensor(f"pld{i}", [128, HT], F32)) for i in range(2)]
        hsb = [S(nc.sbuf_tensor(f"hsb{c}", [128, T], BF16)) for c in range(DC)]
        ctb = [S(nc.sbuf_tensor(f"ctb{c}", [128, D], BF16)) for c in range(DC)]
        cld = S(nc.sbuf_tensor("cld", [128, D], F32))
        x2b = [S(nc.sbuf_tensor(f"x2b{i}", [128, DC * TT2], F32)) for i in range(2)]
        ystg = [S(nc.sbuf_tensor(f"ystg{i}", [128, DC * TT2], F32)) for i in range(2)]
        xhst = S(nc.sbuf_tensor("xhst", [128, DC * TT2], F32))
        hin = S(nc.sbuf_tensor("hin_sb", [128, DC], F32))
        Dc = S(nc.sbuf_tensor("Dc_sb", [128, DC], F32))
        oner = S(nc.sbuf_tensor("oner", [1, 128], F32))
        pbank = [S(nc.psum_tensor(f"pbank{i}", [128, 2 * TT2], F32)) for i in range(4)]
        pmm = [pbank[i // 2][:, (i % 2) * TT2:(i % 2 + 1) * TT2] for i in range(DC)]
        psmr = [S(nc.psum_tensor(f"psmr{i}", [128, 2 * TT2], F32)) for i in range(2)]
        psM = [psmr[i][:, 0:TT2] for i in range(2)]
        psR = [psmr[i][:, TT2:2 * TT2] for i in range(2)]
        lfix = S(nc.semaphore("lfix"))
        lx = [S(nc.semaphore(f"lx{i}")) for i in range(2)]
        lm = [S(nc.semaphore(f"lm{i}")) for i in range(2)]
        gc = [S(nc.semaphore(f"gc{i}")) for i in range(2)]
        glh2 = [S(nc.semaphore(f"glh{i}")) for i in range(2)]
        glp2 = [S(nc.semaphore(f"glp{i}")) for i in range(2)]
        sx_ = [S(nc.semaphore(f"sx{i}")) for i in range(2)]
        sv = S(nc.semaphore("sv"))
        sp = S(nc.semaphore("sp"))
        sa = S(nc.semaphore("sa"))
        vf_s = S(nc.semaphore("vf"))
        blk = S(nc.Block())

        V_INIT = 1
        V_CB = lambda k: 2 + k
        V_HSB2 = lambda k: 10 + k
        V_HSB = lambda c: V_HSB2(2 * c + 1)
        V_X3 = lambda tt: 26 + tt
        V_FIN = V_X3(NT2 - 1)
        P_Y = lambda tt: tt + 1
        A_EV = lambda tt: 8 * (tt + 1)

        @blk.sync
        def _(sync):
            sync.dma_start(out=hin[:], in_=hin_t).then_inc(lfix, 16)
            sync.dma_start(out=Dc[:], in_=Dc_t).then_inc(lfix, 16)
            for tt in range(NT2):
                ts_ = slice(tt * TT2, (tt + 1) * TT2)
                if tt >= 2:
                    sync.wait_ge(sv, V_X3(tt - 2))
                    sync.wait_ge(sx_[tt % 2], 16 * (tt // 2))
                sync.dma_start(
                    out=x2b[tt % 2][:].rearrange("p (c t) -> p c t", c=DC),
                    in_=x_t[:, ts_].rearrange("(c p) t -> p c t", p=128),
                ).then_inc(lx[tt % 2], 16)
                sync.dma_start(out=mrl[tt % 2][:], in_=stats_t[0:1, ts_]
                               ).then_inc(lm[tt % 2], 16)
                sync.dma_start(out=rrl[tt % 2][:], in_=stats_t[1:2, ts_]
                               ).then_inc(lm[tt % 2], 16)
            sync.wait_ge(sv, V_FIN)
            sync.wait_ge(sx_[0], 16 * (NT2 // 2))
            sync.wait_ge(sx_[1], 16 * (NT2 // 2))

        @blk.gpsimd
        def _(gpsimd):
            for k in range(DC):
                if k >= 1:
                    gpsimd.wait_ge(sv, V_CB(k - 1))
                gpsimd.dma_start(out=cld[:],
                                 in_=cT_t[k * 128:(k + 1) * 128, :]
                                 ).then_inc(gc[k % 2], 16)
            for k in range(2 * DC):
                c, half = divmod(k, 2)
                if k >= 2:
                    gpsimd.wait_ge(sv, V_HSB2(k - 2))
                hsl = slice(half * HT, (half + 1) * HT)
                gpsimd.dma_start(out=hsld[k % 2][:],
                                 in_=hs0_t[c * 128:(c + 1) * 128, hsl]
                                 ).then_inc(glh2[k % 2], 16)
                gpsimd.dma_start(out=pld[k % 2][:],
                                 in_=P_t[c * 128:(c + 1) * 128, hsl]
                                 ).then_inc(glp2[k % 2], 16)

        @blk.vector
        def _(vector):
            vf = FE(vector, vf_s)
            vector.memset(oner[:], 1.0).then_inc(sv, 1)
            vector.wait_ge(sv, V_INIT)
            vector.wait_ge(lfix, 32)
            for k in range(DC):
                vector.wait_ge(gc[k % 2], 16 * (k // 2 + 1))
                vector.tensor_copy(out=ctb[k][:], in_=cld[:]).then_inc(sv, 1)
            for k in range(2 * DC):
                c, half = divmod(k, 2)
                hsl = slice(half * HT, (half + 1) * HT)
                vector.wait_ge(glh2[k % 2], 16 * (k // 2 + 1))
                vector.wait_ge(glp2[k % 2], 16 * (k // 2 + 1))
                vector.scalar_tensor_tensor(out=hsb[c][:, hsl], in0=pld[k % 2][:],
                                            scalar=hin[:, c:c + 1],
                                            in1=hsld[k % 2][:],
                                            op0=ALU.mult, op1=ALU.add
                                            ).then_inc(sv, 1)
                vector.wait_ge(sv, V_HSB2(k))
            for tt in range(NT2):
                ds_all = slice(0, DC * TT2)
                vector.wait_ge(lx[tt % 2], 16 * (tt // 2 + 1))
                vector.wait_ge(sp, P_Y(tt))
                vector.wait_ge(sa, A_EV(tt))
                for d in range(DC):
                    ds = slice(d * TT2, (d + 1) * TT2)
                    ins = vector.tensor_sub(out=xhst[:, ds], in0=x2b[tt % 2][:, ds],
                                            in1=psM[tt % 2])
                vf(ins)
                for d in range(DC):
                    ds = slice(d * TT2, (d + 1) * TT2)
                    ins = vector.tensor_tensor(out=xhst[:, ds], in0=xhst[:, ds],
                                               in1=psR[tt % 2], op=ALU.mult)
                vf(ins)
                for d in range(DC):
                    ds = slice(d * TT2, (d + 1) * TT2)
                    ins = vector.scalar_tensor_tensor(out=ystg[tt % 2][:, ds],
                                                      in0=xhst[:, ds],
                                                      scalar=Dc[:, d:d + 1],
                                                      in1=ystg[tt % 2][:, ds],
                                                      op0=ALU.mult, op1=ALU.add)
                vf(ins)
                for d in range(DC):
                    ds = slice(d * TT2, (d + 1) * TT2)
                    ins = vector.tensor_add(out=x2b[tt % 2][:, ds],
                                            in0=ystg[tt % 2][:, ds],
                                            in1=x2b[tt % 2][:, ds])
                ins.then_inc(sv, 1)
                vector.wait_ge(sv, V_X3(tt))

        @blk.tensor
        def _(tensor):
            tensor.wait_ge(lfix, 32)
            tensor.wait_ge(sv, V_HSB(DC - 1))
            for tt in range(NT2):
                ts_ = slice(tt * TT2, (tt + 1) * TT2)
                if tt >= 1:
                    tensor.wait_ge(sa, A_EV(tt - 1))
                if tt >= 2:
                    tensor.wait_ge(sv, V_X3(tt - 2))
                tensor.wait_ge(lm[tt % 2], 16 * (2 * (tt // 2) + 2))
                for d in range(DC):
                    for c in range(DC):
                        tensor.matmul(out=pmm[d],
                                      lhsT=ctb[c][:, d * 128:(d + 1) * 128],
                                      rhs=hsb[c][:, ts_], start=(c == 0),
                                      stop=(c == DC - 1))
                tensor.matmul(out=psM[tt % 2], lhsT=oner[:], rhs=mrl[tt % 2][:],
                              start=True, stop=True)
                tensor.matmul(out=psR[tt % 2], lhsT=oner[:], rhs=rrl[tt % 2][:],
                              start=True, stop=True).then_inc(sp, 1)

        @blk.scalar
        def _(scalar):
            for tt in range(NT2):
                scalar.wait_ge(sp, P_Y(tt))
                if tt >= 2:
                    scalar.wait_ge(sv, V_X3(tt - 2))
                for d in range(DC):
                    ds = slice(d * TT2, (d + 1) * TT2)
                    scalar.activation(out=ystg[tt % 2][:, ds], in_=pmm[d],
                                      func=AF.Identity).then_inc(sa, 1)
                scalar.wait_ge(sv, V_X3(tt))
                scalar.dma_start(
                    out=x3_t[:, tt * TT2:(tt + 1) * TT2].rearrange("(c p) t -> p c t", p=128),
                    in_=x2b[tt % 2][:].rearrange("p (c t) -> p c t", c=DC),
                ).then_inc(sx_[tt % 2], 16)

    return nc



def _asc(a):
    return np.ascontiguousarray(a)


def _shard_u(u):
    out = []
    for c in range(NCORE):
        b, h = c // 2, c % 2
        out.append(_asc(u[b, h * T:(h + 1) * T, :].T))
    return out


_EYE = np.eye(128, dtype=np.float32)


def _bcol(v):
    return np.ascontiguousarray(np.asarray(v).reshape(-1, 128).T)


def _gather_mall(results, key="mloc"):
    m = np.concatenate([np.asarray(r[key]).reshape(1, 1) for r in results], axis=1)
    return np.ascontiguousarray(np.broadcast_to(m, (128, NCORE)))


def kernel(**inputs):
    u = np.asarray(inputs["L0_attn_u"])
    w1T = _asc(np.asarray(inputs["L0_ffn_w1"]).T)
    w2T = _asc(np.asarray(inputs["L0_ffn_w2"]).T)
    waT = _asc(np.asarray(inputs["L0_attn_W_alpha_w"]).T)
    wbT = _asc(np.asarray(inputs["L0_attn_W_B_w"]).T)
    cT = _asc(np.asarray(inputs["L0_attn_C_w"]).T)
    b1c = _bcol(inputs["L0_ffn_b1"])
    b2c = _bcol(inputs["L0_ffn_b2"])
    bac = _bcol(inputs["L0_attn_W_alpha_b"])
    bbc = _bcol(inputs["L0_attn_W_B_b"])
    Dcc = _bcol(inputs["L0_attn_D"])
    shards = _shard_u(u)
    cores = list(range(NCORE))

    nc_ln = build_L1()
    nc_ffn1 = build_L2()
    nc_ffn1p = build_L2(prequant=True)
    nc_ffn2 = build_L3a()
    nc_ffn2p = build_L3a(prequant=True)
    nc_ssm1 = build_L3b()
    nc_ssm2 = build_L4a()

    r1 = run_bass_kernel_spmd(
        nc_ln, [{"u": shards[c], "eye": _EYE} for c in cores], core_ids=cores)
    m1 = _gather_mall(r1.results)

    r2 = run_bass_kernel_spmd(
        nc_ffn1, [{"u": shards[c], "stats": r1.results[c]["stats"], "mall": m1,
                   "w1T": w1T, "b1c": b1c, "eye": _EYE} for c in cores],
        core_ids=cores)
    m2 = _gather_mall(r2.results)

    r3a = run_bass_kernel_spmd(
        nc_ffn2, [{"x": shards[c], "h1": r2.results[c]["h1"], "mall": m2,
                   "w2T": w2T, "b2c": b2c, "eye": _EYE} for c in cores],
        core_ids=cores)

    r3b = run_bass_kernel_spmd(
        nc_ssm1, [{"x": r3a.results[c]["xo"], "waT": waT, "wbT": wbT,
                   "bac": bac, "bbc": bbc} for c in cores], core_ids=cores)

    zeros_hin = np.zeros((128, DC), np.float32)
    hins = [np.asarray(r3b.results[c - 1]["hend"]) if c % 2 == 1 else zeros_hin
            for c in cores]
    r4a = run_bass_kernel_spmd(
        nc_ssm2, [{"x": r3a.results[c]["xo"], "hs0": r3b.results[c]["hs0"],
                   "P": r3b.results[c]["P"], "hin": hins[c],
                   "stats": r3b.results[c]["stats"], "cT": cT, "Dc": Dcc}
                  for c in cores], core_ids=cores)

    r4b = run_bass_kernel_spmd(
        nc_ln, [{"u": r4a.results[c]["x3"], "eye": _EYE} for c in cores],
        core_ids=cores)
    m3 = _gather_mall(r4b.results)

    r5 = run_bass_kernel_spmd(
        nc_ffn1p, [{"u": r4a.results[c]["x3"], "stats": r4b.results[c]["stats"],
                    "mall": m3, "wq1i": r2.results[c]["wq1"],
                    "swi": r2.results[c]["sw"], "b1c": b1c, "eye": _EYE}
                   for c in cores], core_ids=cores)
    m4 = _gather_mall(r5.results)

    r6 = run_bass_kernel_spmd(
        nc_ffn2p, [{"x": r4a.results[c]["x3"], "h1": r5.results[c]["h1"],
                    "mall": m4, "wq2i": r3a.results[c]["wq2"],
                    "swi": r3a.results[c]["sw"], "b2c": b2c, "eye": _EYE}
                   for c in cores], core_ids=cores)

    out = np.empty((4, 8192, D), np.float32)
    for c in cores:
        b, h = c // 2, c % 2
        out[b, h * T:(h + 1) * T, :] = np.asarray(r6.results[c]["xo"]).T
    return out


# version 07
# speedup vs baseline: 1.2306x; 1.0501x over previous
import numpy as np
import ml_dtypes
from contextlib import ExitStack

import concourse.bass as bass
import concourse.mybir as mybir
from concourse.bass_utils import run_bass_kernel_spmd

F32 = mybir.dt.float32
BF16 = mybir.dt.bfloat16
I32 = mybir.dt.int32
AF = mybir.ActivationFunctionType
ALU = mybir.AluOpType
AX = mybir.AxisListType

NCORE = 8
D = 1024
T = 4096
H = 4096
DC = D // 128
HC = H // 128
TT = 512
NT = T // TT
EPS = 1e-5
QP = 127.0
MAGIC = 12582912.0
MINR = 1e-6
INV_D = 1.0 / D



def recip_newton(vec, out, in_, scr):
    vec.reciprocal(out=out, in_=in_)
    vec.tensor_tensor(out=scr, in0=in_, in1=out, op=ALU.mult)
    vec.tensor_scalar(out=scr, in0=scr, scalar1=-1.0, scalar2=2.0,
                      op0=ALU.mult, op1=ALU.add)
    vec.tensor_tensor(out=out, in0=out, in1=scr, op=ALU.mult)


def quant_tiles(vec, out_bf, in_f32, r_ap, scr_i32):
    vec.tensor_scalar(out=scr_i32, in0=in_f32, scalar1=r_ap, scalar2=None,
                      op0=ALU.mult)
    vec.tensor_copy(out=out_bf, in_=scr_i32)




class FE:

    def __init__(self, eng, sem):
        self.eng, self.sem, self.n = eng, sem, 0

    def __call__(self, ins):
        self.n += 1
        ins.then_inc(self.sem, 1)
        self.eng.wait_ge(self.sem, self.n)


def recip_newton_f(vec, fe, out, in_, scr):
    fe(vec.reciprocal(out=out, in_=in_))
    fe(vec.tensor_tensor(out=scr, in0=in_, in1=out, op=ALU.mult))
    fe(vec.tensor_scalar(out=scr, in0=scr, scalar1=-1.0, scalar2=2.0,
                         op0=ALU.mult, op1=ALU.add))
    fe(vec.tensor_tensor(out=out, in0=out, in1=scr, op=ALU.mult))



def build_L1():
    nc = bass.Bass("TRN2", target_bir_lowering=False)
    u_t = nc.dram_tensor("u", [D, T], F32, kind="ExternalInput").ap()
    eye_t = nc.dram_tensor("eye", [128, 128], F32, kind="ExternalInput").ap()
    stats_t = nc.dram_tensor("stats", [2, T], F32, kind="ExternalOutput").ap()
    mloc_t = nc.dram_tensor("mloc", [1, 1], F32, kind="ExternalOutput").ap()

    with ExitStack() as st:
        S = st.enter_context
        us = [S(nc.sbuf_tensor(f"u{d}", [128, T], F32)) for d in range(DC)]
        eye = S(nc.sbuf_tensor("eye_sb", [128, 128], F32))
        onec = S(nc.sbuf_tensor("onec", [128, 1], F32))
        oner = S(nc.sbuf_tensor("oner", [1, 128], F32))
        sq = [S(nc.sbuf_tensor(f"sq{i}", [128, TT], F32)) for i in range(2)]
        mrow = S(nc.sbuf_tensor("mrow", [1, T], F32))
        rrow = S(nc.sbuf_tensor("rrow", [1, T], F32))
        t0 = S(nc.sbuf_tensor("t0", [1, TT], F32))
        t1 = S(nc.sbuf_tensor("t1", [1, TT], F32))
        vrow = S(nc.sbuf_tensor("vrow", [1, T], F32))
        lnt = S(nc.sbuf_tensor("lnt", [1, TT], F32))
        tmp8 = S(nc.sbuf_tensor("tmp8", [128, T], F32))
        red8 = S(nc.sbuf_tensor("red8", [128, DC], F32))
        rt = S(nc.sbuf_tensor("rt", [128, 1], F32))
        mx = S(nc.sbuf_tensor("mx", [128, 1], F32))
        sc = S(nc.sbuf_tensor("sc", [1, 1], F32))
        psA = S(nc.psum_tensor("psA", [128, TT], F32))
        psB = [S(nc.psum_tensor(f"psB{i}", [128, TT], F32)) for i in range(2)]
        psM = [S(nc.psum_tensor(f"psM{i}", [128, TT], F32)) for i in range(2)]
        psR = [S(nc.psum_tensor(f"psR{i}", [128, TT], F32)) for i in range(2)]
        ld = S(nc.semaphore("ld"))
        sa = S(nc.semaphore("sa"))
        sv = S(nc.semaphore("sv"))
        sp = S(nc.semaphore("sp"))
        so = S(nc.semaphore("so"))
        vf_s = S(nc.semaphore("vf"))
        blk = S(nc.Block())

        @blk.sync
        def _(sync):
            for d in range(DC):
                sync.dma_start(out=us[d][:], in_=u_t[d * 128:(d + 1) * 128, :]
                               ).then_inc(ld, 16)
            sync.dma_start(out=eye[:], in_=eye_t).then_inc(ld, 16)
            sync.wait_ge(sv, 2 + NT + 3 * NT + NT + 1)
            sync.dma_start(out=stats_t[0:1, :], in_=mrow[:]).then_inc(so, 16)
            sync.dma_start(out=stats_t[1:2, :], in_=rrow[:]).then_inc(so, 16)
            sync.dma_start(out=mloc_t, in_=sc[:]).then_inc(so, 16)
            sync.wait_ge(so, 48)

        SV_INIT = 2
        SV_A1 = SV_INIT + NT
        SV_A2 = SV_A1 + 3 * NT
        SV_B = SV_A2 + NT
        SV_FIN = SV_B + 1

        @blk.tensor
        def _(tensor):
            tensor.wait_ge(ld, 144)
            tensor.wait_ge(sv, 2)
            for t in range(NT):
                if t >= 1:
                    tensor.wait_ge(sv, SV_INIT + t)
                ts = slice(t * TT, (t + 1) * TT)
                for d in range(DC):
                    ins = tensor.matmul(out=psA[0:1, :], lhsT=onec[:],
                                        rhs=us[d][:, ts],
                                        start=(d == 0), stop=(d == DC - 1))
                ins.then_inc(sp, 1)
            nsq = 0
            for t in range(NT):
                for d in range(DC):
                    nsq += 1
                    tensor.wait_ge(sa, nsq)
                    if d == 0 and t >= 2:
                        tensor.wait_ge(sv, SV_A1 + 3 * (t - 1))
                    tensor.matmul(out=psB[t % 2][0:1, :], lhsT=onec[:],
                                  rhs=sq[nsq % 2][:], start=(d == 0),
                                  stop=(d == DC - 1)).then_inc(sp, 1)
            for t in range(NT):
                ts = slice(t * TT, (t + 1) * TT)
                tensor.wait_ge(sa, NT * DC + 2 * (t + 1))
                if t >= 2:
                    tensor.wait_ge(sv, SV_A2 + (t - 1))
                tensor.matmul(out=psM[t % 2][:], lhsT=oner[:], rhs=mrow[0:1, ts],
                              start=True, stop=True)
                tensor.matmul(out=psR[t % 2][:], lhsT=oner[:], rhs=rrow[0:1, ts],
                              start=True, stop=True).then_inc(sp, 1)
            tensor.wait_ge(sv, SV_B)
            tensor.transpose(out=psA[0:1, 0:128], in_=mx[:], identity=eye[:]
                             ).then_inc(sp, 1)

        @blk.scalar
        def _(scalar):
            scalar.wait_ge(ld, 144)
            n = 0
            for t in range(NT):
                ts = slice(t * TT, (t + 1) * TT)
                for d in range(DC):
                    n += 1
                    if n > 2:
                        scalar.wait_ge(sp, NT + n - 2)
                    scalar.activation(out=sq[n % 2][:], in_=us[d][:, ts],
                                      func=AF.Square).then_inc(sa, 1)
            for t in range(NT):
                ts = slice(t * TT, (t + 1) * TT)
                scalar.wait_ge(sv, 2 + NT + 3 * (t + 1))
                scalar.activation(out=lnt[:], in_=vrow[0:1, ts], func=AF.Ln
                                  ).then_inc(sa, 1)
                scalar.wait_ge(sa, NT * DC + 2 * t + 1)
                scalar.activation(out=rrow[0:1, ts], in_=lnt[:], func=AF.Exp,
                                  scale=-0.5).then_inc(sa, 1)

        @blk.vector
        def _(vector):
            vf = FE(vector, vf_s)
            vector.memset(onec[:], 1.0).then_inc(sv, 1)
            vector.memset(oner[:], 1.0)
            vector.memset(mx[:], 0.0).then_inc(sv, 1)
            vector.wait_ge(sv, SV_INIT)
            for t in range(NT):
                ts = slice(t * TT, (t + 1) * TT)
                vector.wait_ge(sp, t + 1)
                vector.tensor_scalar(out=mrow[0:1, ts], in0=psA[0:1, :],
                                     scalar1=INV_D, scalar2=None,
                                     op0=ALU.mult).then_inc(sv, 1)
            vector.wait_ge(sv, SV_A1)
            for t in range(NT):
                ts = slice(t * TT, (t + 1) * TT)
                vector.wait_ge(sp, NT + DC * (t + 1))
                vector.tensor_scalar(out=t0[:], in0=psB[t % 2][0:1, :],
                                     scalar1=INV_D, scalar2=None, op0=ALU.mult
                                     ).then_inc(sv, 1)
                vector.tensor_tensor(out=t1[:], in0=mrow[0:1, ts],
                                     in1=mrow[0:1, ts], op=ALU.mult).then_inc(sv, 1)
                vector.wait_ge(sv, SV_A1 + 3 * t + 2)
                vector.scalar_tensor_tensor(out=vrow[0:1, ts], in0=t0[:],
                                            scalar=EPS, in1=t1[:], op0=ALU.add,
                                            op1=ALU.subtract).then_inc(sv, 1)
                vector.wait_ge(sv, SV_A1 + 3 * t + 3)
            for t in range(NT):
                ts = slice(t * TT, (t + 1) * TT)
                vector.wait_ge(sp, NT + DC * NT + (t + 1))
                for d in range(DC):
                    ds = slice(d * TT, (d + 1) * TT)
                    ins = vector.tensor_sub(out=tmp8[:, ds], in0=us[d][:, ts],
                                            in1=psM[t % 2][:])
                vf(ins)
                for d in range(DC):
                    ds = slice(d * TT, (d + 1) * TT)
                    ins = vector.tensor_tensor(out=tmp8[:, ds], in0=tmp8[:, ds],
                                               in1=psR[t % 2][:], op=ALU.mult)
                vf(ins)
                for d in range(DC):
                    ds = slice(d * TT, (d + 1) * TT)
                    ins = vector.tensor_reduce(out=red8[:, d:d + 1], in_=tmp8[:, ds],
                                               axis=AX.X, op=ALU.max,
                                               apply_absolute_value=True)
                vf(ins)
                vf(vector.tensor_reduce(out=rt[:], in_=red8[:], axis=AX.X,
                                        op=ALU.max, apply_absolute_value=True))
                vector.tensor_max(out=mx[:], in0=mx[:], in1=rt[:]).then_inc(sv, 1)
                vector.wait_ge(sv, SV_A2 + t + 1)
            vector.wait_ge(sp, NT + DC * NT + NT + 1)
            vector.tensor_reduce(out=sc[:], in_=psA[0:1, 0:128], axis=AX.X,
                                 op=ALU.max, apply_absolute_value=True
                                 ).then_inc(sv, 1)

    return nc





def build_L2(prequant=False):
    nc = bass.Bass("TRN2", target_bir_lowering=False)
    u_t = nc.dram_tensor("u", [D, T], F32, kind="ExternalInput").ap()
    stats_t = nc.dram_tensor("stats", [2, T], F32, kind="ExternalInput").ap()
    mall_t = nc.dram_tensor("mall", [128, NCORE], F32, kind="ExternalInput").ap()
    if prequant:
        wq1i_t = nc.dram_tensor("wq1i", [D, H], BF16, kind="ExternalInput").ap()
        swi_t = nc.dram_tensor("swi", [128, 1], F32, kind="ExternalInput").ap()
    else:
        w1T_t = nc.dram_tensor("w1T", [D, H], F32, kind="ExternalInput").ap()
    b1c_t = nc.dram_tensor("b1c", [128, HC], F32, kind="ExternalInput").ap()
    eye_t = nc.dram_tensor("eye", [128, 128], F32, kind="ExternalInput").ap()
    h1_t = nc.dram_tensor("h1", [H, T], F32, kind="ExternalOutput").ap()
    if not prequant:
        wq1_t = nc.dram_tensor("wq1", [D, H], BF16, kind="ExternalOutput").ap()
        sw_t = nc.dram_tensor("sw", [1, 1], F32, kind="ExternalOutput").ap()
    mloc_t = nc.dram_tensor("mloc", [1, 1], F32, kind="ExternalOutput").ap()
    wq_src_t = wq1i_t if prequant else wq1_t

    NG = HC * NT
    NB = NG // 4
    with ExitStack() as st:
        S = st.enter_context
        mrow_t = S(nc.sbuf_tensor("mrow_t", [1, T], F32))
        rrow_t = S(nc.sbuf_tensor("rrow_t", [1, T], F32))
        mrow = mrow_t[0:1, :]
        rrow = rrow_t[0:1, :]
        ubuf = [S(nc.sbuf_tensor(f"ub{i}", [128, T], F32)) for i in range(2)]
        wbuf = [S(nc.sbuf_tensor(f"wb{i}", [128, H], F32)) for i in range(2)]
        xq = [S(nc.sbuf_tensor(f"xq{d}", [128, T], BF16)) for d in range(DC)]
        wqs = [S(nc.sbuf_tensor(f"wqs{i}", [128, DC * 128], BF16)) for i in range(2)]
        wqo = [S(nc.sbuf_tensor(f"wqo{i}", [128, H], BF16)) for i in range(2)]
        wint = S(nc.sbuf_tensor("wint", [128, H // 4], I32))
        xint = S(nc.sbuf_tensor("xint", [128, TT], I32))
        qf = S(nc.sbuf_tensor("qf", [128, TT], F32))
        red8 = S(nc.sbuf_tensor("red8", [128, DC], F32))
        rt2 = S(nc.sbuf_tensor("rt2", [128, 1], F32))
        eye = S(nc.sbuf_tensor("eye_sb", [128, 128], F32))
        oner = S(nc.sbuf_tensor("oner", [1, 128], F32))
        mallb = S(nc.sbuf_tensor("mallb", [128, NCORE], F32))
        b1c = S(nc.sbuf_tensor("b1c_sb", [128, HC], F32))
        b1s = S(nc.sbuf_tensor("b1s", [128, HC], F32))
        tmpq = S(nc.sbuf_tensor("tmpq", [128, T], F32))
        wmx = S(nc.sbuf_tensor("wmx", [128, 1], F32))
        gmx = S(nc.sbuf_tensor("gmx", [128, 1], F32))
        mxm = S(nc.sbuf_tensor("mxm", [128, 1], F32))
        sxb = S(nc.sbuf_tensor("sxb", [128, 1], F32))
        rxb = S(nc.sbuf_tensor("rxb", [128, 1], F32))
        s1b = S(nc.sbuf_tensor("s1b", [128, 1], F32))
        swb = S(nc.sbuf_tensor("swb", [128, 1], F32))
        rwb = S(nc.sbuf_tensor("rwb", [128, 1], F32))
        sc1 = S(nc.sbuf_tensor("sc1", [128, 1], F32))
        wsc = S(nc.sbuf_tensor("wsc", [1, 1], F32))
        sw_s = S(nc.sbuf_tensor("sw_s", [1, 1], F32))
        rw1 = S(nc.sbuf_tensor("rw1", [1, 1], F32))
        r1a = S(nc.sbuf_tensor("r1a", [1, 1], F32))
        srow = S(nc.sbuf_tensor("srow", [1, 2], F32))
        mgl = S(nc.sbuf_tensor("mgl", [1, 1], F32))
        pmm = [S(nc.psum_tensor(f"pmm{i}", [128, TT], F32)) for i in range(4)]
        psE = S(nc.psum_tensor("psE", [128, TT], F32))
        psF = S(nc.psum_tensor("psF", [128, TT], F32))
        paux = S(nc.psum_tensor("paux", [128, 128], F32))
        pbc = S(nc.psum_tensor("pbc", [128, 2], F32))
        lfix = S(nc.semaphore("lfix"))
        lu = [S(nc.semaphore(f"lu{i}")) for i in range(2)]
        gw = [S(nc.semaphore(f"gw{i}")) for i in range(2)]
        gq = [S(nc.semaphore(f"gq{i}")) for i in range(2)]
        sq = [S(nc.semaphore(f"sq{i}")) for i in range(2)]
        sh = [S(nc.semaphore(f"sh{i}")) for i in range(2)]
        sv = S(nc.semaphore("sv"))
        sp = S(nc.semaphore("sp"))
        sa = S(nc.semaphore("sa"))
        so = S(nc.semaphore("so"))
        vf_s = S(nc.semaphore("vf"))
        af_s = S(nc.semaphore("af"))
        blk = S(nc.Block())

        V_ONES = 1
        V_WCH = lambda d: 2 + d
        V_WMX = 10
        V_SROW = 11
        V_SCAL = 12
        V_WQ = lambda d: 13 + d
        V_XQ = lambda t: 21 + t
        V_G = lambda g: 29 + g
        V_GDONE = 29 + NG
        V_FIN = 30 + NG
        P_TR = 1
        P_BC = 2
        P_B = lambda t: 3 + t
        P_G = lambda g: 11 + g
        P_FINT = 11 + NG

        @blk.sync
        def _(sync):
            sync.dma_start(out=mrow_t[:], in_=stats_t[0:1, :]).then_inc(lfix, 16)
            sync.dma_start(out=rrow_t[:], in_=stats_t[1:2, :]).then_inc(lfix, 16)
            sync.dma_start(out=mallb[:], in_=mall_t).then_inc(lfix, 16)
            sync.dma_start(out=b1c[:], in_=b1c_t).then_inc(lfix, 16)
            sync.dma_start(out=eye[:], in_=eye_t).then_inc(lfix, 16)
            if prequant:
                sync.dma_start(out=swb[:], in_=swi_t).then_inc(lfix, 16)
            for t in range(NT):
                if t >= 2:
                    sync.wait_ge(sv, V_XQ(t - 2))
                sync.dma_start(
                    out=ubuf[t % 2][:].rearrange("p (c t) -> p c t", c=DC),
                    in_=u_t[:, t * TT:(t + 1) * TT].rearrange("(c p) t -> p c t", p=128),
                ).then_inc(lu[t % 2], 16)
            sync.wait_ge(sv, V_FIN)
            sync.dma_start(out=mloc_t, in_=mgl[:]).then_inc(so, 16)
            if not prequant:
                sync.dma_start(out=sw_t, in_=sw_s[:]).then_inc(so, 16)
                sync.wait_ge(so, 32)
                sync.wait_ge(sq[0], 16 * (DC // 2))
                sync.wait_ge(sq[1], 16 * (DC // 2))
            else:
                sync.wait_ge(so, 16)
            sync.wait_ge(sh[0], 16 * (NB // 2))
            sync.wait_ge(sh[1], 16 * (NB // 2))

        @blk.gpsimd
        def _(gpsimd):
            if not prequant:
                for d in range(DC):
                    if d >= 2:
                        gpsimd.wait_ge(sv, V_WCH(d - 2))
                    gpsimd.dma_start(out=wbuf[d % 2][:],
                                     in_=w1T_t[d * 128:(d + 1) * 128, :]
                                     ).then_inc(gw[d % 2], 16)
                gpsimd.wait_ge(sv, V_WCH(DC - 1))
                for d in range(DC):
                    if d >= 2:
                        gpsimd.wait_ge(sv, V_WQ(d - 2))
                    gpsimd.dma_start(out=wbuf[d % 2][:],
                                     in_=w1T_t[d * 128:(d + 1) * 128, :]
                                     ).then_inc(gw[d % 2], 16)
                gpsimd.wait_ge(sq[0], 16 * (DC // 2))
                gpsimd.wait_ge(sq[1], 16 * (DC // 2))
            for h in range(HC):
                if h >= 2:
                    gpsimd.wait_ge(sp, P_G((h - 1) * NT - 1))
                gpsimd.dma_start(
                    out=wqs[h % 2][:].rearrange("p (c m) -> p c m", c=DC),
                    in_=wq_src_t[:, h * 128:(h + 1) * 128].rearrange("(c p) m -> p c m", p=128),
                ).then_inc(gq[h % 2], 16)

        @blk.vector
        def _(vector):
            vf = FE(vector, vf_s)
            vector.memset(oner[:], 1.0)
            vector.memset(wmx[:], 0.0)
            vector.memset(gmx[:], 0.0).then_inc(sv, 1)
            vector.wait_ge(sv, V_ONES)
            vector.wait_ge(lfix, 16 * (6 if prequant else 5))
            vf(vector.tensor_reduce(out=mxm[:], in_=mallb[:], axis=AX.X, op=ALU.max,
                                    apply_absolute_value=True))
            vf(vector.tensor_scalar(out=sxb[:], in0=mxm[:], scalar1=MINR,
                                    scalar2=1.0 / QP, op0=ALU.max, op1=ALU.mult))
            recip_newton_f(vector, vf, rxb[:], sxb[:], sc1[:])
            vf(vector.tensor_scalar(out=rrow, in0=rrow, scalar1=rxb[0:1, 0:1],
                                    scalar2=None, op0=ALU.mult))
            if not prequant:
                for d in range(DC):
                    vector.wait_ge(gw[d % 2], 16 * (d // 2 + 1))
                    vector.tensor_reduce(out=red8[:, d:d + 1], in_=wbuf[d % 2][:],
                                         axis=AX.X, op=ALU.max,
                                         apply_absolute_value=True).then_inc(sv, 1)
                vector.wait_ge(sv, V_WCH(DC - 1))
                vector.tensor_reduce(out=wmx[:], in_=red8[:], axis=AX.X, op=ALU.max,
                                     apply_absolute_value=True).then_inc(sv, 1)
                vector.wait_ge(sv, V_WMX)
                vector.wait_ge(sp, P_TR)
                vf(vector.tensor_reduce(out=wsc[:], in_=paux[0:1, 0:128], axis=AX.X,
                                        op=ALU.max, apply_absolute_value=True))
                vf(vector.tensor_scalar(out=sw_s[:], in0=wsc[:], scalar1=MINR,
                                        scalar2=1.0 / QP, op0=ALU.max, op1=ALU.mult))
            else:
                for _ in range(DC + 1):
                    vector.nop().then_inc(sv, 1)
                vector.wait_ge(sv, V_WMX)
            if prequant:
                recip_newton_f(vector, vf, rwb[:], swb[:], sc1[:])
                vector.nop().then_inc(sv, 1)
                vf(vector.tensor_tensor(out=s1b[:], in0=swb[:], in1=sxb[:],
                                        op=ALU.mult))
                vector.tensor_scalar(out=b1s[:], in0=b1c[:], scalar1=s1b[:, 0:1],
                                     scalar2=None, op0=ALU.mult).then_inc(sv, 1)
            else:
                recip_newton_f(vector, vf, rw1[:], sw_s[:], r1a[:])
                vector.tensor_copy(out=srow[0:1, 0:1], in_=sw_s[:])
                vector.tensor_copy(out=srow[0:1, 1:2], in_=rw1[:]).then_inc(sv, 1)
                vector.wait_ge(sp, P_BC)
                vector.tensor_copy(out=swb[:], in_=pbc[:, 0:1])
                vf(vector.tensor_copy(out=rwb[:], in_=pbc[:, 1:2]))
                vf(vector.tensor_tensor(out=s1b[:], in0=swb[:], in1=sxb[:],
                                        op=ALU.mult))
                vector.tensor_scalar(out=b1s[:], in0=b1c[:], scalar1=s1b[:, 0:1],
                                     scalar2=None, op0=ALU.mult).then_inc(sv, 1)
            for d in range(DC):
                if prequant:
                    vector.nop().then_inc(sv, 1)
                    continue
                vector.wait_ge(gw[d % 2], 16 * (4 + d // 2 + 1))
                if d >= 2:
                    vector.wait_ge(sq[d % 2], 16 * (d // 2))
                for k in range(4):
                    hs = slice(k * (H // 4), (k + 1) * (H // 4))
                    wf = tmpq[:, 0:H // 4]
                    vf(vector.tensor_scalar(out=wf, in0=wbuf[d % 2][:, hs],
                                            scalar1=rwb[:, 0:1], scalar2=MAGIC,
                                            op0=ALU.mult, op1=ALU.add))
                    vf(vector.tensor_scalar(out=wint[:], in0=wf, scalar1=MAGIC,
                                            scalar2=None, op0=ALU.subtract))
                    ins = vector.tensor_copy(out=wqo[d % 2][:, hs], in_=wint[:])
                    if k < 3:
                        vf(ins)
                ins.then_inc(sv, 1)
                vector.wait_ge(sv, V_WQ(d))
            for t in range(NT):
                ts_ = slice(t * TT, (t + 1) * TT)
                vector.wait_ge(lu[t % 2], 16 * (t // 2 + 1))
                vector.wait_ge(sp, P_B(t))
                for d in range(DC):
                    ds = slice(d * TT, (d + 1) * TT)
                    ins = vector.tensor_sub(out=tmpq[:, ds], in0=ubuf[t % 2][:, ds],
                                            in1=psE[:])
                vf(ins)
                for d in range(DC):
                    ds = slice(d * TT, (d + 1) * TT)
                    vf(vector.tensor_tensor(out=qf[:], in0=tmpq[:, ds],
                                            in1=psF[:],
                                            op=ALU.mult))
                    vf(vector.tensor_scalar(out=xint[:], in0=qf[:], scalar1=MAGIC,
                                            scalar2=MAGIC, op0=ALU.add,
                                            op1=ALU.subtract))
                    ins = vector.tensor_copy(out=xq[d][:, ts_], in_=xint[:])
                    if d < DC - 1:
                        vf(ins)
                ins.then_inc(sv, 1)
                vector.wait_ge(sv, V_XQ(t))
            for g in range(NG):
                vector.wait_ge(sa, g + 1)
                stg = wbuf[(g // 4) % 2]
                col = (g % 4) * TT
                vector.tensor_reduce(out=red8[:, g % 8:g % 8 + 1],
                                     in_=stg[:, col:col + TT], axis=AX.X,
                                     op=ALU.max, apply_absolute_value=True
                                     ).then_inc(sv, 1)
                if g % 8 == 7:
                    vector.wait_ge(sv, V_G(g))
                    vf(vector.tensor_reduce(out=rt2[:], in_=red8[:], axis=AX.X,
                                            op=ALU.max, apply_absolute_value=True))
                    if g == NG - 1:
                        vector.tensor_max(out=gmx[:], in0=gmx[:], in1=rt2[:]
                                          ).then_inc(sv, 1)
                    else:
                        vf(vector.tensor_max(out=gmx[:], in0=gmx[:], in1=rt2[:]))
            vector.wait_ge(sp, P_FINT)
            vector.tensor_reduce(out=mgl[:], in_=paux[0:1, 0:128], axis=AX.X,
                                 op=ALU.max, apply_absolute_value=True).then_inc(sv, 1)

        @blk.tensor
        def _(tensor):
            tensor.wait_ge(lfix, 16 * (6 if prequant else 5))
            if not prequant:
                tensor.wait_ge(sv, V_WMX)
                tensor.transpose(out=paux[0:1, 0:128], in_=wmx[:], identity=eye[:]
                                 ).then_inc(sp, 1)
            else:
                tensor.nop().then_inc(sp, 1)
            if prequant:
                tensor.nop().then_inc(sp, 1)
            else:
                tensor.wait_ge(sv, V_SROW)
                tensor.matmul(out=pbc[:], lhsT=oner[:], rhs=srow[:], start=True,
                              stop=True).then_inc(sp, 1)
            for t in range(NT):
                ts_ = slice(t * TT, (t + 1) * TT)
                if t >= 1:
                    tensor.wait_ge(sv, V_XQ(t - 1))
                tensor.matmul(out=psE[:], lhsT=oner[:],
                              rhs=mrow[:, ts_], start=True, stop=True)
                tensor.matmul(out=psF[:], lhsT=oner[:],
                              rhs=rrow[:, ts_], start=True, stop=True
                              ).then_inc(sp, 1)
            g = 0
            for h in range(HC):
                tensor.wait_ge(gq[h % 2], 16 * (h // 2 + 1))
                for t in range(NT):
                    ts_ = slice(t * TT, (t + 1) * TT)
                    if h == 0:
                        tensor.wait_ge(sv, V_XQ(t))
                    if g >= 4:
                        tensor.wait_ge(sa, g - 3)
                    for d in range(DC):
                        ins = tensor.matmul(out=pmm[g % 4][:],
                                            lhsT=wqs[h % 2][:, d * 128:(d + 1) * 128],
                                            rhs=xq[d][:, ts_], start=(d == 0),
                                            stop=(d == DC - 1))
                    ins.then_inc(sp, 1)
                    g += 1
            tensor.wait_ge(sv, V_GDONE)
            tensor.transpose(out=paux[0:1, 0:128], in_=gmx[:], identity=eye[:]
                             ).then_inc(sp, 1)

        @blk.scalar
        def _(scalar):
            if not prequant:
                for d in range(DC):
                    scalar.wait_ge(sv, V_WQ(d))
                    scalar.dma_start(out=wq1_t[d * 128:(d + 1) * 128, :],
                                     in_=wqo[d % 2][:]).then_inc(sq[d % 2], 16)
            scalar.wait_ge(sv, V_SCAL)
            scalar.wait_ge(sv, V_WQ(DC - 1))
            g = 0
            for h in range(HC):
                for t in range(NT):
                    b = g // 4
                    stg = wbuf[b % 2]
                    col = (g % 4) * TT
                    if g % 4 == 0 and b >= 2:
                        scalar.wait_ge(sh[b % 2], 16 * (b // 2))
                        scalar.wait_ge(sv, V_G(4 * (b - 1) - 1))
                    scalar.wait_ge(sp, P_G(g))
                    scalar.activation(out=stg[:, col:col + TT],
                                      in_=pmm[g % 4][:], func=AF.Gelu,
                                      bias=b1s[:, h:h + 1],
                                      scale=s1b[:, 0:1]).then_inc(sa, 1)
                    if g % 4 == 3:
                        scalar.wait_ge(sa, g + 1)
                        th = (t - 3) // 4
                        scalar.dma_start(
                            out=h1_t[h * 128:(h + 1) * 128,
                                     th * 2048:(th + 1) * 2048],
                            in_=stg[:, 0:2048]).then_inc(sh[b % 2], 16)
                    g += 1

    return nc





def build_L3a(prequant=False):
    nc = bass.Bass("TRN2", target_bir_lowering=False)
    x_t = nc.dram_tensor("x", [D, T], F32, kind="ExternalInput").ap()
    h1_t = nc.dram_tensor("h1", [H, T], F32, kind="ExternalInput").ap()
    mall_t = nc.dram_tensor("mall", [128, NCORE], F32, kind="ExternalInput").ap()
    if prequant:
        wq2i_t = nc.dram_tensor("wq2i", [H, D], BF16, kind="ExternalInput").ap()
        swi_t = nc.dram_tensor("swi", [128, 1], F32, kind="ExternalInput").ap()
    else:
        w2T_t = nc.dram_tensor("w2T", [H, D], F32, kind="ExternalInput").ap()
    b2c_t = nc.dram_tensor("b2c", [128, DC], F32, kind="ExternalInput").ap()
    eye_t = nc.dram_tensor("eye", [128, 128], F32, kind="ExternalInput").ap()
    xo_t = nc.dram_tensor("xo", [D, T], F32, kind="ExternalOutput").ap()
    if not prequant:
        wq2_t = nc.dram_tensor("wq2", [H, D], BF16, kind="ExternalOutput").ap()
        sw_t = nc.dram_tensor("sw", [1, 1], F32, kind="ExternalOutput").ap()

    NW = HC
    NQ = 4
    with ExitStack() as st:
        S = st.enter_context
        wbuf = [S(nc.sbuf_tensor(f"wb{i}", [128, D], F32)) for i in range(2)]
        wq2g = [S(nc.sbuf_tensor(f"wq2g{i}", [128, 4 * D], BF16)) for i in range(NW // 4)]
        wq2r = [wq2g[h // 4][:, (h % 4) * D:(h % 4 + 1) * D] for h in range(NW)]
        hbuf = [S(nc.sbuf_tensor(f"hb{i}", [128, DC * TT], F32)) for i in range(2)]
        hq = [S(nc.sbuf_tensor(f"hq{i}", [128, DC * TT], BF16)) for i in range(2)]
        ubuf = [S(nc.sbuf_tensor(f"ub{i}", [128, T], F32)) for i in range(2)]
        xo = [S(nc.sbuf_tensor(f"xo_sb{i}", [128, T], F32)) for i in range(2)]
        wint = S(nc.sbuf_tensor("wint", [128, D], I32))
        wf = S(nc.sbuf_tensor("wf", [128, D], F32))
        xint = S(nc.sbuf_tensor("xint", [128, TT], I32))
        qf = S(nc.sbuf_tensor("qf", [128, TT], F32))
        eye = S(nc.sbuf_tensor("eye_sb", [128, 128], F32))
        oner = S(nc.sbuf_tensor("oner", [1, 128], F32))
        mallb = S(nc.sbuf_tensor("mallb", [128, NCORE], F32))
        b2c = S(nc.sbuf_tensor("b2c_sb", [128, DC], F32))
        b2s = S(nc.sbuf_tensor("b2s", [128, DC], F32))
        red8 = S(nc.sbuf_tensor("red8", [128, NW], F32))
        wmx = S(nc.sbuf_tensor("wmx", [128, 1], F32))
        mxm = S(nc.sbuf_tensor("mxm", [128, 1], F32))
        sxb = S(nc.sbuf_tensor("sxb", [128, 1], F32))
        rxb = S(nc.sbuf_tensor("rxb", [128, 1], F32))
        s5b = S(nc.sbuf_tensor("s5b", [128, 1], F32))
        swb = S(nc.sbuf_tensor("swb", [128, 1], F32))
        rwb = S(nc.sbuf_tensor("rwb", [128, 1], F32))
        sc1 = S(nc.sbuf_tensor("sc1", [128, 1], F32))
        wsc = S(nc.sbuf_tensor("wsc", [1, 1], F32))
        sw_s = S(nc.sbuf_tensor("sw_s", [1, 1], F32))
        rw1 = S(nc.sbuf_tensor("rw1", [1, 1], F32))
        r1a = S(nc.sbuf_tensor("r1a", [1, 1], F32))
        srow = S(nc.sbuf_tensor("srow", [1, 2], F32))
        pmm = [S(nc.psum_tensor(f"pmm{i}", [128, TT], F32)) for i in range(DC)]
        lfix = S(nc.semaphore("lfix"))
        lu = [S(nc.semaphore(f"lu{i}")) for i in range(2)]
        lh = [S(nc.semaphore(f"lh{i}")) for i in range(2)]
        gw = [S(nc.semaphore(f"gw{i}")) for i in range(2)]
        sx_ = [S(nc.semaphore(f"sx{i}")) for i in range(2)]
        swq = [S(nc.semaphore(f"swq{i}")) for i in range(2)]
        so_ = S(nc.semaphore("so_"))
        sv = S(nc.semaphore("sv"))
        sp = S(nc.semaphore("sp"))
        sa = S(nc.semaphore("sa"))
        vf_s = S(nc.semaphore("vf"))
        blk = S(nc.Block())

        V_ONES = 1
        V_WCH = lambda h: 2 + h
        V_WMX = 2 + NW
        V_SROW = V_WMX + 1
        V_SCAL = V_SROW + 1
        V_WQ = lambda h: V_SCAL + 1 + h
        _pos = {}
        _ctr = V_WQ(NW - 1)
        for _t in range(NT):
            for _q in range(NQ):
                _ctr += 1
                _pos[("q", _t, _q)] = _ctr
            if _t >= 1:
                _ctr += 1
                _pos[("a", _t - 1)] = _ctr
        _ctr += 1
        _pos[("a", NT - 1)] = _ctr
        V_HQ = lambda t, q: _pos[("q", t, q)]
        V_XO = lambda t: _pos[("a", t)]
        V_FIN = V_XO(NT - 1)
        V_SWST = V_SROW
        P_TR = 1
        P_BC = 2
        P_MM = lambda t, q: 3 + t * NQ + q
        P_T = lambda t: P_MM(t, NQ - 1)

        @blk.sync
        def _(sync):
            sync.dma_start(out=mallb[:], in_=mall_t).then_inc(lfix, 16)
            sync.dma_start(out=b2c[:], in_=b2c_t).then_inc(lfix, 16)
            sync.dma_start(out=eye[:], in_=eye_t).then_inc(lfix, 16)
            if prequant:
                sync.dma_start(out=swb[:], in_=swi_t).then_inc(lfix, 16)
            for t in range(NT):
                if t >= 2:
                    sync.wait_ge(sv, V_XO(t - 2))
                sync.dma_start(
                    out=ubuf[t % 2][:].rearrange("p (c t) -> p c t", c=DC),
                    in_=x_t[:, t * TT:(t + 1) * TT].rearrange("(c p) t -> p c t", p=128),
                ).then_inc(lu[t % 2], 16)
                for j in (4 * t + 1, 4 * t + 3):
                    tq, qq = divmod(j, NQ)
                    if j >= 2:
                        sync.wait_ge(sv, V_HQ(*divmod(j - 2, NQ)))
                    sync.dma_start(
                        out=hbuf[1][:].rearrange("p (c t) -> p c t", c=DC),
                        in_=h1_t[qq * (H // NQ):(qq + 1) * (H // NQ),
                                 tq * TT:(tq + 1) * TT].rearrange("(c p) t -> p c t", p=128),
                    ).then_inc(lh[1], 16)
            sync.wait_ge(sv, V_FIN)
            if not prequant:
                sync.wait_ge(sv, V_SWST)
                sync.dma_start(out=sw_t, in_=sw_s[:]).then_inc(so_, 16)
                sync.wait_ge(so_, 16)
                sync.wait_ge(swq[0], 16 * (NW // 2))
                sync.wait_ge(swq[1], 16 * (NW // 2))
            sync.wait_ge(sx_[0], 16 * (NT // 2))
            sync.wait_ge(sx_[1], 16 * (NT // 2))

        @blk.gpsimd
        def _(gpsimd):
            if prequant:
                for gidx in range(NW // 4):
                    gpsimd.dma_start(
                        out=wq2g[gidx][:].rearrange("p (c m) -> p c m", c=4),
                        in_=wq2i_t[gidx * 512:(gidx + 1) * 512, :].rearrange(
                            "(c p) m -> p c m", p=128),
                    ).then_inc(gw[gidx % 2], 16)
            else:
                for h in range(NW):
                    if h >= 2:
                        gpsimd.wait_ge(sv, V_WCH(h - 2))
                    gpsimd.dma_start(out=wbuf[h % 2][:],
                                     in_=w2T_t[h * 128:(h + 1) * 128, :]
                                     ).then_inc(gw[h % 2], 16)
                gpsimd.wait_ge(sv, V_WCH(NW - 1))
                for h in range(NW):
                    if h >= 2:
                        gpsimd.wait_ge(sv, V_WQ(h - 2))
                    gpsimd.dma_start(out=wbuf[h % 2][:],
                                     in_=w2T_t[h * 128:(h + 1) * 128, :]
                                     ).then_inc(gw[h % 2], 16)
            for t in range(NT):
                for q in range(NQ):
                    j = t * NQ + q
                    if j % 2 != 0:
                        continue
                    if j >= 2:
                        gpsimd.wait_ge(sv, V_HQ(*divmod(j - 2, NQ)))
                    gpsimd.dma_start(
                        out=hbuf[0][:].rearrange("p (c t) -> p c t", c=DC),
                        in_=h1_t[q * (H // NQ):(q + 1) * (H // NQ),
                                 t * TT:(t + 1) * TT].rearrange("(c p) t -> p c t", p=128),
                    ).then_inc(lh[0], 16)

        @blk.vector
        def _(vector):
            vf = FE(vector, vf_s)
            vector.memset(oner[:], 1.0)
            vector.memset(wmx[:], 0.0).then_inc(sv, 1)
            vector.wait_ge(sv, V_ONES)
            vector.wait_ge(lfix, 16 * (4 if prequant else 3))
            vf(vector.tensor_reduce(out=mxm[:], in_=mallb[:], axis=AX.X, op=ALU.max,
                                    apply_absolute_value=True))
            vf(vector.tensor_scalar(out=sxb[:], in0=mxm[:], scalar1=MINR,
                                    scalar2=1.0 / QP, op0=ALU.max, op1=ALU.mult))
            recip_newton_f(vector, vf, rxb[:], sxb[:], sc1[:])
            if not prequant:
                for h in range(NW):
                    vector.wait_ge(gw[h % 2], 16 * (h // 2 + 1))
                    vector.tensor_reduce(out=red8[:, h:h + 1], in_=wbuf[h % 2][:],
                                         axis=AX.X, op=ALU.max,
                                         apply_absolute_value=True).then_inc(sv, 1)
                vector.wait_ge(sv, V_WCH(NW - 1))
                vector.tensor_reduce(out=wmx[:], in_=red8[:], axis=AX.X, op=ALU.max,
                                     apply_absolute_value=True).then_inc(sv, 1)
                vector.wait_ge(sv, V_WMX)
                vector.wait_ge(sp, P_TR)
                vf(vector.tensor_reduce(out=wsc[:], in_=pmm[0][0:1, 0:128],
                                        axis=AX.X, op=ALU.max,
                                        apply_absolute_value=True))
                vf(vector.tensor_scalar(out=sw_s[:], in0=wsc[:], scalar1=MINR,
                                        scalar2=1.0 / QP, op0=ALU.max, op1=ALU.mult))
            else:
                for _ in range(NW + 1):
                    vector.nop().then_inc(sv, 1)
                vector.wait_ge(sv, V_WMX)
            if prequant:
                recip_newton_f(vector, vf, rwb[:], swb[:], sc1[:])
                vector.nop().then_inc(sv, 1)
            else:
                recip_newton_f(vector, vf, rw1[:], sw_s[:], r1a[:])
                vector.tensor_copy(out=srow[0:1, 0:1], in_=sw_s[:])
                vector.tensor_copy(out=srow[0:1, 1:2], in_=rw1[:]).then_inc(sv, 1)
                vector.wait_ge(sp, P_BC)
                vector.tensor_copy(out=swb[:], in_=pmm[0][:, 0:1])
                vf(vector.tensor_copy(out=rwb[:], in_=pmm[0][:, 1:2]))
            vf(vector.tensor_tensor(out=s5b[:], in0=swb[:], in1=sxb[:], op=ALU.mult))
            vf(vector.tensor_scalar(out=s5b[:], in0=s5b[:], scalar1=0.5,
                                    scalar2=None, op0=ALU.mult))
            vector.tensor_scalar(out=b2s[:], in0=b2c[:], scalar1=s5b[:, 0:1],
                                 scalar2=None, op0=ALU.mult).then_inc(sv, 1)
            for h in range(NW):
                if prequant:
                    gidx = h // 4
                    vector.wait_ge(gw[gidx % 2], 16 * (gidx // 2 + 1))
                    vector.nop().then_inc(sv, 1)
                    continue
                vector.wait_ge(gw[h % 2], 16 * (NW // 2 + h // 2 + 1))
                vf(vector.tensor_scalar(out=wf[:], in0=wbuf[h % 2][:],
                                        scalar1=rwb[:, 0:1], scalar2=MAGIC,
                                        op0=ALU.mult, op1=ALU.add))
                vf(vector.tensor_scalar(out=wint[:], in0=wf[:], scalar1=MAGIC,
                                        scalar2=None, op0=ALU.subtract))
                vector.tensor_copy(out=wq2r[h], in_=wint[:]).then_inc(sv, 1)
                vector.wait_ge(sv, V_WQ(h))
            def residual_add(t):
                vector.wait_ge(lu[t % 2], 16 * (t // 2 + 1))
                vector.wait_ge(sa, DC * (t + 1))
                for d in range(DC):
                    ds = slice(d * TT, (d + 1) * TT)
                    ins = vector.tensor_add(out=xo[t % 2][:, ds],
                                            in0=xo[t % 2][:, ds],
                                            in1=ubuf[t % 2][:, ds])
                ins.then_inc(sv, 1)
                vector.wait_ge(sv, V_XO(t))

            for t in range(NT):
                for q in range(NQ):
                    j = t * NQ + q
                    i = j % 2
                    vector.wait_ge(lh[i], 16 * (j // 2 + 1))
                    if j >= 2:
                        vector.wait_ge(sp, P_MM(*divmod(j - 2, NQ)))
                    for c in range(DC):
                        cs = slice(c * TT, (c + 1) * TT)
                        vf(vector.tensor_scalar(out=qf[:], in0=hbuf[i][:, cs],
                                                scalar1=rxb[:, 0:1], scalar2=MAGIC,
                                                op0=ALU.mult, op1=ALU.add))
                        vf(vector.tensor_scalar(out=xint[:], in0=qf[:], scalar1=MAGIC,
                                                scalar2=None, op0=ALU.subtract))
                        ins = vector.tensor_copy(out=hq[i][:, cs], in_=xint[:])
                        if c < DC - 1:
                            vf(ins)
                    ins.then_inc(sv, 1)
                    vector.wait_ge(sv, V_HQ(t, q))
                if t >= 1:
                    residual_add(t - 1)
            residual_add(NT - 1)

        @blk.tensor
        def _(tensor):
            tensor.wait_ge(lfix, 16 * (4 if prequant else 3))
            if not prequant:
                tensor.wait_ge(sv, V_WMX)
                tensor.transpose(out=pmm[0][0:1, 0:128], in_=wmx[:],
                                 identity=eye[:]).then_inc(sp, 1)
            else:
                tensor.nop().then_inc(sp, 1)
            if prequant:
                tensor.nop().then_inc(sp, 1)
            else:
                tensor.wait_ge(sv, V_SROW)
                tensor.matmul(out=pmm[0][:, 0:2], lhsT=oner[:], rhs=srow[:],
                              start=True, stop=True).then_inc(sp, 1)
            for t in range(NT):
                for q in range(NQ):
                    i = (t * NQ + q) % 2
                    if t == 0:
                        tensor.wait_ge(sv, V_WQ(q * DC + DC - 1))
                    tensor.wait_ge(sv, V_HQ(t, q))
                    if q == 0 and t >= 1:
                        tensor.wait_ge(sa, DC * t)
                    for hh in range(DC):
                        h = q * DC + hh
                        for d in range(DC):
                            ins = tensor.matmul(
                                out=pmm[d][:],
                                lhsT=wq2r[h][:, d * 128:(d + 1) * 128],
                                rhs=hq[i][:, hh * TT:(hh + 1) * TT],
                                start=(h == 0), stop=(h == NW - 1))
                    ins.then_inc(sp, 1)

        @blk.scalar
        def _(scalar):
            if not prequant:
                for h in range(NW):
                    scalar.wait_ge(sv, V_WQ(h))
                    scalar.dma_start(out=wq2_t[h * 128:(h + 1) * 128, :],
                                     in_=wq2r[h]).then_inc(swq[h % 2], 16)
            scalar.wait_ge(sv, V_SCAL)
            for t in range(NT):
                scalar.wait_ge(sp, P_T(t))
                if t >= 2:
                    scalar.wait_ge(sx_[t % 2], 16 * (t // 2))
                for d in range(DC):
                    ds = slice(d * TT, (d + 1) * TT)
                    scalar.activation(out=xo[t % 2][:, ds], in_=pmm[d][:],
                                      func=AF.Identity, bias=b2s[:, d:d + 1],
                                      scale=s5b[:, 0:1]).then_inc(sa, 1)
                scalar.wait_ge(sv, V_XO(t))
                scalar.dma_start(
                    out=xo_t[:, t * TT:(t + 1) * TT].rearrange("(c p) t -> p c t", p=128),
                    in_=xo[t % 2][:].rearrange("p (c t) -> p c t", c=DC),
                ).then_inc(sx_[t % 2], 16)

    return nc






def build_L3b():
    nc = bass.Bass("TRN2", target_bir_lowering=False)
    x_t = nc.dram_tensor("x", [D, T], F32, kind="ExternalInput").ap()
    waT_t = nc.dram_tensor("waT", [D, D], F32, kind="ExternalInput").ap()
    wbT_t = nc.dram_tensor("wbT", [D, D], F32, kind="ExternalInput").ap()
    bac_t = nc.dram_tensor("bac", [128, DC], F32, kind="ExternalInput").ap()
    bbc_t = nc.dram_tensor("bbc", [128, DC], F32, kind="ExternalInput").ap()
    stats_t = nc.dram_tensor("stats", [2, T], F32, kind="ExternalOutput").ap()
    hs0_t = nc.dram_tensor("hs0", [D, T], F32, kind="ExternalOutput").ap()
    P_t = nc.dram_tensor("P", [D, T], F32, kind="ExternalOutput").ap()
    hend_t = nc.dram_tensor("hend", [128, DC], F32, kind="ExternalOutput").ap()

    with ExitStack() as st:
        S = st.enter_context
        xbuf = [S(nc.sbuf_tensor(f"xbuf{i}", [128, T], F32)) for i in range(2)]
        sq = [S(nc.sbuf_tensor(f"sq{i}", [128, TT], F32)) for i in range(2)]
        xh = [S(nc.sbuf_tensor(f"xh{d}", [128, T], BF16)) for d in range(DC)]
        wab = [S(nc.sbuf_tensor(f"wab{d}", [128, D], BF16)) for d in range(DC)]
        wbb = [S(nc.sbuf_tensor(f"wbb{d}", [128, D], BF16)) for d in range(DC)]
        wld = [S(nc.sbuf_tensor(f"wld{i}", [128, D], F32)) for i in range(2)]
        alpha = S(nc.sbuf_tensor("alpha", [128, T], F32))
        bu = S(nc.sbuf_tensor("bu", [128, T], F32))
        hs0 = S(nc.sbuf_tensor("hs0_sb", [128, T], F32))
        Pb = xbuf[1]
        bac = S(nc.sbuf_tensor("bac_sb", [128, DC], F32))
        bbc = S(nc.sbuf_tensor("bbc_sb", [128, DC], F32))
        hend = S(nc.sbuf_tensor("hend_sb", [128, DC], F32))
        onec = S(nc.sbuf_tensor("onec", [128, 1], F32))
        oner = S(nc.sbuf_tensor("oner", [1, 128], F32))
        t0 = S(nc.sbuf_tensor("t0", [1, TT], F32))
        t1 = S(nc.sbuf_tensor("t1", [1, TT], F32))
        vr = S(nc.sbuf_tensor("vr", [1, TT], F32))
        lnt = S(nc.sbuf_tensor("lnt", [1, TT], F32))
        mr = S(nc.sbuf_tensor("mr", [1, TT], F32))
        rr = S(nc.sbuf_tensor("rr", [1, TT], F32))
        tmpq = S(nc.sbuf_tensor("tmpq", [128, TT], F32))
        psS = [S(nc.psum_tensor(f"psS{i}", [128, TT], F32)) for i in range(2)]
        psM = [S(nc.psum_tensor(f"psM{i}", [128, TT], F32)) for i in range(2)]
        psA = [S(nc.psum_tensor(f"psA{i}", [128, TT], F32)) for i in range(2)]
        psB = [S(nc.psum_tensor(f"psB{i}", [128, TT], F32)) for i in range(2)]
        pA = [psA[0], psA[1], psS[0], psS[1]]
        pB = [psB[0], psB[1], psM[0], psM[1]]
        lfix = S(nc.semaphore("lfix"))
        lx2 = [S(nc.semaphore(f"lx{i}")) for i in range(2)]
        gw = [S(nc.semaphore(f"gw{i}")) for i in range(2)]
        sh = [S(nc.semaphore(f"sh{i}")) for i in range(2)]
        srl = S(nc.semaphore("srl"))
        she = S(nc.semaphore("she"))
        sv = S(nc.semaphore("sv"))
        sp = S(nc.semaphore("sp"))
        sa = S(nc.semaphore("sa"))
        vf_s = S(nc.semaphore("vf"))
        af_s = S(nc.semaphore("af"))
        blk = S(nc.Block())

        V_INIT = 1
        V_WB = lambda k: 2 + k
        V_ROW = lambda t: 18 + 3 * t
        V_VAR = lambda t: 19 + 3 * t
        V_XH = lambda t: 20 + 3 * t
        V_PHA = V_XH(NT - 1)
        V_Z = V_PHA + 1
        V_SC = lambda c: V_Z + 1 + 3 * c + 0
        V_PS = lambda c: V_Z + 2 + 3 * c
        V_HE = lambda c: V_Z + 3 + 3 * c
        V_FIN = V_HE(DC - 1)
        A_SQ = lambda t, d: 9 * t + d + 1
        A_RST = lambda t: 9 * t + 9
        A_EV = lambda c, tt: 9 * NT + 2 * (c * NT + tt) + 2
        P_S1 = lambda t: 10 * t + 1
        P_SQ = lambda t, d: 10 * t + 2 + d
        P_S2 = lambda t: 10 * t + 9
        P_BC = lambda t: 10 * t + 10
        P_AB = lambda c, tt: 10 * NT + c * NT + tt + 1

        @blk.sync
        def _(sync):
            sync.dma_start(out=bac[:], in_=bac_t).then_inc(lfix, 16)
            sync.dma_start(out=bbc[:], in_=bbc_t).then_inc(lfix, 16)
            for t in range(NT):
                if t >= 2:
                    sync.wait_ge(sv, V_XH(t - 2))
                sync.dma_start(
                    out=xbuf[t % 2][:].rearrange("p (c t) -> p c t", c=DC),
                    in_=x_t[:, t * TT:(t + 1) * TT].rearrange("(c p) t -> p c t", p=128),
                ).then_inc(lx2[t % 2], 16)
            sync.wait_ge(sv, V_FIN)
            sync.dma_start(out=hend_t, in_=hend[:]).then_inc(she, 16)
            sync.wait_ge(she, 16)
            sync.wait_ge(srl, 16 * 2 * NT)
            sync.wait_ge(sh[0], 16 * DC)
            sync.wait_ge(sh[1], 16 * DC)

        @blk.gpsimd
        def _(gpsimd):
            for k in range(2 * DC):
                if k >= 2:
                    gpsimd.wait_ge(sv, V_WB(k - 2))
                src_t = waT_t if k < DC else wbT_t
                d = k % DC
                gpsimd.dma_start(out=wld[k % 2][:],
                                 in_=src_t[d * 128:(d + 1) * 128, :]
                                 ).then_inc(gw[k % 2], 16)

        @blk.vector
        def _(vector):
            vf = FE(vector, vf_s)
            vector.memset(onec[:], 1.0)
            vector.memset(oner[:], 1.0).then_inc(sv, 1)
            vector.wait_ge(sv, V_INIT)
            for k in range(2 * DC):
                vector.wait_ge(gw[k % 2], 16 * (k // 2 + 1))
                dst = wab[k] if k < DC else wbb[k - DC]
                vector.tensor_copy(out=dst[:], in_=wld[k % 2][:]).then_inc(sv, 1)
            for t in range(NT):
                ts_ = slice(t * TT, (t + 1) * TT)
                vector.wait_ge(sp, P_S1(t))
                if t >= 1:
                    vector.wait_ge(srl, 32 * t)
                vf(vector.tensor_scalar(out=mr[:], in0=psS[t % 2][0:1, :],
                                        scalar1=INV_D, scalar2=None, op0=ALU.mult))
                vector.wait_ge(sp, P_S2(t))
                vector.tensor_scalar(out=t0[:], in0=psM[t % 2][0:1, :],
                                     scalar1=INV_D, scalar2=None, op0=ALU.mult
                                     ).then_inc(sv, 1)
                vf(vector.tensor_tensor(out=t1[:], in0=mr[:], in1=mr[:],
                                        op=ALU.mult))
                vector.scalar_tensor_tensor(out=vr[:], in0=t0[:], scalar=EPS,
                                            in1=t1[:], op0=ALU.add,
                                            op1=ALU.subtract).then_inc(sv, 1)
                vector.wait_ge(sv, V_VAR(t))
                vector.wait_ge(sp, P_BC(t))
                for d in range(DC):
                    ds = slice(d * TT, (d + 1) * TT)
                    vf(vector.tensor_sub(out=tmpq[:], in0=xbuf[t % 2][:, ds],
                                         in1=psA[t % 2][:]))
                    ins = vector.tensor_tensor(out=xh[d][:, ts_], in0=tmpq[:],
                                               in1=psB[t % 2][:], op=ALU.mult)
                    if d < DC - 1:
                        vf(ins)
                ins.then_inc(sv, 1)
                vector.wait_ge(sv, V_XH(t))
            vector.memset(xbuf[0][:], 0.0).then_inc(sv, 1)
            vector.wait_ge(sv, V_Z)
            for c in range(DC):
                vector.wait_ge(sa, A_EV(c, NT - 1))
                if c >= 1:
                    vector.wait_ge(sh[0], 16 * c)
                    vector.wait_ge(sh[1], 16 * c)
                vector.tensor_tensor_scan(out=hs0[:], data0=alpha[:], data1=bu[:],
                                          initial=0.0, op0=ALU.mult, op1=ALU.add
                                          ).then_inc(sv, 1)
                vector.tensor_tensor_scan(out=Pb[:], data0=alpha[:], data1=xbuf[0][:],
                                          initial=1.0, op0=ALU.mult, op1=ALU.add
                                          ).then_inc(sv, 1)
                vector.wait_ge(sv, V_SC(c))
                vector.tensor_copy(out=hend[:, c:c + 1], in_=hs0[:, T - 1:T]
                                   ).then_inc(sv, 1)
                vector.wait_ge(sv, V_HE(c))

        @blk.tensor
        def _(tensor):
            tensor.wait_ge(sv, V_INIT)
            for t in range(NT):
                ts_ = slice(t * TT, (t + 1) * TT)
                tensor.wait_ge(lx2[t % 2], 16 * (t // 2 + 1))
                if t >= 2:
                    tensor.wait_ge(sv, V_XH(t - 2))
                for d in range(DC):
                    ds = slice(d * TT, (d + 1) * TT)
                    ins = tensor.matmul(out=psS[t % 2][0:1, :], lhsT=onec[:],
                                        rhs=xbuf[t % 2][:, ds], start=(d == 0),
                                        stop=(d == DC - 1))
                ins.then_inc(sp, 1)
                for d in range(DC):
                    tensor.wait_ge(sa, A_SQ(t, d))
                    tensor.matmul(out=psM[t % 2][0:1, :], lhsT=onec[:],
                                  rhs=sq[d % 2][:], start=(d == 0),
                                  stop=(d == DC - 1)).then_inc(sp, 1)
                tensor.wait_ge(sa, A_RST(t))
                tensor.matmul(out=psA[t % 2][:], lhsT=oner[:], rhs=mr[:],
                              start=True, stop=True)
                tensor.matmul(out=psB[t % 2][:], lhsT=oner[:], rhs=rr[:],
                              start=True, stop=True).then_inc(sp, 1)
            tensor.wait_ge(sv, V_WB(2 * DC - 1))
            tensor.wait_ge(sv, V_PHA)
            for c in range(DC):
                ns = slice(c * 128, (c + 1) * 128)
                for tt in range(NT):
                    ts_ = slice(tt * TT, (tt + 1) * TT)
                    j = c * NT + tt
                    if j >= 4:
                        tensor.wait_ge(sa, A_EV(*divmod(j - 4, NT)))
                    for d in range(DC):
                        tensor.matmul(out=pA[j % 4][:], lhsT=wab[d][:, ns],
                                      rhs=xh[d][:, ts_], start=(d == 0),
                                      stop=(d == DC - 1))
                    for d in range(DC):
                        ins = tensor.matmul(out=pB[j % 4][:], lhsT=wbb[d][:, ns],
                                            rhs=xh[d][:, ts_], start=(d == 0),
                                            stop=(d == DC - 1))
                    ins.then_inc(sp, 1)

        @blk.scalar
        def _(scalar):
            af = FE(scalar, af_s)
            scalar.wait_ge(lfix, 32)
            for t in range(NT):
                ts_ = slice(t * TT, (t + 1) * TT)
                scalar.wait_ge(lx2[t % 2], 16 * (t // 2 + 1))
                if t >= 1:
                    scalar.wait_ge(srl, 32 * t)
                for d in range(DC):
                    ds = slice(d * TT, (d + 1) * TT)
                    g = t * DC + d
                    if g >= 2:
                        scalar.wait_ge(sp, P_SQ((g - 2) // DC, (g - 2) % DC))
                    scalar.activation(out=sq[d % 2][:], in_=xbuf[t % 2][:, ds],
                                      func=AF.Square).then_inc(sa, 1)
                scalar.wait_ge(sv, V_VAR(t))
                af(scalar.activation(out=lnt[:], in_=vr[:], func=AF.Ln))
                scalar.activation(out=rr[:], in_=lnt[:], func=AF.Exp,
                                  scale=-0.5).then_inc(sa, 1)
                scalar.wait_ge(sa, A_RST(t))
                scalar.dma_start(out=stats_t[0:1, ts_], in_=mr[:]).then_inc(srl, 16)
                scalar.dma_start(out=stats_t[1:2, ts_], in_=rr[:]).then_inc(srl, 16)
            for c in range(DC):
                if c >= 1:
                    scalar.wait_ge(sv, V_SC(c - 1))
                    scalar.dma_start(out=hs0_t[(c - 1) * 128:c * 128, :],
                                     in_=hs0[:]).then_inc(sh[0], 16)
                    scalar.wait_ge(sv, V_PS(c - 1))
                    scalar.dma_start(out=P_t[(c - 1) * 128:c * 128, :],
                                     in_=Pb[:]).then_inc(sh[1], 16)
                for tt in range(NT):
                    ts_ = slice(tt * TT, (tt + 1) * TT)
                    j = c * NT + tt
                    scalar.wait_ge(sp, P_AB(c, tt))
                    if tt == 0 and c >= 1:
                        scalar.wait_ge(sv, V_PS(c - 1))
                    scalar.activation(out=alpha[:, ts_], in_=pA[j % 4][:],
                                      func=AF.Sigmoid, bias=bac[:, c:c + 1]
                                      ).then_inc(sa, 1)
                    scalar.activation(out=bu[:, ts_], in_=pB[j % 4][:],
                                      func=AF.Identity, bias=bbc[:, c:c + 1]
                                      ).then_inc(sa, 1)
            scalar.wait_ge(sv, V_SC(DC - 1))
            scalar.dma_start(out=hs0_t[(DC - 1) * 128:DC * 128, :], in_=hs0[:]
                             ).then_inc(sh[0], 16)
            scalar.wait_ge(sv, V_PS(DC - 1))
            scalar.dma_start(out=P_t[(DC - 1) * 128:DC * 128, :], in_=Pb[:]
                             ).then_inc(sh[1], 16)

    return nc





def build_L4a():
    nc = bass.Bass("TRN2", target_bir_lowering=False)
    x_t = nc.dram_tensor("x", [D, T], F32, kind="ExternalInput").ap()
    hs0_t = nc.dram_tensor("hs0", [D, T], F32, kind="ExternalInput").ap()
    P_t = nc.dram_tensor("P", [D, T], F32, kind="ExternalInput").ap()
    hin_t = nc.dram_tensor("hin", [128, DC], F32, kind="ExternalInput").ap()
    stats_t = nc.dram_tensor("stats", [2, T], F32, kind="ExternalInput").ap()
    cT_t = nc.dram_tensor("cT", [D, D], F32, kind="ExternalInput").ap()
    Dc_t = nc.dram_tensor("Dc", [128, DC], F32, kind="ExternalInput").ap()
    x3_t = nc.dram_tensor("x3", [D, T], F32, kind="ExternalOutput").ap()

    TT2 = 256
    NT2 = T // TT2
    with ExitStack() as st:
        S = st.enter_context
        mrl = [S(nc.sbuf_tensor(f"mrl{i}", [1, TT2], F32)) for i in range(2)]
        rrl = [S(nc.sbuf_tensor(f"rrl{i}", [1, TT2], F32)) for i in range(2)]
        HT = T // 2
        hsld = [S(nc.sbuf_tensor(f"hsld{i}", [128, HT], F32)) for i in range(2)]
        pld = [S(nc.sbuf_tensor(f"pld{i}", [128, HT], F32)) for i in range(2)]
        hsb = [S(nc.sbuf_tensor(f"hsb{c}", [128, T], BF16)) for c in range(DC)]
        ctb = [S(nc.sbuf_tensor(f"ctb{c}", [128, D], BF16)) for c in range(DC)]
        cld = S(nc.sbuf_tensor("cld", [128, D], F32))
        x2b = [S(nc.sbuf_tensor(f"x2b{i}", [128, DC * TT2], F32)) for i in range(2)]
        ystg = [S(nc.sbuf_tensor(f"ystg{i}", [128, DC * TT2], F32)) for i in range(2)]
        xhst = S(nc.sbuf_tensor("xhst", [128, DC * TT2], F32))
        hin = S(nc.sbuf_tensor("hin_sb", [128, DC], F32))
        Dc = S(nc.sbuf_tensor("Dc_sb", [128, DC], F32))
        oner = S(nc.sbuf_tensor("oner", [1, 128], F32))
        pbank = [S(nc.psum_tensor(f"pbank{i}", [128, 2 * TT2], F32)) for i in range(4)]
        pmm = [pbank[i // 2][:, (i % 2) * TT2:(i % 2 + 1) * TT2] for i in range(DC)]
        psmr = [S(nc.psum_tensor(f"psmr{i}", [128, 2 * TT2], F32)) for i in range(2)]
        psM = [psmr[i][:, 0:TT2] for i in range(2)]
        psR = [psmr[i][:, TT2:2 * TT2] for i in range(2)]
        lfix = S(nc.semaphore("lfix"))
        lx = [S(nc.semaphore(f"lx{i}")) for i in range(2)]
        lm = [S(nc.semaphore(f"lm{i}")) for i in range(2)]
        gc = [S(nc.semaphore(f"gc{i}")) for i in range(2)]
        glh2 = [S(nc.semaphore(f"glh{i}")) for i in range(2)]
        glp2 = [S(nc.semaphore(f"glp{i}")) for i in range(2)]
        sx_ = [S(nc.semaphore(f"sx{i}")) for i in range(2)]
        sv = S(nc.semaphore("sv"))
        sp = S(nc.semaphore("sp"))
        sa = S(nc.semaphore("sa"))
        vf_s = S(nc.semaphore("vf"))
        blk = S(nc.Block())

        V_INIT = 1
        V_CB = lambda k: 2 + k
        V_HSB2 = lambda k: 10 + k
        V_HSB = lambda c: V_HSB2(2 * c + 1)
        V_X3 = lambda tt: 26 + tt
        V_FIN = V_X3(NT2 - 1)
        P_Y = lambda tt: tt + 1
        A_EV = lambda tt: 8 * (tt + 1)

        @blk.sync
        def _(sync):
            sync.dma_start(out=hin[:], in_=hin_t).then_inc(lfix, 16)
            sync.dma_start(out=Dc[:], in_=Dc_t).then_inc(lfix, 16)
            for tt in range(NT2):
                ts_ = slice(tt * TT2, (tt + 1) * TT2)
                if tt >= 2:
                    sync.wait_ge(sv, V_X3(tt - 2))
                    sync.wait_ge(sx_[tt % 2], 16 * (tt // 2))
                sync.dma_start(
                    out=x2b[tt % 2][:].rearrange("p (c t) -> p c t", c=DC),
                    in_=x_t[:, ts_].rearrange("(c p) t -> p c t", p=128),
                ).then_inc(lx[tt % 2], 16)
                sync.dma_start(out=mrl[tt % 2][:], in_=stats_t[0:1, ts_]
                               ).then_inc(lm[tt % 2], 16)
                sync.dma_start(out=rrl[tt % 2][:], in_=stats_t[1:2, ts_]
                               ).then_inc(lm[tt % 2], 16)
            sync.wait_ge(sv, V_FIN)
            sync.wait_ge(sx_[0], 16 * (NT2 // 2))
            sync.wait_ge(sx_[1], 16 * (NT2 // 2))

        @blk.gpsimd
        def _(gpsimd):
            for k in range(DC):
                if k >= 1:
                    gpsimd.wait_ge(sv, V_CB(k - 1))
                gpsimd.dma_start(out=cld[:],
                                 in_=cT_t[k * 128:(k + 1) * 128, :]
                                 ).then_inc(gc[k % 2], 16)
            for k in range(2 * DC):
                c, half = divmod(k, 2)
                if k >= 2:
                    gpsimd.wait_ge(sv, V_HSB2(k - 2))
                hsl = slice(half * HT, (half + 1) * HT)
                gpsimd.dma_start(out=hsld[k % 2][:],
                                 in_=hs0_t[c * 128:(c + 1) * 128, hsl]
                                 ).then_inc(glh2[k % 2], 16)
                gpsimd.dma_start(out=pld[k % 2][:],
                                 in_=P_t[c * 128:(c + 1) * 128, hsl]
                                 ).then_inc(glp2[k % 2], 16)

        @blk.vector
        def _(vector):
            vf = FE(vector, vf_s)
            vector.memset(oner[:], 1.0).then_inc(sv, 1)
            vector.wait_ge(sv, V_INIT)
            vector.wait_ge(lfix, 32)
            for k in range(DC):
                vector.wait_ge(gc[k % 2], 16 * (k // 2 + 1))
                vector.tensor_copy(out=ctb[k][:], in_=cld[:]).then_inc(sv, 1)
            for k in range(2 * DC):
                c, half = divmod(k, 2)
                hsl = slice(half * HT, (half + 1) * HT)
                vector.wait_ge(glh2[k % 2], 16 * (k // 2 + 1))
                vector.wait_ge(glp2[k % 2], 16 * (k // 2 + 1))
                vector.scalar_tensor_tensor(out=hsb[c][:, hsl], in0=pld[k % 2][:],
                                            scalar=hin[:, c:c + 1],
                                            in1=hsld[k % 2][:],
                                            op0=ALU.mult, op1=ALU.add
                                            ).then_inc(sv, 1)
                vector.wait_ge(sv, V_HSB2(k))
            for tt in range(NT2):
                ds_all = slice(0, DC * TT2)
                vector.wait_ge(lx[tt % 2], 16 * (tt // 2 + 1))
                vector.wait_ge(sp, P_Y(tt))
                vector.wait_ge(sa, A_EV(tt))
                for d in range(DC):
                    ds = slice(d * TT2, (d + 1) * TT2)
                    ins = vector.tensor_sub(out=xhst[:, ds], in0=x2b[tt % 2][:, ds],
                                            in1=psM[tt % 2])
                vf(ins)
                for d in range(DC):
                    ds = slice(d * TT2, (d + 1) * TT2)
                    ins = vector.tensor_tensor(out=xhst[:, ds], in0=xhst[:, ds],
                                               in1=psR[tt % 2], op=ALU.mult)
                vf(ins)
                for d in range(DC):
                    ds = slice(d * TT2, (d + 1) * TT2)
                    ins = vector.scalar_tensor_tensor(out=ystg[tt % 2][:, ds],
                                                      in0=xhst[:, ds],
                                                      scalar=Dc[:, d:d + 1],
                                                      in1=ystg[tt % 2][:, ds],
                                                      op0=ALU.mult, op1=ALU.add)
                vf(ins)
                for d in range(DC):
                    ds = slice(d * TT2, (d + 1) * TT2)
                    ins = vector.tensor_add(out=x2b[tt % 2][:, ds],
                                            in0=ystg[tt % 2][:, ds],
                                            in1=x2b[tt % 2][:, ds])
                ins.then_inc(sv, 1)
                vector.wait_ge(sv, V_X3(tt))

        @blk.tensor
        def _(tensor):
            tensor.wait_ge(lfix, 32)
            tensor.wait_ge(sv, V_HSB(DC - 1))
            for tt in range(NT2):
                ts_ = slice(tt * TT2, (tt + 1) * TT2)
                if tt >= 1:
                    tensor.wait_ge(sa, A_EV(tt - 1))
                if tt >= 2:
                    tensor.wait_ge(sv, V_X3(tt - 2))
                tensor.wait_ge(lm[tt % 2], 16 * (2 * (tt // 2) + 2))
                for d in range(DC):
                    for c in range(DC):
                        tensor.matmul(out=pmm[d],
                                      lhsT=ctb[c][:, d * 128:(d + 1) * 128],
                                      rhs=hsb[c][:, ts_], start=(c == 0),
                                      stop=(c == DC - 1))
                tensor.matmul(out=psM[tt % 2], lhsT=oner[:], rhs=mrl[tt % 2][:],
                              start=True, stop=True)
                tensor.matmul(out=psR[tt % 2], lhsT=oner[:], rhs=rrl[tt % 2][:],
                              start=True, stop=True).then_inc(sp, 1)

        @blk.scalar
        def _(scalar):
            for tt in range(NT2):
                scalar.wait_ge(sp, P_Y(tt))
                if tt >= 2:
                    scalar.wait_ge(sv, V_X3(tt - 2))
                for d in range(DC):
                    ds = slice(d * TT2, (d + 1) * TT2)
                    scalar.activation(out=ystg[tt % 2][:, ds], in_=pmm[d],
                                      func=AF.Identity).then_inc(sa, 1)
                scalar.wait_ge(sv, V_X3(tt))
                scalar.dma_start(
                    out=x3_t[:, tt * TT2:(tt + 1) * TT2].rearrange("(c p) t -> p c t", p=128),
                    in_=x2b[tt % 2][:].rearrange("p (c t) -> p c t", c=DC),
                ).then_inc(sx_[tt % 2], 16)

    return nc



def _asc(a):
    return np.ascontiguousarray(a)


def _shard_u(u):
    out = []
    for c in range(NCORE):
        b, h = c // 2, c % 2
        out.append(_asc(u[b, h * T:(h + 1) * T, :].T))
    return out


_EYE = np.eye(128, dtype=np.float32)


def _bcol(v):
    return np.ascontiguousarray(np.asarray(v).reshape(-1, 128).T)


def _gather_mall(results, key="mloc"):
    m = np.concatenate([np.asarray(r[key]).reshape(1, 1) for r in results], axis=1)
    return np.ascontiguousarray(np.broadcast_to(m, (128, NCORE)))


def kernel(**inputs):
    u = np.asarray(inputs["L0_attn_u"])
    w1T = _asc(np.asarray(inputs["L0_ffn_w1"]).T)
    w2T = _asc(np.asarray(inputs["L0_ffn_w2"]).T)
    waT = _asc(np.asarray(inputs["L0_attn_W_alpha_w"]).T)
    wbT = _asc(np.asarray(inputs["L0_attn_W_B_w"]).T)
    cT = _asc(np.asarray(inputs["L0_attn_C_w"]).T)
    b1c = _bcol(inputs["L0_ffn_b1"])
    b2c = _bcol(inputs["L0_ffn_b2"])
    bac = _bcol(inputs["L0_attn_W_alpha_b"])
    bbc = _bcol(inputs["L0_attn_W_B_b"])
    Dcc = _bcol(inputs["L0_attn_D"])
    shards = _shard_u(u)
    cores = list(range(NCORE))

    nc_ln = build_L1()
    nc_ffn1 = build_L2()
    nc_ffn1p = build_L2(prequant=True)
    nc_ffn2 = build_L3a()
    nc_ffn2p = build_L3a(prequant=True)
    nc_ssm1 = build_L3b()
    nc_ssm2 = build_L4a()

    r1 = run_bass_kernel_spmd(
        nc_ln, [{"u": shards[c], "eye": _EYE} for c in cores], core_ids=cores)
    m1 = _gather_mall(r1.results)

    r2 = run_bass_kernel_spmd(
        nc_ffn1, [{"u": shards[c], "stats": r1.results[c]["stats"], "mall": m1,
                   "w1T": w1T, "b1c": b1c, "eye": _EYE} for c in cores],
        core_ids=cores)
    m2 = _gather_mall(r2.results)

    r3a = run_bass_kernel_spmd(
        nc_ffn2, [{"x": shards[c], "h1": r2.results[c]["h1"], "mall": m2,
                   "w2T": w2T, "b2c": b2c, "eye": _EYE} for c in cores],
        core_ids=cores)

    r3b = run_bass_kernel_spmd(
        nc_ssm1, [{"x": r3a.results[c]["xo"], "waT": waT, "wbT": wbT,
                   "bac": bac, "bbc": bbc} for c in cores], core_ids=cores)

    zeros_hin = np.zeros((128, DC), np.float32)
    hins = [np.asarray(r3b.results[c - 1]["hend"]) if c % 2 == 1 else zeros_hin
            for c in cores]
    r4a = run_bass_kernel_spmd(
        nc_ssm2, [{"x": r3a.results[c]["xo"], "hs0": r3b.results[c]["hs0"],
                   "P": r3b.results[c]["P"], "hin": hins[c],
                   "stats": r3b.results[c]["stats"], "cT": cT, "Dc": Dcc}
                  for c in cores], core_ids=cores)

    r4b = run_bass_kernel_spmd(
        nc_ln, [{"u": r4a.results[c]["x3"], "eye": _EYE} for c in cores],
        core_ids=cores)
    m3 = _gather_mall(r4b.results)

    r5 = run_bass_kernel_spmd(
        nc_ffn1p, [{"u": r4a.results[c]["x3"], "stats": r4b.results[c]["stats"],
                    "mall": m3, "wq1i": r2.results[c]["wq1"],
                    "swi": np.ascontiguousarray(np.broadcast_to(
                        np.asarray(r2.results[c]["sw"]).reshape(1, 1), (128, 1))),
                    "b1c": b1c, "eye": _EYE}
                   for c in cores], core_ids=cores)
    m4 = _gather_mall(r5.results)

    r6 = run_bass_kernel_spmd(
        nc_ffn2p, [{"x": r4a.results[c]["x3"], "h1": r5.results[c]["h1"],
                    "mall": m4, "wq2i": r3a.results[c]["wq2"],
                    "swi": np.ascontiguousarray(np.broadcast_to(
                        np.asarray(r3a.results[c]["sw"]).reshape(1, 1), (128, 1))),
                    "b2c": b2c, "eye": _EYE}
                   for c in cores], core_ids=cores)

    out = np.empty((4, 8192, D), np.float32)
    for c in cores:
        b, h = c // 2, c % 2
        out[b, h * T:(h + 1) * T, :] = np.asarray(r6.results[c]["xo"]).T
    return out


# version 08
# speedup vs baseline: 1.2316x; 1.0008x over previous
import numpy as np
import ml_dtypes
from contextlib import ExitStack

import concourse.bass as bass
import concourse.mybir as mybir
from concourse.bass_utils import run_bass_kernel_spmd

F32 = mybir.dt.float32
BF16 = mybir.dt.bfloat16
I32 = mybir.dt.int32
AF = mybir.ActivationFunctionType
ALU = mybir.AluOpType
AX = mybir.AxisListType

NCORE = 8
D = 1024
T = 4096
H = 4096
DC = D // 128
HC = H // 128
TT = 512
NT = T // TT
EPS = 1e-5
QP = 127.0
MAGIC = 12582912.0
MINR = 1e-6
INV_D = 1.0 / D



def recip_newton(vec, out, in_, scr):
    vec.reciprocal(out=out, in_=in_)
    vec.tensor_tensor(out=scr, in0=in_, in1=out, op=ALU.mult)
    vec.tensor_scalar(out=scr, in0=scr, scalar1=-1.0, scalar2=2.0,
                      op0=ALU.mult, op1=ALU.add)
    vec.tensor_tensor(out=out, in0=out, in1=scr, op=ALU.mult)


def quant_tiles(vec, out_bf, in_f32, r_ap, scr_i32):
    vec.tensor_scalar(out=scr_i32, in0=in_f32, scalar1=r_ap, scalar2=None,
                      op0=ALU.mult)
    vec.tensor_copy(out=out_bf, in_=scr_i32)




class FE:

    def __init__(self, eng, sem):
        self.eng, self.sem, self.n = eng, sem, 0

    def __call__(self, ins):
        self.n += 1
        ins.then_inc(self.sem, 1)
        self.eng.wait_ge(self.sem, self.n)


def recip_newton_f(vec, fe, out, in_, scr):
    fe(vec.reciprocal(out=out, in_=in_))
    fe(vec.tensor_tensor(out=scr, in0=in_, in1=out, op=ALU.mult))
    fe(vec.tensor_scalar(out=scr, in0=scr, scalar1=-1.0, scalar2=2.0,
                         op0=ALU.mult, op1=ALU.add))
    fe(vec.tensor_tensor(out=out, in0=out, in1=scr, op=ALU.mult))



def build_L1():
    nc = bass.Bass("TRN2", target_bir_lowering=False)
    u_t = nc.dram_tensor("u", [D, T], F32, kind="ExternalInput").ap()
    eye_t = nc.dram_tensor("eye", [128, 128], F32, kind="ExternalInput").ap()
    stats_t = nc.dram_tensor("stats", [2, T], F32, kind="ExternalOutput").ap()
    mloc_t = nc.dram_tensor("mloc", [1, 1], F32, kind="ExternalOutput").ap()

    with ExitStack() as st:
        S = st.enter_context
        us = [S(nc.sbuf_tensor(f"u{d}", [128, T], F32)) for d in range(DC)]
        eye = S(nc.sbuf_tensor("eye_sb", [128, 128], F32))
        onec = S(nc.sbuf_tensor("onec", [128, 1], F32))
        oner = S(nc.sbuf_tensor("oner", [1, 128], F32))
        sq = [S(nc.sbuf_tensor(f"sq{i}", [128, TT], F32)) for i in range(2)]
        mrow = S(nc.sbuf_tensor("mrow", [1, T], F32))
        rrow = S(nc.sbuf_tensor("rrow", [1, T], F32))
        t0 = S(nc.sbuf_tensor("t0", [1, TT], F32))
        t1 = S(nc.sbuf_tensor("t1", [1, TT], F32))
        vrow = S(nc.sbuf_tensor("vrow", [1, T], F32))
        lnt = S(nc.sbuf_tensor("lnt", [1, TT], F32))
        tmp8 = S(nc.sbuf_tensor("tmp8", [128, T], F32))
        red8 = S(nc.sbuf_tensor("red8", [128, DC], F32))
        rt = S(nc.sbuf_tensor("rt", [128, 1], F32))
        mx = S(nc.sbuf_tensor("mx", [128, 1], F32))
        sc = S(nc.sbuf_tensor("sc", [1, 1], F32))
        psA = S(nc.psum_tensor("psA", [128, TT], F32))
        psB = [S(nc.psum_tensor(f"psB{i}", [128, TT], F32)) for i in range(2)]
        psM = [S(nc.psum_tensor(f"psM{i}", [128, TT], F32)) for i in range(2)]
        psR = [S(nc.psum_tensor(f"psR{i}", [128, TT], F32)) for i in range(2)]
        ld = S(nc.semaphore("ld"))
        sa = S(nc.semaphore("sa"))
        sv = S(nc.semaphore("sv"))
        sp = S(nc.semaphore("sp"))
        so = S(nc.semaphore("so"))
        vf_s = S(nc.semaphore("vf"))
        blk = S(nc.Block())

        @blk.sync
        def _(sync):
            for d in range(DC):
                sync.dma_start(out=us[d][:], in_=u_t[d * 128:(d + 1) * 128, :]
                               ).then_inc(ld, 16)
            sync.dma_start(out=eye[:], in_=eye_t).then_inc(ld, 16)
            sync.wait_ge(sv, 2 + NT + 3 * NT + NT + 1)
            sync.dma_start(out=stats_t[0:1, :], in_=mrow[:]).then_inc(so, 16)
            sync.dma_start(out=stats_t[1:2, :], in_=rrow[:]).then_inc(so, 16)
            sync.dma_start(out=mloc_t, in_=sc[:]).then_inc(so, 16)
            sync.wait_ge(so, 48)

        SV_INIT = 2
        SV_A1 = SV_INIT + NT
        SV_A2 = SV_A1 + 3 * NT
        SV_B = SV_A2 + NT
        SV_FIN = SV_B + 1

        @blk.tensor
        def _(tensor):
            tensor.wait_ge(ld, 144)
            tensor.wait_ge(sv, 2)
            for t in range(NT):
                if t >= 1:
                    tensor.wait_ge(sv, SV_INIT + t)
                ts = slice(t * TT, (t + 1) * TT)
                for d in range(DC):
                    ins = tensor.matmul(out=psA[0:1, :], lhsT=onec[:],
                                        rhs=us[d][:, ts],
                                        start=(d == 0), stop=(d == DC - 1))
                ins.then_inc(sp, 1)
            nsq = 0
            for t in range(NT):
                for d in range(DC):
                    nsq += 1
                    tensor.wait_ge(sa, nsq)
                    if d == 0 and t >= 2:
                        tensor.wait_ge(sv, SV_A1 + 3 * (t - 1))
                    tensor.matmul(out=psB[t % 2][0:1, :], lhsT=onec[:],
                                  rhs=sq[nsq % 2][:], start=(d == 0),
                                  stop=(d == DC - 1)).then_inc(sp, 1)
            for t in range(NT):
                ts = slice(t * TT, (t + 1) * TT)
                tensor.wait_ge(sa, NT * DC + 2 * (t + 1))
                if t >= 2:
                    tensor.wait_ge(sv, SV_A2 + (t - 1))
                tensor.matmul(out=psM[t % 2][:], lhsT=oner[:], rhs=mrow[0:1, ts],
                              start=True, stop=True)
                tensor.matmul(out=psR[t % 2][:], lhsT=oner[:], rhs=rrow[0:1, ts],
                              start=True, stop=True).then_inc(sp, 1)
            tensor.wait_ge(sv, SV_B)
            tensor.transpose(out=psA[0:1, 0:128], in_=mx[:], identity=eye[:]
                             ).then_inc(sp, 1)

        @blk.scalar
        def _(scalar):
            scalar.wait_ge(ld, 144)
            n = 0
            for t in range(NT):
                ts = slice(t * TT, (t + 1) * TT)
                for d in range(DC):
                    n += 1
                    if n > 2:
                        scalar.wait_ge(sp, NT + n - 2)
                    scalar.activation(out=sq[n % 2][:], in_=us[d][:, ts],
                                      func=AF.Square).then_inc(sa, 1)
            for t in range(NT):
                ts = slice(t * TT, (t + 1) * TT)
                scalar.wait_ge(sv, 2 + NT + 3 * (t + 1))
                if t >= 1:
                    scalar.wait_ge(sa, NT * DC + 2 * t)
                scalar.activation(out=lnt[:], in_=vrow[0:1, ts], func=AF.Ln
                                  ).then_inc(sa, 1)
                scalar.wait_ge(sa, NT * DC + 2 * t + 1)
                scalar.activation(out=rrow[0:1, ts], in_=lnt[:], func=AF.Exp,
                                  scale=-0.5).then_inc(sa, 1)

        @blk.vector
        def _(vector):
            vf = FE(vector, vf_s)
            vector.memset(onec[:], 1.0).then_inc(sv, 1)
            vector.memset(oner[:], 1.0)
            vector.memset(mx[:], 0.0).then_inc(sv, 1)
            vector.wait_ge(sv, SV_INIT)
            for t in range(NT):
                ts = slice(t * TT, (t + 1) * TT)
                vector.wait_ge(sp, t + 1)
                vector.tensor_scalar(out=mrow[0:1, ts], in0=psA[0:1, :],
                                     scalar1=INV_D, scalar2=None,
                                     op0=ALU.mult).then_inc(sv, 1)
            vector.wait_ge(sv, SV_A1)
            for t in range(NT):
                ts = slice(t * TT, (t + 1) * TT)
                vector.wait_ge(sp, NT + DC * (t + 1))
                vector.tensor_scalar(out=t0[:], in0=psB[t % 2][0:1, :],
                                     scalar1=INV_D, scalar2=None, op0=ALU.mult
                                     ).then_inc(sv, 1)
                vector.tensor_tensor(out=t1[:], in0=mrow[0:1, ts],
                                     in1=mrow[0:1, ts], op=ALU.mult).then_inc(sv, 1)
                vector.wait_ge(sv, SV_A1 + 3 * t + 2)
                vector.scalar_tensor_tensor(out=vrow[0:1, ts], in0=t0[:],
                                            scalar=EPS, in1=t1[:], op0=ALU.add,
                                            op1=ALU.subtract).then_inc(sv, 1)
                vector.wait_ge(sv, SV_A1 + 3 * t + 3)
            for t in range(NT):
                ts = slice(t * TT, (t + 1) * TT)
                vector.wait_ge(sp, NT + DC * NT + (t + 1))
                for d in range(DC):
                    ds = slice(d * TT, (d + 1) * TT)
                    ins = vector.tensor_sub(out=tmp8[:, ds], in0=us[d][:, ts],
                                            in1=psM[t % 2][:])
                vf(ins)
                for d in range(DC):
                    ds = slice(d * TT, (d + 1) * TT)
                    ins = vector.tensor_tensor(out=tmp8[:, ds], in0=tmp8[:, ds],
                                               in1=psR[t % 2][:], op=ALU.mult)
                vf(ins)
                for d in range(DC):
                    ds = slice(d * TT, (d + 1) * TT)
                    ins = vector.tensor_reduce(out=red8[:, d:d + 1], in_=tmp8[:, ds],
                                               axis=AX.X, op=ALU.max,
                                               apply_absolute_value=True)
                vf(ins)
                vf(vector.tensor_reduce(out=rt[:], in_=red8[:], axis=AX.X,
                                        op=ALU.max, apply_absolute_value=True))
                vector.tensor_max(out=mx[:], in0=mx[:], in1=rt[:]).then_inc(sv, 1)
                vector.wait_ge(sv, SV_A2 + t + 1)
            vector.wait_ge(sp, NT + DC * NT + NT + 1)
            vector.tensor_reduce(out=sc[:], in_=psA[0:1, 0:128], axis=AX.X,
                                 op=ALU.max, apply_absolute_value=True
                                 ).then_inc(sv, 1)

    return nc





def build_L2(prequant=False):
    nc = bass.Bass("TRN2", target_bir_lowering=False)
    u_t = nc.dram_tensor("u", [D, T], F32, kind="ExternalInput").ap()
    stats_t = nc.dram_tensor("stats", [2, T], F32, kind="ExternalInput").ap()
    mall_t = nc.dram_tensor("mall", [128, NCORE], F32, kind="ExternalInput").ap()
    if prequant:
        wq1i_t = nc.dram_tensor("wq1i", [D, H], BF16, kind="ExternalInput").ap()
        swi_t = nc.dram_tensor("swi", [128, 1], F32, kind="ExternalInput").ap()
    else:
        w1T_t = nc.dram_tensor("w1T", [D, H], F32, kind="ExternalInput").ap()
    b1c_t = nc.dram_tensor("b1c", [128, HC], F32, kind="ExternalInput").ap()
    eye_t = nc.dram_tensor("eye", [128, 128], F32, kind="ExternalInput").ap()
    h1_t = nc.dram_tensor("h1", [H, T], F32, kind="ExternalOutput").ap()
    if not prequant:
        wq1_t = nc.dram_tensor("wq1", [D, H], BF16, kind="ExternalOutput").ap()
        sw_t = nc.dram_tensor("sw", [1, 1], F32, kind="ExternalOutput").ap()
    mloc_t = nc.dram_tensor("mloc", [1, 1], F32, kind="ExternalOutput").ap()
    wq_src_t = wq1i_t if prequant else wq1_t

    NG = HC * NT
    NB = NG // 4
    with ExitStack() as st:
        S = st.enter_context
        mrow_t = S(nc.sbuf_tensor("mrow_t", [1, T], F32))
        rrow_t = S(nc.sbuf_tensor("rrow_t", [1, T], F32))
        mrow = mrow_t[0:1, :]
        rrow = rrow_t[0:1, :]
        ubuf = [S(nc.sbuf_tensor(f"ub{i}", [128, T], F32)) for i in range(2)]
        wbuf = [S(nc.sbuf_tensor(f"wb{i}", [128, H], F32)) for i in range(2)]
        xq = [S(nc.sbuf_tensor(f"xq{d}", [128, T], BF16)) for d in range(DC)]
        wqs = [S(nc.sbuf_tensor(f"wqs{i}", [128, DC * 128], BF16)) for i in range(2)]
        wqo = [S(nc.sbuf_tensor(f"wqo{i}", [128, H], BF16)) for i in range(2)]
        wint = S(nc.sbuf_tensor("wint", [128, H // 4], I32))
        xint = S(nc.sbuf_tensor("xint", [128, TT], I32))
        qf = S(nc.sbuf_tensor("qf", [128, TT], F32))
        red8 = S(nc.sbuf_tensor("red8", [128, DC], F32))
        rt2 = S(nc.sbuf_tensor("rt2", [128, 1], F32))
        eye = S(nc.sbuf_tensor("eye_sb", [128, 128], F32))
        oner = S(nc.sbuf_tensor("oner", [1, 128], F32))
        mallb = S(nc.sbuf_tensor("mallb", [128, NCORE], F32))
        b1c = S(nc.sbuf_tensor("b1c_sb", [128, HC], F32))
        b1s = S(nc.sbuf_tensor("b1s", [128, HC], F32))
        tmpq = S(nc.sbuf_tensor("tmpq", [128, T], F32))
        wmx = S(nc.sbuf_tensor("wmx", [128, 1], F32))
        gmx = S(nc.sbuf_tensor("gmx", [128, 1], F32))
        mxm = S(nc.sbuf_tensor("mxm", [128, 1], F32))
        sxb = S(nc.sbuf_tensor("sxb", [128, 1], F32))
        rxb = S(nc.sbuf_tensor("rxb", [128, 1], F32))
        s1b = S(nc.sbuf_tensor("s1b", [128, 1], F32))
        swb = S(nc.sbuf_tensor("swb", [128, 1], F32))
        rwb = S(nc.sbuf_tensor("rwb", [128, 1], F32))
        sc1 = S(nc.sbuf_tensor("sc1", [128, 1], F32))
        wsc = S(nc.sbuf_tensor("wsc", [1, 1], F32))
        sw_s = S(nc.sbuf_tensor("sw_s", [1, 1], F32))
        rw1 = S(nc.sbuf_tensor("rw1", [1, 1], F32))
        r1a = S(nc.sbuf_tensor("r1a", [1, 1], F32))
        srow = S(nc.sbuf_tensor("srow", [1, 2], F32))
        mgl = S(nc.sbuf_tensor("mgl", [1, 1], F32))
        pmm = [S(nc.psum_tensor(f"pmm{i}", [128, TT], F32)) for i in range(4)]
        psE = S(nc.psum_tensor("psE", [128, TT], F32))
        psF = S(nc.psum_tensor("psF", [128, TT], F32))
        paux = S(nc.psum_tensor("paux", [128, 128], F32))
        pbc = S(nc.psum_tensor("pbc", [128, 2], F32))
        lfix = S(nc.semaphore("lfix"))
        lu = [S(nc.semaphore(f"lu{i}")) for i in range(2)]
        gw = [S(nc.semaphore(f"gw{i}")) for i in range(2)]
        gq = [S(nc.semaphore(f"gq{i}")) for i in range(2)]
        sq = [S(nc.semaphore(f"sq{i}")) for i in range(2)]
        sh = [S(nc.semaphore(f"sh{i}")) for i in range(2)]
        sv = S(nc.semaphore("sv"))
        sp = S(nc.semaphore("sp"))
        sa = S(nc.semaphore("sa"))
        so = S(nc.semaphore("so"))
        vf_s = S(nc.semaphore("vf"))
        af_s = S(nc.semaphore("af"))
        blk = S(nc.Block())

        V_ONES = 1
        V_WCH = lambda d: 2 + d
        V_WMX = 10
        V_SROW = 11
        V_SCAL = 12
        V_WQ = lambda d: 13 + d
        V_XQ = lambda t: 21 + t
        V_G = lambda g: 29 + g
        V_GDONE = 29 + NG
        V_FIN = 30 + NG
        P_TR = 1
        P_BC = 2
        P_B = lambda t: 3 + t
        P_G = lambda g: 11 + g
        P_FINT = 11 + NG

        @blk.sync
        def _(sync):
            sync.dma_start(out=mrow_t[:], in_=stats_t[0:1, :]).then_inc(lfix, 16)
            sync.dma_start(out=rrow_t[:], in_=stats_t[1:2, :]).then_inc(lfix, 16)
            sync.dma_start(out=mallb[:], in_=mall_t).then_inc(lfix, 16)
            sync.dma_start(out=b1c[:], in_=b1c_t).then_inc(lfix, 16)
            sync.dma_start(out=eye[:], in_=eye_t).then_inc(lfix, 16)
            if prequant:
                sync.dma_start(out=swb[:], in_=swi_t).then_inc(lfix, 16)
            for t in range(NT):
                if t >= 2:
                    sync.wait_ge(sv, V_XQ(t - 2))
                sync.dma_start(
                    out=ubuf[t % 2][:].rearrange("p (c t) -> p c t", c=DC),
                    in_=u_t[:, t * TT:(t + 1) * TT].rearrange("(c p) t -> p c t", p=128),
                ).then_inc(lu[t % 2], 16)
            sync.wait_ge(sv, V_FIN)
            sync.dma_start(out=mloc_t, in_=mgl[:]).then_inc(so, 16)
            if not prequant:
                sync.dma_start(out=sw_t, in_=sw_s[:]).then_inc(so, 16)
                sync.wait_ge(so, 32)
                sync.wait_ge(sq[0], 16 * (DC // 2))
                sync.wait_ge(sq[1], 16 * (DC // 2))
            else:
                sync.wait_ge(so, 16)
            sync.wait_ge(sh[0], 16 * (NB // 2))
            sync.wait_ge(sh[1], 16 * (NB // 2))

        @blk.gpsimd
        def _(gpsimd):
            if not prequant:
                for d in range(DC):
                    if d >= 2:
                        gpsimd.wait_ge(sv, V_WCH(d - 2))
                    gpsimd.dma_start(out=wbuf[d % 2][:],
                                     in_=w1T_t[d * 128:(d + 1) * 128, :]
                                     ).then_inc(gw[d % 2], 16)
                gpsimd.wait_ge(sv, V_WCH(DC - 1))
                for d in range(DC):
                    if d >= 2:
                        gpsimd.wait_ge(sv, V_WQ(d - 2))
                    gpsimd.dma_start(out=wbuf[d % 2][:],
                                     in_=w1T_t[d * 128:(d + 1) * 128, :]
                                     ).then_inc(gw[d % 2], 16)
                gpsimd.wait_ge(sq[0], 16 * (DC // 2))
                gpsimd.wait_ge(sq[1], 16 * (DC // 2))
            for h in range(HC):
                if h >= 2:
                    gpsimd.wait_ge(sp, P_G((h - 1) * NT - 1))
                gpsimd.dma_start(
                    out=wqs[h % 2][:].rearrange("p (c m) -> p c m", c=DC),
                    in_=wq_src_t[:, h * 128:(h + 1) * 128].rearrange("(c p) m -> p c m", p=128),
                ).then_inc(gq[h % 2], 16)

        @blk.vector
        def _(vector):
            vf = FE(vector, vf_s)
            vector.memset(oner[:], 1.0)
            vector.memset(wmx[:], 0.0)
            vector.memset(gmx[:], 0.0).then_inc(sv, 1)
            vector.wait_ge(sv, V_ONES)
            vector.wait_ge(lfix, 16 * (6 if prequant else 5))
            vf(vector.tensor_reduce(out=mxm[:], in_=mallb[:], axis=AX.X, op=ALU.max,
                                    apply_absolute_value=True))
            vf(vector.tensor_scalar(out=sxb[:], in0=mxm[:], scalar1=MINR,
                                    scalar2=1.0 / QP, op0=ALU.max, op1=ALU.mult))
            recip_newton_f(vector, vf, rxb[:], sxb[:], sc1[:])
            vf(vector.tensor_scalar(out=rrow, in0=rrow, scalar1=rxb[0:1, 0:1],
                                    scalar2=None, op0=ALU.mult))
            if not prequant:
                for d in range(DC):
                    vector.wait_ge(gw[d % 2], 16 * (d // 2 + 1))
                    vector.tensor_reduce(out=red8[:, d:d + 1], in_=wbuf[d % 2][:],
                                         axis=AX.X, op=ALU.max,
                                         apply_absolute_value=True).then_inc(sv, 1)
                vector.wait_ge(sv, V_WCH(DC - 1))
                vector.tensor_reduce(out=wmx[:], in_=red8[:], axis=AX.X, op=ALU.max,
                                     apply_absolute_value=True).then_inc(sv, 1)
                vector.wait_ge(sv, V_WMX)
                vector.wait_ge(sp, P_TR)
                vf(vector.tensor_reduce(out=wsc[:], in_=paux[0:1, 0:128], axis=AX.X,
                                        op=ALU.max, apply_absolute_value=True))
                vf(vector.tensor_scalar(out=sw_s[:], in0=wsc[:], scalar1=MINR,
                                        scalar2=1.0 / QP, op0=ALU.max, op1=ALU.mult))
            else:
                for _ in range(DC + 1):
                    vector.nop().then_inc(sv, 1)
                vector.wait_ge(sv, V_WMX)
            if prequant:
                recip_newton_f(vector, vf, rwb[:], swb[:], sc1[:])
                vector.nop().then_inc(sv, 1)
                vf(vector.tensor_tensor(out=s1b[:], in0=swb[:], in1=sxb[:],
                                        op=ALU.mult))
                vector.tensor_scalar(out=b1s[:], in0=b1c[:], scalar1=s1b[:, 0:1],
                                     scalar2=None, op0=ALU.mult).then_inc(sv, 1)
            else:
                recip_newton_f(vector, vf, rw1[:], sw_s[:], r1a[:])
                vector.tensor_copy(out=srow[0:1, 0:1], in_=sw_s[:])
                vector.tensor_copy(out=srow[0:1, 1:2], in_=rw1[:]).then_inc(sv, 1)
                vector.wait_ge(sp, P_BC)
                vector.tensor_copy(out=swb[:], in_=pbc[:, 0:1])
                vf(vector.tensor_copy(out=rwb[:], in_=pbc[:, 1:2]))
                vf(vector.tensor_tensor(out=s1b[:], in0=swb[:], in1=sxb[:],
                                        op=ALU.mult))
                vector.tensor_scalar(out=b1s[:], in0=b1c[:], scalar1=s1b[:, 0:1],
                                     scalar2=None, op0=ALU.mult).then_inc(sv, 1)
            for d in range(DC):
                if prequant:
                    vector.nop().then_inc(sv, 1)
                    continue
                vector.wait_ge(gw[d % 2], 16 * (4 + d // 2 + 1))
                if d >= 2:
                    vector.wait_ge(sq[d % 2], 16 * (d // 2))
                for k in range(4):
                    hs = slice(k * (H // 4), (k + 1) * (H // 4))
                    wf = tmpq[:, 0:H // 4]
                    vf(vector.tensor_scalar(out=wf, in0=wbuf[d % 2][:, hs],
                                            scalar1=rwb[:, 0:1], scalar2=MAGIC,
                                            op0=ALU.mult, op1=ALU.add))
                    vf(vector.tensor_scalar(out=wint[:], in0=wf, scalar1=MAGIC,
                                            scalar2=None, op0=ALU.subtract))
                    ins = vector.tensor_copy(out=wqo[d % 2][:, hs], in_=wint[:])
                    if k < 3:
                        vf(ins)
                ins.then_inc(sv, 1)
                vector.wait_ge(sv, V_WQ(d))
            for t in range(NT):
                ts_ = slice(t * TT, (t + 1) * TT)
                vector.wait_ge(lu[t % 2], 16 * (t // 2 + 1))
                vector.wait_ge(sp, P_B(t))
                for d in range(DC):
                    ds = slice(d * TT, (d + 1) * TT)
                    ins = vector.tensor_sub(out=tmpq[:, ds], in0=ubuf[t % 2][:, ds],
                                            in1=psE[:])
                vf(ins)
                for d in range(DC):
                    ds = slice(d * TT, (d + 1) * TT)
                    vf(vector.tensor_tensor(out=qf[:], in0=tmpq[:, ds],
                                            in1=psF[:],
                                            op=ALU.mult))
                    vf(vector.tensor_scalar(out=xint[:], in0=qf[:], scalar1=MAGIC,
                                            scalar2=MAGIC, op0=ALU.add,
                                            op1=ALU.subtract))
                    ins = vector.tensor_copy(out=xq[d][:, ts_], in_=xint[:])
                    if d < DC - 1:
                        vf(ins)
                ins.then_inc(sv, 1)
                vector.wait_ge(sv, V_XQ(t))
            for g in range(NG):
                vector.wait_ge(sa, g + 1)
                stg = wbuf[(g // 4) % 2]
                col = (g % 4) * TT
                vector.tensor_reduce(out=red8[:, g % 8:g % 8 + 1],
                                     in_=stg[:, col:col + TT], axis=AX.X,
                                     op=ALU.max, apply_absolute_value=True
                                     ).then_inc(sv, 1)
                if g % 8 == 7:
                    vector.wait_ge(sv, V_G(g))
                    vf(vector.tensor_reduce(out=rt2[:], in_=red8[:], axis=AX.X,
                                            op=ALU.max, apply_absolute_value=True))
                    if g == NG - 1:
                        vector.tensor_max(out=gmx[:], in0=gmx[:], in1=rt2[:]
                                          ).then_inc(sv, 1)
                    else:
                        vf(vector.tensor_max(out=gmx[:], in0=gmx[:], in1=rt2[:]))
            vector.wait_ge(sp, P_FINT)
            vector.tensor_reduce(out=mgl[:], in_=paux[0:1, 0:128], axis=AX.X,
                                 op=ALU.max, apply_absolute_value=True).then_inc(sv, 1)

        @blk.tensor
        def _(tensor):
            tensor.wait_ge(lfix, 16 * (6 if prequant else 5))
            if not prequant:
                tensor.wait_ge(sv, V_WMX)
                tensor.transpose(out=paux[0:1, 0:128], in_=wmx[:], identity=eye[:]
                                 ).then_inc(sp, 1)
            else:
                tensor.nop().then_inc(sp, 1)
            if prequant:
                tensor.nop().then_inc(sp, 1)
            else:
                tensor.wait_ge(sv, V_SROW)
                tensor.matmul(out=pbc[:], lhsT=oner[:], rhs=srow[:], start=True,
                              stop=True).then_inc(sp, 1)
            for t in range(NT):
                ts_ = slice(t * TT, (t + 1) * TT)
                if t >= 1:
                    tensor.wait_ge(sv, V_XQ(t - 1))
                tensor.matmul(out=psE[:], lhsT=oner[:],
                              rhs=mrow[:, ts_], start=True, stop=True)
                tensor.matmul(out=psF[:], lhsT=oner[:],
                              rhs=rrow[:, ts_], start=True, stop=True
                              ).then_inc(sp, 1)
            g = 0
            for h in range(HC):
                tensor.wait_ge(gq[h % 2], 16 * (h // 2 + 1))
                for t in range(NT):
                    ts_ = slice(t * TT, (t + 1) * TT)
                    if h == 0:
                        tensor.wait_ge(sv, V_XQ(t))
                    if g >= 4:
                        tensor.wait_ge(sa, g - 3)
                    for d in range(DC):
                        ins = tensor.matmul(out=pmm[g % 4][:],
                                            lhsT=wqs[h % 2][:, d * 128:(d + 1) * 128],
                                            rhs=xq[d][:, ts_], start=(d == 0),
                                            stop=(d == DC - 1))
                    ins.then_inc(sp, 1)
                    g += 1
            tensor.wait_ge(sv, V_GDONE)
            tensor.transpose(out=paux[0:1, 0:128], in_=gmx[:], identity=eye[:]
                             ).then_inc(sp, 1)

        @blk.scalar
        def _(scalar):
            if not prequant:
                for d in range(DC):
                    scalar.wait_ge(sv, V_WQ(d))
                    scalar.dma_start(out=wq1_t[d * 128:(d + 1) * 128, :],
                                     in_=wqo[d % 2][:]).then_inc(sq[d % 2], 16)
            scalar.wait_ge(sv, V_SCAL)
            scalar.wait_ge(sv, V_WQ(DC - 1))
            g = 0
            for h in range(HC):
                for t in range(NT):
                    b = g // 4
                    stg = wbuf[b % 2]
                    col = (g % 4) * TT
                    if g % 4 == 0 and b >= 2:
                        scalar.wait_ge(sh[b % 2], 16 * (b // 2))
                        scalar.wait_ge(sv, V_G(4 * (b - 1) - 1))
                    scalar.wait_ge(sp, P_G(g))
                    scalar.activation(out=stg[:, col:col + TT],
                                      in_=pmm[g % 4][:], func=AF.Gelu,
                                      bias=b1s[:, h:h + 1],
                                      scale=s1b[:, 0:1]).then_inc(sa, 1)
                    if g % 4 == 3:
                        scalar.wait_ge(sa, g + 1)
                        th = (t - 3) // 4
                        scalar.dma_start(
                            out=h1_t[h * 128:(h + 1) * 128,
                                     th * 2048:(th + 1) * 2048],
                            in_=stg[:, 0:2048]).then_inc(sh[b % 2], 16)
                    g += 1

    return nc





def build_L3a(prequant=False):
    nc = bass.Bass("TRN2", target_bir_lowering=False)
    x_t = nc.dram_tensor("x", [D, T], F32, kind="ExternalInput").ap()
    h1_t = nc.dram_tensor("h1", [H, T], F32, kind="ExternalInput").ap()
    mall_t = nc.dram_tensor("mall", [128, NCORE], F32, kind="ExternalInput").ap()
    if prequant:
        wq2i_t = nc.dram_tensor("wq2i", [H, D], BF16, kind="ExternalInput").ap()
        swi_t = nc.dram_tensor("swi", [128, 1], F32, kind="ExternalInput").ap()
    else:
        w2T_t = nc.dram_tensor("w2T", [H, D], F32, kind="ExternalInput").ap()
    b2c_t = nc.dram_tensor("b2c", [128, DC], F32, kind="ExternalInput").ap()
    eye_t = nc.dram_tensor("eye", [128, 128], F32, kind="ExternalInput").ap()
    xo_t = nc.dram_tensor("xo", [D, T], F32, kind="ExternalOutput").ap()
    if not prequant:
        wq2_t = nc.dram_tensor("wq2", [H, D], BF16, kind="ExternalOutput").ap()
        sw_t = nc.dram_tensor("sw", [1, 1], F32, kind="ExternalOutput").ap()

    NW = HC
    NQ = 4
    with ExitStack() as st:
        S = st.enter_context
        wbuf = [S(nc.sbuf_tensor(f"wb{i}", [128, D], F32)) for i in range(2)]
        wq2g = [S(nc.sbuf_tensor(f"wq2g{i}", [128, 4 * D], BF16)) for i in range(NW // 4)]
        wq2r = [wq2g[h // 4][:, (h % 4) * D:(h % 4 + 1) * D] for h in range(NW)]
        hbuf = [S(nc.sbuf_tensor(f"hb{i}", [128, DC * TT], F32)) for i in range(2)]
        hq = [S(nc.sbuf_tensor(f"hq{i}", [128, DC * TT], BF16)) for i in range(2)]
        ubuf = [S(nc.sbuf_tensor(f"ub{i}", [128, T], F32)) for i in range(2)]
        xo = [S(nc.sbuf_tensor(f"xo_sb{i}", [128, T], F32)) for i in range(2)]
        wint = S(nc.sbuf_tensor("wint", [128, D], I32))
        wf = S(nc.sbuf_tensor("wf", [128, D], F32))
        xint = S(nc.sbuf_tensor("xint", [128, TT], I32))
        qf = S(nc.sbuf_tensor("qf", [128, TT], F32))
        eye = S(nc.sbuf_tensor("eye_sb", [128, 128], F32))
        oner = S(nc.sbuf_tensor("oner", [1, 128], F32))
        mallb = S(nc.sbuf_tensor("mallb", [128, NCORE], F32))
        b2c = S(nc.sbuf_tensor("b2c_sb", [128, DC], F32))
        b2s = S(nc.sbuf_tensor("b2s", [128, DC], F32))
        red8 = S(nc.sbuf_tensor("red8", [128, NW], F32))
        wmx = S(nc.sbuf_tensor("wmx", [128, 1], F32))
        mxm = S(nc.sbuf_tensor("mxm", [128, 1], F32))
        sxb = S(nc.sbuf_tensor("sxb", [128, 1], F32))
        rxb = S(nc.sbuf_tensor("rxb", [128, 1], F32))
        s5b = S(nc.sbuf_tensor("s5b", [128, 1], F32))
        swb = S(nc.sbuf_tensor("swb", [128, 1], F32))
        rwb = S(nc.sbuf_tensor("rwb", [128, 1], F32))
        sc1 = S(nc.sbuf_tensor("sc1", [128, 1], F32))
        wsc = S(nc.sbuf_tensor("wsc", [1, 1], F32))
        sw_s = S(nc.sbuf_tensor("sw_s", [1, 1], F32))
        rw1 = S(nc.sbuf_tensor("rw1", [1, 1], F32))
        r1a = S(nc.sbuf_tensor("r1a", [1, 1], F32))
        srow = S(nc.sbuf_tensor("srow", [1, 2], F32))
        pmm = [S(nc.psum_tensor(f"pmm{i}", [128, TT], F32)) for i in range(DC)]
        lfix = S(nc.semaphore("lfix"))
        lu = [S(nc.semaphore(f"lu{i}")) for i in range(2)]
        lh = [S(nc.semaphore(f"lh{i}")) for i in range(2)]
        gw = [S(nc.semaphore(f"gw{i}")) for i in range(2)]
        sx_ = [S(nc.semaphore(f"sx{i}")) for i in range(2)]
        swq = [S(nc.semaphore(f"swq{i}")) for i in range(2)]
        so_ = S(nc.semaphore("so_"))
        sv = S(nc.semaphore("sv"))
        sp = S(nc.semaphore("sp"))
        sa = S(nc.semaphore("sa"))
        vf_s = S(nc.semaphore("vf"))
        blk = S(nc.Block())

        V_ONES = 1
        V_WCH = lambda h: 2 + h
        V_WMX = 2 + NW
        V_SROW = V_WMX + 1
        V_SCAL = V_SROW + 1
        V_WQ = lambda h: V_SCAL + 1 + h
        _pos = {}
        _ctr = V_WQ(NW - 1)
        for _t in range(NT):
            for _q in range(NQ):
                _ctr += 1
                _pos[("q", _t, _q)] = _ctr
            if _t >= 1:
                _ctr += 1
                _pos[("a", _t - 1)] = _ctr
        _ctr += 1
        _pos[("a", NT - 1)] = _ctr
        V_HQ = lambda t, q: _pos[("q", t, q)]
        V_XO = lambda t: _pos[("a", t)]
        V_FIN = V_XO(NT - 1)
        V_SWST = V_SROW
        P_TR = 1
        P_BC = 2
        P_MM = lambda t, q: 3 + t * NQ + q
        P_T = lambda t: P_MM(t, NQ - 1)

        @blk.sync
        def _(sync):
            sync.dma_start(out=mallb[:], in_=mall_t).then_inc(lfix, 16)
            sync.dma_start(out=b2c[:], in_=b2c_t).then_inc(lfix, 16)
            sync.dma_start(out=eye[:], in_=eye_t).then_inc(lfix, 16)
            if prequant:
                sync.dma_start(out=swb[:], in_=swi_t).then_inc(lfix, 16)
            for t in range(NT):
                if t >= 2:
                    sync.wait_ge(sv, V_XO(t - 2))
                sync.dma_start(
                    out=ubuf[t % 2][:].rearrange("p (c t) -> p c t", c=DC),
                    in_=x_t[:, t * TT:(t + 1) * TT].rearrange("(c p) t -> p c t", p=128),
                ).then_inc(lu[t % 2], 16)
                for j in (4 * t + 1, 4 * t + 3):
                    tq, qq = divmod(j, NQ)
                    if j >= 2:
                        sync.wait_ge(sv, V_HQ(*divmod(j - 2, NQ)))
                    sync.dma_start(
                        out=hbuf[1][:].rearrange("p (c t) -> p c t", c=DC),
                        in_=h1_t[qq * (H // NQ):(qq + 1) * (H // NQ),
                                 tq * TT:(tq + 1) * TT].rearrange("(c p) t -> p c t", p=128),
                    ).then_inc(lh[1], 16)
            sync.wait_ge(sv, V_FIN)
            if not prequant:
                sync.wait_ge(sv, V_SWST)
                sync.dma_start(out=sw_t, in_=sw_s[:]).then_inc(so_, 16)
                sync.wait_ge(so_, 16)
                sync.wait_ge(swq[0], 16 * (NW // 2))
                sync.wait_ge(swq[1], 16 * (NW // 2))
            sync.wait_ge(sx_[0], 16 * (NT // 2))
            sync.wait_ge(sx_[1], 16 * (NT // 2))

        @blk.gpsimd
        def _(gpsimd):
            if prequant:
                for gidx in range(NW // 4):
                    gpsimd.dma_start(
                        out=wq2g[gidx][:].rearrange("p (c m) -> p c m", c=4),
                        in_=wq2i_t[gidx * 512:(gidx + 1) * 512, :].rearrange(
                            "(c p) m -> p c m", p=128),
                    ).then_inc(gw[gidx % 2], 16)
            else:
                for h in range(NW):
                    if h >= 2:
                        gpsimd.wait_ge(sv, V_WCH(h - 2))
                    gpsimd.dma_start(out=wbuf[h % 2][:],
                                     in_=w2T_t[h * 128:(h + 1) * 128, :]
                                     ).then_inc(gw[h % 2], 16)
                gpsimd.wait_ge(sv, V_WCH(NW - 1))
                for h in range(NW):
                    if h >= 2:
                        gpsimd.wait_ge(sv, V_WQ(h - 2))
                    gpsimd.dma_start(out=wbuf[h % 2][:],
                                     in_=w2T_t[h * 128:(h + 1) * 128, :]
                                     ).then_inc(gw[h % 2], 16)
            for t in range(NT):
                for q in range(NQ):
                    j = t * NQ + q
                    if j % 2 != 0:
                        continue
                    if j >= 2:
                        gpsimd.wait_ge(sv, V_HQ(*divmod(j - 2, NQ)))
                    gpsimd.dma_start(
                        out=hbuf[0][:].rearrange("p (c t) -> p c t", c=DC),
                        in_=h1_t[q * (H // NQ):(q + 1) * (H // NQ),
                                 t * TT:(t + 1) * TT].rearrange("(c p) t -> p c t", p=128),
                    ).then_inc(lh[0], 16)

        @blk.vector
        def _(vector):
            vf = FE(vector, vf_s)
            vector.memset(oner[:], 1.0)
            vector.memset(wmx[:], 0.0).then_inc(sv, 1)
            vector.wait_ge(sv, V_ONES)
            vector.wait_ge(lfix, 16 * (4 if prequant else 3))
            vf(vector.tensor_reduce(out=mxm[:], in_=mallb[:], axis=AX.X, op=ALU.max,
                                    apply_absolute_value=True))
            vf(vector.tensor_scalar(out=sxb[:], in0=mxm[:], scalar1=MINR,
                                    scalar2=1.0 / QP, op0=ALU.max, op1=ALU.mult))
            recip_newton_f(vector, vf, rxb[:], sxb[:], sc1[:])
            if not prequant:
                for h in range(NW):
                    vector.wait_ge(gw[h % 2], 16 * (h // 2 + 1))
                    vector.tensor_reduce(out=red8[:, h:h + 1], in_=wbuf[h % 2][:],
                                         axis=AX.X, op=ALU.max,
                                         apply_absolute_value=True).then_inc(sv, 1)
                vector.wait_ge(sv, V_WCH(NW - 1))
                vector.tensor_reduce(out=wmx[:], in_=red8[:], axis=AX.X, op=ALU.max,
                                     apply_absolute_value=True).then_inc(sv, 1)
                vector.wait_ge(sv, V_WMX)
                vector.wait_ge(sp, P_TR)
                vf(vector.tensor_reduce(out=wsc[:], in_=pmm[0][0:1, 0:128],
                                        axis=AX.X, op=ALU.max,
                                        apply_absolute_value=True))
                vf(vector.tensor_scalar(out=sw_s[:], in0=wsc[:], scalar1=MINR,
                                        scalar2=1.0 / QP, op0=ALU.max, op1=ALU.mult))
            else:
                for _ in range(NW + 1):
                    vector.nop().then_inc(sv, 1)
                vector.wait_ge(sv, V_WMX)
            if prequant:
                recip_newton_f(vector, vf, rwb[:], swb[:], sc1[:])
                vector.nop().then_inc(sv, 1)
            else:
                recip_newton_f(vector, vf, rw1[:], sw_s[:], r1a[:])
                vector.tensor_copy(out=srow[0:1, 0:1], in_=sw_s[:])
                vector.tensor_copy(out=srow[0:1, 1:2], in_=rw1[:]).then_inc(sv, 1)
                vector.wait_ge(sp, P_BC)
                vector.tensor_copy(out=swb[:], in_=pmm[0][:, 0:1])
                vf(vector.tensor_copy(out=rwb[:], in_=pmm[0][:, 1:2]))
            vf(vector.tensor_tensor(out=s5b[:], in0=swb[:], in1=sxb[:], op=ALU.mult))
            vf(vector.tensor_scalar(out=s5b[:], in0=s5b[:], scalar1=0.5,
                                    scalar2=None, op0=ALU.mult))
            vector.tensor_scalar(out=b2s[:], in0=b2c[:], scalar1=s5b[:, 0:1],
                                 scalar2=None, op0=ALU.mult).then_inc(sv, 1)
            for h in range(NW):
                if prequant:
                    gidx = h // 4
                    vector.wait_ge(gw[gidx % 2], 16 * (gidx // 2 + 1))
                    vector.nop().then_inc(sv, 1)
                    continue
                vector.wait_ge(gw[h % 2], 16 * (NW // 2 + h // 2 + 1))
                vf(vector.tensor_scalar(out=wf[:], in0=wbuf[h % 2][:],
                                        scalar1=rwb[:, 0:1], scalar2=MAGIC,
                                        op0=ALU.mult, op1=ALU.add))
                vf(vector.tensor_scalar(out=wint[:], in0=wf[:], scalar1=MAGIC,
                                        scalar2=None, op0=ALU.subtract))
                vector.tensor_copy(out=wq2r[h], in_=wint[:]).then_inc(sv, 1)
                vector.wait_ge(sv, V_WQ(h))
            def residual_add(t):
                vector.wait_ge(lu[t % 2], 16 * (t // 2 + 1))
                vector.wait_ge(sa, DC * (t + 1))
                for d in range(DC):
                    ds = slice(d * TT, (d + 1) * TT)
                    ins = vector.tensor_add(out=xo[t % 2][:, ds],
                                            in0=xo[t % 2][:, ds],
                                            in1=ubuf[t % 2][:, ds])
                ins.then_inc(sv, 1)
                vector.wait_ge(sv, V_XO(t))

            for t in range(NT):
                for q in range(NQ):
                    j = t * NQ + q
                    i = j % 2
                    vector.wait_ge(lh[i], 16 * (j // 2 + 1))
                    if j >= 2:
                        vector.wait_ge(sp, P_MM(*divmod(j - 2, NQ)))
                    for c in range(DC):
                        cs = slice(c * TT, (c + 1) * TT)
                        vf(vector.tensor_scalar(out=qf[:], in0=hbuf[i][:, cs],
                                                scalar1=rxb[:, 0:1], scalar2=MAGIC,
                                                op0=ALU.mult, op1=ALU.add))
                        vf(vector.tensor_scalar(out=xint[:], in0=qf[:], scalar1=MAGIC,
                                                scalar2=None, op0=ALU.subtract))
                        ins = vector.tensor_copy(out=hq[i][:, cs], in_=xint[:])
                        if c < DC - 1:
                            vf(ins)
                    ins.then_inc(sv, 1)
                    vector.wait_ge(sv, V_HQ(t, q))
                if t >= 1:
                    residual_add(t - 1)
            residual_add(NT - 1)

        @blk.tensor
        def _(tensor):
            tensor.wait_ge(lfix, 16 * (4 if prequant else 3))
            if not prequant:
                tensor.wait_ge(sv, V_WMX)
                tensor.transpose(out=pmm[0][0:1, 0:128], in_=wmx[:],
                                 identity=eye[:]).then_inc(sp, 1)
            else:
                tensor.nop().then_inc(sp, 1)
            if prequant:
                tensor.nop().then_inc(sp, 1)
            else:
                tensor.wait_ge(sv, V_SROW)
                tensor.matmul(out=pmm[0][:, 0:2], lhsT=oner[:], rhs=srow[:],
                              start=True, stop=True).then_inc(sp, 1)
            for t in range(NT):
                for q in range(NQ):
                    i = (t * NQ + q) % 2
                    if t == 0:
                        tensor.wait_ge(sv, V_WQ(q * DC + DC - 1))
                    tensor.wait_ge(sv, V_HQ(t, q))
                    if q == 0 and t >= 1:
                        tensor.wait_ge(sa, DC * t)
                    for hh in range(DC):
                        h = q * DC + hh
                        for d in range(DC):
                            ins = tensor.matmul(
                                out=pmm[d][:],
                                lhsT=wq2r[h][:, d * 128:(d + 1) * 128],
                                rhs=hq[i][:, hh * TT:(hh + 1) * TT],
                                start=(h == 0), stop=(h == NW - 1))
                    ins.then_inc(sp, 1)

        @blk.scalar
        def _(scalar):
            if not prequant:
                for h in range(NW):
                    scalar.wait_ge(sv, V_WQ(h))
                    scalar.dma_start(out=wq2_t[h * 128:(h + 1) * 128, :],
                                     in_=wq2r[h]).then_inc(swq[h % 2], 16)
            scalar.wait_ge(sv, V_SCAL)
            for t in range(NT):
                scalar.wait_ge(sp, P_T(t))
                if t >= 2:
                    scalar.wait_ge(sx_[t % 2], 16 * (t // 2))
                for d in range(DC):
                    ds = slice(d * TT, (d + 1) * TT)
                    scalar.activation(out=xo[t % 2][:, ds], in_=pmm[d][:],
                                      func=AF.Identity, bias=b2s[:, d:d + 1],
                                      scale=s5b[:, 0:1]).then_inc(sa, 1)
                scalar.wait_ge(sv, V_XO(t))
                scalar.dma_start(
                    out=xo_t[:, t * TT:(t + 1) * TT].rearrange("(c p) t -> p c t", p=128),
                    in_=xo[t % 2][:].rearrange("p (c t) -> p c t", c=DC),
                ).then_inc(sx_[t % 2], 16)

    return nc






def build_L3b():
    nc = bass.Bass("TRN2", target_bir_lowering=False)
    x_t = nc.dram_tensor("x", [D, T], F32, kind="ExternalInput").ap()
    waT_t = nc.dram_tensor("waT", [D, D], F32, kind="ExternalInput").ap()
    wbT_t = nc.dram_tensor("wbT", [D, D], F32, kind="ExternalInput").ap()
    bac_t = nc.dram_tensor("bac", [128, DC], F32, kind="ExternalInput").ap()
    bbc_t = nc.dram_tensor("bbc", [128, DC], F32, kind="ExternalInput").ap()
    stats_t = nc.dram_tensor("stats", [2, T], F32, kind="ExternalOutput").ap()
    hs0_t = nc.dram_tensor("hs0", [D, T], F32, kind="ExternalOutput").ap()
    P_t = nc.dram_tensor("P", [D, T], F32, kind="ExternalOutput").ap()
    hend_t = nc.dram_tensor("hend", [128, DC], F32, kind="ExternalOutput").ap()

    with ExitStack() as st:
        S = st.enter_context
        xbuf = [S(nc.sbuf_tensor(f"xbuf{i}", [128, T], F32)) for i in range(2)]
        sq = [S(nc.sbuf_tensor(f"sq{i}", [128, TT], F32)) for i in range(2)]
        xh = [S(nc.sbuf_tensor(f"xh{d}", [128, T], BF16)) for d in range(DC)]
        wab = [S(nc.sbuf_tensor(f"wab{d}", [128, D], BF16)) for d in range(DC)]
        wbb = [S(nc.sbuf_tensor(f"wbb{d}", [128, D], BF16)) for d in range(DC)]
        wld = [S(nc.sbuf_tensor(f"wld{i}", [128, D], F32)) for i in range(2)]
        alpha = S(nc.sbuf_tensor("alpha", [128, T], F32))
        bu = S(nc.sbuf_tensor("bu", [128, T], F32))
        hs0 = S(nc.sbuf_tensor("hs0_sb", [128, T], F32))
        Pb = xbuf[1]
        bac = S(nc.sbuf_tensor("bac_sb", [128, DC], F32))
        bbc = S(nc.sbuf_tensor("bbc_sb", [128, DC], F32))
        hend = S(nc.sbuf_tensor("hend_sb", [128, DC], F32))
        onec = S(nc.sbuf_tensor("onec", [128, 1], F32))
        oner = S(nc.sbuf_tensor("oner", [1, 128], F32))
        t0 = S(nc.sbuf_tensor("t0", [1, TT], F32))
        t1 = S(nc.sbuf_tensor("t1", [1, TT], F32))
        vr = S(nc.sbuf_tensor("vr", [1, TT], F32))
        lnt = S(nc.sbuf_tensor("lnt", [1, TT], F32))
        mr = S(nc.sbuf_tensor("mr", [1, TT], F32))
        rr = S(nc.sbuf_tensor("rr", [1, TT], F32))
        tmpq = S(nc.sbuf_tensor("tmpq", [128, TT], F32))
        psS = [S(nc.psum_tensor(f"psS{i}", [128, TT], F32)) for i in range(2)]
        psM = [S(nc.psum_tensor(f"psM{i}", [128, TT], F32)) for i in range(2)]
        psA = [S(nc.psum_tensor(f"psA{i}", [128, TT], F32)) for i in range(2)]
        psB = [S(nc.psum_tensor(f"psB{i}", [128, TT], F32)) for i in range(2)]
        pA = [psA[0], psA[1], psS[0], psS[1]]
        pB = [psB[0], psB[1], psM[0], psM[1]]
        lfix = S(nc.semaphore("lfix"))
        lx2 = [S(nc.semaphore(f"lx{i}")) for i in range(2)]
        gw = [S(nc.semaphore(f"gw{i}")) for i in range(2)]
        sh = [S(nc.semaphore(f"sh{i}")) for i in range(2)]
        srl = S(nc.semaphore("srl"))
        she = S(nc.semaphore("she"))
        sv = S(nc.semaphore("sv"))
        sp = S(nc.semaphore("sp"))
        sa = S(nc.semaphore("sa"))
        vf_s = S(nc.semaphore("vf"))
        af_s = S(nc.semaphore("af"))
        blk = S(nc.Block())

        V_INIT = 1
        V_WB = lambda k: 2 + k
        V_ROW = lambda t: 18 + 3 * t
        V_VAR = lambda t: 19 + 3 * t
        V_XH = lambda t: 20 + 3 * t
        V_PHA = V_XH(NT - 1)
        V_Z = V_PHA + 1
        V_SC = lambda c: V_Z + 1 + 3 * c + 0
        V_PS = lambda c: V_Z + 2 + 3 * c
        V_HE = lambda c: V_Z + 3 + 3 * c
        V_FIN = V_HE(DC - 1)
        A_SQ = lambda t, d: 9 * t + d + 1
        A_RST = lambda t: 9 * t + 9
        A_EV = lambda c, tt: 9 * NT + 2 * (c * NT + tt) + 2
        P_S1 = lambda t: 10 * t + 1
        P_SQ = lambda t, d: 10 * t + 2 + d
        P_S2 = lambda t: 10 * t + 9
        P_BC = lambda t: 10 * t + 10
        P_AB = lambda c, tt: 10 * NT + c * NT + tt + 1

        @blk.sync
        def _(sync):
            sync.dma_start(out=bac[:], in_=bac_t).then_inc(lfix, 16)
            sync.dma_start(out=bbc[:], in_=bbc_t).then_inc(lfix, 16)
            for t in range(NT):
                if t >= 2:
                    sync.wait_ge(sv, V_XH(t - 2))
                sync.dma_start(
                    out=xbuf[t % 2][:].rearrange("p (c t) -> p c t", c=DC),
                    in_=x_t[:, t * TT:(t + 1) * TT].rearrange("(c p) t -> p c t", p=128),
                ).then_inc(lx2[t % 2], 16)
            sync.wait_ge(sv, V_FIN)
            sync.dma_start(out=hend_t, in_=hend[:]).then_inc(she, 16)
            sync.wait_ge(she, 16)
            sync.wait_ge(srl, 16 * 2 * NT)
            sync.wait_ge(sh[0], 16 * DC)
            sync.wait_ge(sh[1], 16 * DC)

        @blk.gpsimd
        def _(gpsimd):
            for k in range(2 * DC):
                if k >= 2:
                    gpsimd.wait_ge(sv, V_WB(k - 2))
                src_t = waT_t if k < DC else wbT_t
                d = k % DC
                gpsimd.dma_start(out=wld[k % 2][:],
                                 in_=src_t[d * 128:(d + 1) * 128, :]
                                 ).then_inc(gw[k % 2], 16)

        @blk.vector
        def _(vector):
            vf = FE(vector, vf_s)
            vector.memset(onec[:], 1.0)
            vector.memset(oner[:], 1.0).then_inc(sv, 1)
            vector.wait_ge(sv, V_INIT)
            for k in range(2 * DC):
                vector.wait_ge(gw[k % 2], 16 * (k // 2 + 1))
                dst = wab[k] if k < DC else wbb[k - DC]
                vector.tensor_copy(out=dst[:], in_=wld[k % 2][:]).then_inc(sv, 1)
            for t in range(NT):
                ts_ = slice(t * TT, (t + 1) * TT)
                vector.wait_ge(sp, P_S1(t))
                if t >= 1:
                    vector.wait_ge(srl, 32 * t)
                vf(vector.tensor_scalar(out=mr[:], in0=psS[t % 2][0:1, :],
                                        scalar1=INV_D, scalar2=None, op0=ALU.mult))
                vector.wait_ge(sp, P_S2(t))
                vector.tensor_scalar(out=t0[:], in0=psM[t % 2][0:1, :],
                                     scalar1=INV_D, scalar2=None, op0=ALU.mult
                                     ).then_inc(sv, 1)
                vf(vector.tensor_tensor(out=t1[:], in0=mr[:], in1=mr[:],
                                        op=ALU.mult))
                vector.scalar_tensor_tensor(out=vr[:], in0=t0[:], scalar=EPS,
                                            in1=t1[:], op0=ALU.add,
                                            op1=ALU.subtract).then_inc(sv, 1)
                vector.wait_ge(sv, V_VAR(t))
                vector.wait_ge(sp, P_BC(t))
                for d in range(DC):
                    ds = slice(d * TT, (d + 1) * TT)
                    vf(vector.tensor_sub(out=tmpq[:], in0=xbuf[t % 2][:, ds],
                                         in1=psA[t % 2][:]))
                    ins = vector.tensor_tensor(out=xh[d][:, ts_], in0=tmpq[:],
                                               in1=psB[t % 2][:], op=ALU.mult)
                    if d < DC - 1:
                        vf(ins)
                ins.then_inc(sv, 1)
                vector.wait_ge(sv, V_XH(t))
            vector.memset(xbuf[0][:], 0.0).then_inc(sv, 1)
            vector.wait_ge(sv, V_Z)
            for c in range(DC):
                vector.wait_ge(sa, A_EV(c, NT - 1))
                if c >= 1:
                    vector.wait_ge(sh[0], 16 * c)
                    vector.wait_ge(sh[1], 16 * c)
                vector.tensor_tensor_scan(out=hs0[:], data0=alpha[:], data1=bu[:],
                                          initial=0.0, op0=ALU.mult, op1=ALU.add
                                          ).then_inc(sv, 1)
                vector.tensor_tensor_scan(out=Pb[:], data0=alpha[:], data1=xbuf[0][:],
                                          initial=1.0, op0=ALU.mult, op1=ALU.add
                                          ).then_inc(sv, 1)
                vector.wait_ge(sv, V_SC(c))
                vector.tensor_copy(out=hend[:, c:c + 1], in_=hs0[:, T - 1:T]
                                   ).then_inc(sv, 1)
                vector.wait_ge(sv, V_HE(c))

        @blk.tensor
        def _(tensor):
            tensor.wait_ge(sv, V_INIT)
            for t in range(NT):
                ts_ = slice(t * TT, (t + 1) * TT)
                tensor.wait_ge(lx2[t % 2], 16 * (t // 2 + 1))
                if t >= 2:
                    tensor.wait_ge(sv, V_XH(t - 2))
                for d in range(DC):
                    ds = slice(d * TT, (d + 1) * TT)
                    ins = tensor.matmul(out=psS[t % 2][0:1, :], lhsT=onec[:],
                                        rhs=xbuf[t % 2][:, ds], start=(d == 0),
                                        stop=(d == DC - 1))
                ins.then_inc(sp, 1)
                for d in range(DC):
                    tensor.wait_ge(sa, A_SQ(t, d))
                    tensor.matmul(out=psM[t % 2][0:1, :], lhsT=onec[:],
                                  rhs=sq[d % 2][:], start=(d == 0),
                                  stop=(d == DC - 1)).then_inc(sp, 1)
                tensor.wait_ge(sa, A_RST(t))
                tensor.matmul(out=psA[t % 2][:], lhsT=oner[:], rhs=mr[:],
                              start=True, stop=True)
                tensor.matmul(out=psB[t % 2][:], lhsT=oner[:], rhs=rr[:],
                              start=True, stop=True).then_inc(sp, 1)
            tensor.wait_ge(sv, V_WB(2 * DC - 1))
            tensor.wait_ge(sv, V_PHA)
            for c in range(DC):
                ns = slice(c * 128, (c + 1) * 128)
                for tt in range(NT):
                    ts_ = slice(tt * TT, (tt + 1) * TT)
                    j = c * NT + tt
                    if j >= 4:
                        tensor.wait_ge(sa, A_EV(*divmod(j - 4, NT)))
                    for d in range(DC):
                        tensor.matmul(out=pA[j % 4][:], lhsT=wab[d][:, ns],
                                      rhs=xh[d][:, ts_], start=(d == 0),
                                      stop=(d == DC - 1))
                    for d in range(DC):
                        ins = tensor.matmul(out=pB[j % 4][:], lhsT=wbb[d][:, ns],
                                            rhs=xh[d][:, ts_], start=(d == 0),
                                            stop=(d == DC - 1))
                    ins.then_inc(sp, 1)

        @blk.scalar
        def _(scalar):
            af = FE(scalar, af_s)
            scalar.wait_ge(lfix, 32)
            for t in range(NT):
                ts_ = slice(t * TT, (t + 1) * TT)
                scalar.wait_ge(lx2[t % 2], 16 * (t // 2 + 1))
                if t >= 1:
                    scalar.wait_ge(srl, 32 * t)
                for d in range(DC):
                    ds = slice(d * TT, (d + 1) * TT)
                    g = t * DC + d
                    if g >= 2:
                        scalar.wait_ge(sp, P_SQ((g - 2) // DC, (g - 2) % DC))
                    scalar.activation(out=sq[d % 2][:], in_=xbuf[t % 2][:, ds],
                                      func=AF.Square).then_inc(sa, 1)
                scalar.wait_ge(sv, V_VAR(t))
                if t >= 1:
                    scalar.wait_ge(sa, A_RST(t - 1))
                af(scalar.activation(out=lnt[:], in_=vr[:], func=AF.Ln))
                scalar.activation(out=rr[:], in_=lnt[:], func=AF.Exp,
                                  scale=-0.5).then_inc(sa, 1)
                scalar.wait_ge(sa, A_RST(t))
                scalar.dma_start(out=stats_t[0:1, ts_], in_=mr[:]).then_inc(srl, 16)
                scalar.dma_start(out=stats_t[1:2, ts_], in_=rr[:]).then_inc(srl, 16)
            for c in range(DC):
                if c >= 1:
                    scalar.wait_ge(sv, V_SC(c - 1))
                    scalar.dma_start(out=hs0_t[(c - 1) * 128:c * 128, :],
                                     in_=hs0[:]).then_inc(sh[0], 16)
                    scalar.wait_ge(sv, V_PS(c - 1))
                    scalar.dma_start(out=P_t[(c - 1) * 128:c * 128, :],
                                     in_=Pb[:]).then_inc(sh[1], 16)
                for tt in range(NT):
                    ts_ = slice(tt * TT, (tt + 1) * TT)
                    j = c * NT + tt
                    scalar.wait_ge(sp, P_AB(c, tt))
                    if tt == 0 and c >= 1:
                        scalar.wait_ge(sv, V_PS(c - 1))
                    scalar.activation(out=alpha[:, ts_], in_=pA[j % 4][:],
                                      func=AF.Sigmoid, bias=bac[:, c:c + 1]
                                      ).then_inc(sa, 1)
                    scalar.activation(out=bu[:, ts_], in_=pB[j % 4][:],
                                      func=AF.Identity, bias=bbc[:, c:c + 1]
                                      ).then_inc(sa, 1)
            scalar.wait_ge(sv, V_SC(DC - 1))
            scalar.dma_start(out=hs0_t[(DC - 1) * 128:DC * 128, :], in_=hs0[:]
                             ).then_inc(sh[0], 16)
            scalar.wait_ge(sv, V_PS(DC - 1))
            scalar.dma_start(out=P_t[(DC - 1) * 128:DC * 128, :], in_=Pb[:]
                             ).then_inc(sh[1], 16)

    return nc





def build_L4a():
    nc = bass.Bass("TRN2", target_bir_lowering=False)
    x_t = nc.dram_tensor("x", [D, T], F32, kind="ExternalInput").ap()
    hs0_t = nc.dram_tensor("hs0", [D, T], F32, kind="ExternalInput").ap()
    P_t = nc.dram_tensor("P", [D, T], F32, kind="ExternalInput").ap()
    hin_t = nc.dram_tensor("hin", [128, DC], F32, kind="ExternalInput").ap()
    stats_t = nc.dram_tensor("stats", [2, T], F32, kind="ExternalInput").ap()
    cT_t = nc.dram_tensor("cT", [D, D], F32, kind="ExternalInput").ap()
    Dc_t = nc.dram_tensor("Dc", [128, DC], F32, kind="ExternalInput").ap()
    x3_t = nc.dram_tensor("x3", [D, T], F32, kind="ExternalOutput").ap()

    TT2 = 256
    NT2 = T // TT2
    with ExitStack() as st:
        S = st.enter_context
        mrl = [S(nc.sbuf_tensor(f"mrl{i}", [1, TT2], F32)) for i in range(2)]
        rrl = [S(nc.sbuf_tensor(f"rrl{i}", [1, TT2], F32)) for i in range(2)]
        HT = T // 2
        hsld = [S(nc.sbuf_tensor(f"hsld{i}", [128, HT], F32)) for i in range(2)]
        pld = [S(nc.sbuf_tensor(f"pld{i}", [128, HT], F32)) for i in range(2)]
        hsb = [S(nc.sbuf_tensor(f"hsb{c}", [128, T], BF16)) for c in range(DC)]
        ctb = [S(nc.sbuf_tensor(f"ctb{c}", [128, D], BF16)) for c in range(DC)]
        cld = S(nc.sbuf_tensor("cld", [128, D], F32))
        x2b = [S(nc.sbuf_tensor(f"x2b{i}", [128, DC * TT2], F32)) for i in range(2)]
        ystg = [S(nc.sbuf_tensor(f"ystg{i}", [128, DC * TT2], F32)) for i in range(2)]
        xhst = S(nc.sbuf_tensor("xhst", [128, DC * TT2], F32))
        hin = S(nc.sbuf_tensor("hin_sb", [128, DC], F32))
        Dc = S(nc.sbuf_tensor("Dc_sb", [128, DC], F32))
        oner = S(nc.sbuf_tensor("oner", [1, 128], F32))
        pbank = [S(nc.psum_tensor(f"pbank{i}", [128, 2 * TT2], F32)) for i in range(4)]
        pmm = [pbank[i // 2][:, (i % 2) * TT2:(i % 2 + 1) * TT2] for i in range(DC)]
        psmr = [S(nc.psum_tensor(f"psmr{i}", [128, 2 * TT2], F32)) for i in range(2)]
        psM = [psmr[i][:, 0:TT2] for i in range(2)]
        psR = [psmr[i][:, TT2:2 * TT2] for i in range(2)]
        lfix = S(nc.semaphore("lfix"))
        lx = [S(nc.semaphore(f"lx{i}")) for i in range(2)]
        lm = [S(nc.semaphore(f"lm{i}")) for i in range(2)]
        gc = [S(nc.semaphore(f"gc{i}")) for i in range(2)]
        glh2 = [S(nc.semaphore(f"glh{i}")) for i in range(2)]
        glp2 = [S(nc.semaphore(f"glp{i}")) for i in range(2)]
        sx_ = [S(nc.semaphore(f"sx{i}")) for i in range(2)]
        sv = S(nc.semaphore("sv"))
        sp = S(nc.semaphore("sp"))
        sa = S(nc.semaphore("sa"))
        vf_s = S(nc.semaphore("vf"))
        blk = S(nc.Block())

        V_INIT = 1
        V_CB = lambda k: 2 + k
        V_HSB2 = lambda k: 10 + k
        V_HSB = lambda c: V_HSB2(2 * c + 1)
        V_X3 = lambda tt: 26 + tt
        V_FIN = V_X3(NT2 - 1)
        P_Y = lambda tt: tt + 1
        A_EV = lambda tt: 8 * (tt + 1)

        @blk.sync
        def _(sync):
            sync.dma_start(out=hin[:], in_=hin_t).then_inc(lfix, 16)
            sync.dma_start(out=Dc[:], in_=Dc_t).then_inc(lfix, 16)
            for tt in range(NT2):
                ts_ = slice(tt * TT2, (tt + 1) * TT2)
                if tt >= 2:
                    sync.wait_ge(sv, V_X3(tt - 2))
                    sync.wait_ge(sx_[tt % 2], 16 * (tt // 2))
                sync.dma_start(
                    out=x2b[tt % 2][:].rearrange("p (c t) -> p c t", c=DC),
                    in_=x_t[:, ts_].rearrange("(c p) t -> p c t", p=128),
                ).then_inc(lx[tt % 2], 16)
                sync.dma_start(out=mrl[tt % 2][:], in_=stats_t[0:1, ts_]
                               ).then_inc(lm[tt % 2], 16)
                sync.dma_start(out=rrl[tt % 2][:], in_=stats_t[1:2, ts_]
                               ).then_inc(lm[tt % 2], 16)
            sync.wait_ge(sv, V_FIN)
            sync.wait_ge(sx_[0], 16 * (NT2 // 2))
            sync.wait_ge(sx_[1], 16 * (NT2 // 2))

        @blk.gpsimd
        def _(gpsimd):
            for k in range(DC):
                if k >= 1:
                    gpsimd.wait_ge(sv, V_CB(k - 1))
                gpsimd.dma_start(out=cld[:],
                                 in_=cT_t[k * 128:(k + 1) * 128, :]
                                 ).then_inc(gc[k % 2], 16)
            for k in range(2 * DC):
                c, half = divmod(k, 2)
                if k >= 2:
                    gpsimd.wait_ge(sv, V_HSB2(k - 2))
                hsl = slice(half * HT, (half + 1) * HT)
                gpsimd.dma_start(out=hsld[k % 2][:],
                                 in_=hs0_t[c * 128:(c + 1) * 128, hsl]
                                 ).then_inc(glh2[k % 2], 16)
                gpsimd.dma_start(out=pld[k % 2][:],
                                 in_=P_t[c * 128:(c + 1) * 128, hsl]
                                 ).then_inc(glp2[k % 2], 16)

        @blk.vector
        def _(vector):
            vf = FE(vector, vf_s)
            vector.memset(oner[:], 1.0).then_inc(sv, 1)
            vector.wait_ge(sv, V_INIT)
            vector.wait_ge(lfix, 32)
            for k in range(DC):
                vector.wait_ge(gc[k % 2], 16 * (k // 2 + 1))
                vector.tensor_copy(out=ctb[k][:], in_=cld[:]).then_inc(sv, 1)
            for k in range(2 * DC):
                c, half = divmod(k, 2)
                hsl = slice(half * HT, (half + 1) * HT)
                vector.wait_ge(glh2[k % 2], 16 * (k // 2 + 1))
                vector.wait_ge(glp2[k % 2], 16 * (k // 2 + 1))
                vector.scalar_tensor_tensor(out=hsb[c][:, hsl], in0=pld[k % 2][:],
                                            scalar=hin[:, c:c + 1],
                                            in1=hsld[k % 2][:],
                                            op0=ALU.mult, op1=ALU.add
                                            ).then_inc(sv, 1)
                vector.wait_ge(sv, V_HSB2(k))
            for tt in range(NT2):
                ds_all = slice(0, DC * TT2)
                vector.wait_ge(lx[tt % 2], 16 * (tt // 2 + 1))
                vector.wait_ge(sp, P_Y(tt))
                vector.wait_ge(sa, A_EV(tt))
                for d in range(DC):
                    ds = slice(d * TT2, (d + 1) * TT2)
                    ins = vector.tensor_sub(out=xhst[:, ds], in0=x2b[tt % 2][:, ds],
                                            in1=psM[tt % 2])
                vf(ins)
                for d in range(DC):
                    ds = slice(d * TT2, (d + 1) * TT2)
                    ins = vector.tensor_tensor(out=xhst[:, ds], in0=xhst[:, ds],
                                               in1=psR[tt % 2], op=ALU.mult)
                vf(ins)
                for d in range(DC):
                    ds = slice(d * TT2, (d + 1) * TT2)
                    ins = vector.scalar_tensor_tensor(out=ystg[tt % 2][:, ds],
                                                      in0=xhst[:, ds],
                                                      scalar=Dc[:, d:d + 1],
                                                      in1=ystg[tt % 2][:, ds],
                                                      op0=ALU.mult, op1=ALU.add)
                vf(ins)
                for d in range(DC):
                    ds = slice(d * TT2, (d + 1) * TT2)
                    ins = vector.tensor_add(out=x2b[tt % 2][:, ds],
                                            in0=ystg[tt % 2][:, ds],
                                            in1=x2b[tt % 2][:, ds])
                ins.then_inc(sv, 1)
                vector.wait_ge(sv, V_X3(tt))

        @blk.tensor
        def _(tensor):
            tensor.wait_ge(lfix, 32)
            tensor.wait_ge(sv, V_HSB(DC - 1))
            for tt in range(NT2):
                ts_ = slice(tt * TT2, (tt + 1) * TT2)
                if tt >= 1:
                    tensor.wait_ge(sa, A_EV(tt - 1))
                if tt >= 2:
                    tensor.wait_ge(sv, V_X3(tt - 2))
                tensor.wait_ge(lm[tt % 2], 16 * (2 * (tt // 2) + 2))
                for d in range(DC):
                    for c in range(DC):
                        tensor.matmul(out=pmm[d],
                                      lhsT=ctb[c][:, d * 128:(d + 1) * 128],
                                      rhs=hsb[c][:, ts_], start=(c == 0),
                                      stop=(c == DC - 1))
                tensor.matmul(out=psM[tt % 2], lhsT=oner[:], rhs=mrl[tt % 2][:],
                              start=True, stop=True)
                tensor.matmul(out=psR[tt % 2], lhsT=oner[:], rhs=rrl[tt % 2][:],
                              start=True, stop=True).then_inc(sp, 1)

        @blk.scalar
        def _(scalar):
            for tt in range(NT2):
                scalar.wait_ge(sp, P_Y(tt))
                if tt >= 2:
                    scalar.wait_ge(sv, V_X3(tt - 2))
                for d in range(DC):
                    ds = slice(d * TT2, (d + 1) * TT2)
                    scalar.activation(out=ystg[tt % 2][:, ds], in_=pmm[d],
                                      func=AF.Identity).then_inc(sa, 1)
                scalar.wait_ge(sv, V_X3(tt))
                scalar.dma_start(
                    out=x3_t[:, tt * TT2:(tt + 1) * TT2].rearrange("(c p) t -> p c t", p=128),
                    in_=x2b[tt % 2][:].rearrange("p (c t) -> p c t", c=DC),
                ).then_inc(sx_[tt % 2], 16)

    return nc



def _asc(a):
    return np.ascontiguousarray(a)


def _shard_u(u):
    out = []
    for c in range(NCORE):
        b, h = c // 2, c % 2
        out.append(_asc(u[b, h * T:(h + 1) * T, :].T))
    return out


_EYE = np.eye(128, dtype=np.float32)


def _bcol(v):
    return np.ascontiguousarray(np.asarray(v).reshape(-1, 128).T)


def _gather_mall(results, key="mloc"):
    m = np.concatenate([np.asarray(r[key]).reshape(1, 1) for r in results], axis=1)
    return np.ascontiguousarray(np.broadcast_to(m, (128, NCORE)))


def kernel(**inputs):
    u = np.asarray(inputs["L0_attn_u"])
    w1T = _asc(np.asarray(inputs["L0_ffn_w1"]).T)
    w2T = _asc(np.asarray(inputs["L0_ffn_w2"]).T)
    waT = _asc(np.asarray(inputs["L0_attn_W_alpha_w"]).T)
    wbT = _asc(np.asarray(inputs["L0_attn_W_B_w"]).T)
    cT = _asc(np.asarray(inputs["L0_attn_C_w"]).T)
    b1c = _bcol(inputs["L0_ffn_b1"])
    b2c = _bcol(inputs["L0_ffn_b2"])
    bac = _bcol(inputs["L0_attn_W_alpha_b"])
    bbc = _bcol(inputs["L0_attn_W_B_b"])
    Dcc = _bcol(inputs["L0_attn_D"])
    shards = _shard_u(u)
    cores = list(range(NCORE))

    nc_ln = build_L1()
    nc_ffn1 = build_L2()
    nc_ffn1p = build_L2(prequant=True)
    nc_ffn2 = build_L3a()
    nc_ffn2p = build_L3a(prequant=True)
    nc_ssm1 = build_L3b()
    nc_ssm2 = build_L4a()

    r1 = run_bass_kernel_spmd(
        nc_ln, [{"u": shards[c], "eye": _EYE} for c in cores], core_ids=cores)
    m1 = _gather_mall(r1.results)

    r2 = run_bass_kernel_spmd(
        nc_ffn1, [{"u": shards[c], "stats": r1.results[c]["stats"], "mall": m1,
                   "w1T": w1T, "b1c": b1c, "eye": _EYE} for c in cores],
        core_ids=cores)
    m2 = _gather_mall(r2.results)

    r3a = run_bass_kernel_spmd(
        nc_ffn2, [{"x": shards[c], "h1": r2.results[c]["h1"], "mall": m2,
                   "w2T": w2T, "b2c": b2c, "eye": _EYE} for c in cores],
        core_ids=cores)

    r3b = run_bass_kernel_spmd(
        nc_ssm1, [{"x": r3a.results[c]["xo"], "waT": waT, "wbT": wbT,
                   "bac": bac, "bbc": bbc} for c in cores], core_ids=cores)

    zeros_hin = np.zeros((128, DC), np.float32)
    hins = [np.asarray(r3b.results[c - 1]["hend"]) if c % 2 == 1 else zeros_hin
            for c in cores]
    r4a = run_bass_kernel_spmd(
        nc_ssm2, [{"x": r3a.results[c]["xo"], "hs0": r3b.results[c]["hs0"],
                   "P": r3b.results[c]["P"], "hin": hins[c],
                   "stats": r3b.results[c]["stats"], "cT": cT, "Dc": Dcc}
                  for c in cores], core_ids=cores)

    r4b = run_bass_kernel_spmd(
        nc_ln, [{"u": r4a.results[c]["x3"], "eye": _EYE} for c in cores],
        core_ids=cores)
    m3 = _gather_mall(r4b.results)

    r5 = run_bass_kernel_spmd(
        nc_ffn1p, [{"u": r4a.results[c]["x3"], "stats": r4b.results[c]["stats"],
                    "mall": m3, "wq1i": r2.results[c]["wq1"],
                    "swi": np.ascontiguousarray(np.broadcast_to(
                        np.asarray(r2.results[c]["sw"]).reshape(1, 1), (128, 1))),
                    "b1c": b1c, "eye": _EYE}
                   for c in cores], core_ids=cores)
    m4 = _gather_mall(r5.results)

    r6 = run_bass_kernel_spmd(
        nc_ffn2p, [{"x": r4a.results[c]["x3"], "h1": r5.results[c]["h1"],
                    "mall": m4, "wq2i": r3a.results[c]["wq2"],
                    "swi": np.ascontiguousarray(np.broadcast_to(
                        np.asarray(r3a.results[c]["sw"]).reshape(1, 1), (128, 1))),
                    "b2c": b2c, "eye": _EYE}
                   for c in cores], core_ids=cores)

    out = np.empty((4, 8192, D), np.float32)
    for c in cores:
        b, h = c // 2, c % 2
        out[b, h * T:(h + 1) * T, :] = np.asarray(r6.results[c]["xo"]).T
    return out
